# version 16
# baseline (speedup 1.0000x reference)
import numpy as np
from contextlib import ExitStack

import concourse.bass as bass
import concourse.mybir as mybir
from concourse.bass_utils import run_bass_kernel_spmd

F32 = mybir.dt.float32
F32R = mybir.dt.float32r
BF16 = mybir.dt.bfloat16
AF = mybir.ActivationFunctionType
ALU = mybir.AluOpType
AX = mybir.AxisListType

ENGINES = ("PE", "ACT", "DVE", "POOL", "SP")
QUEUE_OF = {"SP": "QSP", "ACT": "QACT", "POOL": "QPOOL", "DVE": "QDVE", "PE": "QPE"}
ALL_STREAMS = ENGINES + tuple(QUEUE_OF.values())


class _Op:
    __slots__ = ("idx", "stream", "issuer", "fn", "reads", "writes", "inc_amt",
                 "needs_inc", "inc_val", "deps", "waits", "tag", "wait_queues",
                 "fence_out")


class Prog:

    def __init__(self, nc):
        self.nc = nc
        self.ops = []
        self._writer = {}
        self._readers = {}

    def op(self, engine, fn, reads=(), writes=(), tag="", preads=(), dep_ops=(),
           fence_out=False):
        assert engine in ENGINES, engine
        return self._add(engine, engine, fn, reads, writes, 1, False, tag, preads,
                         dep_ops, fence_out)

    def dma(self, issuer, fn, reads=(), writes=(), tag=""):
        assert issuer in ENGINES, issuer
        return self._add(QUEUE_OF[issuer], issuer, fn, reads, writes, 16, True, tag, (), (), False)

    def _add(self, stream, issuer, fn, reads, writes, inc_amt, force_inc, tag, preads=(),
             dep_ops=(), fence_out=False):
        o = _Op()
        o.idx = len(self.ops)
        o.stream = stream
        o.issuer = issuer
        o.fn = fn
        o.reads = list(reads)
        o.writes = list(writes)
        o.inc_amt = inc_amt
        o.needs_inc = force_inc
        o.inc_val = None
        o.tag = tag
        o.wait_queues = False
        o.fence_out = fence_out
        deps = {}

        def add_pdep(d):
            if d is None:
                return
            k = d.stream
            if k not in deps or deps[k].idx < d.idx:
                deps[k] = d

        def add_dep(d):
            if d is None:
                return
            if d.stream == stream == "PE":
                return
            if d.stream == stream and stream.startswith("Q"):
                return
            k = d.stream
            if k not in deps or deps[k].idx < d.idx:
                deps[k] = d

        for b in preads:
            add_pdep(self._writer.get(b))
            o.reads.append(b)
        for d0 in dep_ops:
            add_pdep(d0)
        for b in o.reads:
            add_dep(self._writer.get(b))
        for b in o.writes:
            add_dep(self._writer.get(b))
            for r in self._readers.get(b, ()):
                add_dep(r)
        o.deps = [d.idx for d in deps.values()]
        for b in o.reads:
            self._readers.setdefault(b, []).append(o)
        for b in o.writes:
            self._writer[b] = o
            self._readers[b] = []
        self.ops.append(o)
        return o

    def finalize(self):
        for o in self.ops:
            for di in o.deps:
                self.ops[di].needs_inc = True
        counts = {s: 0 for s in ALL_STREAMS}
        qprefix = []
        qtot = {s: 0 for s in ALL_STREAMS if s.startswith("Q")}
        for o in self.ops:
            qprefix.append(dict(qtot))
            if o.needs_inc:
                counts[o.stream] += o.inc_amt
                o.inc_val = counts[o.stream]
                if o.stream.startswith("Q"):
                    qtot[o.stream] = o.inc_val
        known = {e: {} for e in ENGINES}
        for o in self.ops:
            w = {}
            for di in o.deps:
                d = self.ops[di]
                assert d.inc_val is not None
                v = qprefix[o.idx][d.stream] if d.stream.startswith("Q") else d.inc_val
                if w.get(d.stream, 0) < v:
                    w[d.stream] = v
            if o.wait_queues:
                for qs, qv in qprefix[o.idx].items():
                    if qv > 0 and w.get(qs, 0) < qv:
                        w[qs] = qv
            ks = known[o.issuer]
            o.waits = []
            for s in sorted(w):
                v = w[s]
                if ks.get(s, 0) < v:
                    o.waits.append((s, v))
                    ks[s] = v
        self.final_counts = counts

    def n_waits(self):
        return sum(len(o.waits) for o in self.ops)

    _barn = 0

    def barrier(self):
        n = Prog._barn
        Prog._barn += 1
        last = {}
        for o in self.ops:
            if o.stream in ENGINES:
                last[o.stream] = o
        for E in ENGINES:
            self.op(E, lambda e: e.nop(), writes=[("bar1", n, E)], tag=f"bar1_{E}_{n}",
                    dep_ops=[last[E]] if E in last else [])
        for E in ENGINES:
            o = self.op(E, lambda e: e.nop(),
                        reads=[("bar1", n, E2) for E2 in ENGINES],
                        writes=[("bar2", n, E)], tag=f"bar2_{E}_{n}")
            o.wait_queues = True

    def build(self, stack):
        self.finalize()
        nc = self.nc
        sems = {}
        for s in ALL_STREAMS:
            sems[s] = stack.enter_context(nc.semaphore("sem_" + s))
        per_eng = {e: [] for e in ENGINES}
        for o in self.ops:
            per_eng[o.issuer].append(o)

        block = stack.enter_context(nc.Block())

        def mk(eng_name):
            ops = per_eng[eng_name]

            def body(eng):
                with nc.allow_low_precision("f32r operands/stream by design"):
                    for o in ops:
                        for (s, v) in o.waits:
                            eng.wait_ge(sems[s], v)
                        inst = o.fn(eng)
                        if o.needs_inc:
                            inst.then_inc(sems[o.stream], o.inc_amt)
                if eng_name == "SP":
                    for s in ALL_STREAMS:
                        if s != "SP" and self.final_counts[s] > 0:
                            eng.wait_ge(sems[s], self.final_counts[s])
            return body

        block.tensor(mk("PE"))
        block.scalar(mk("ACT"))
        block.vector(mk("DVE"))
        block.gpsimd(mk("POOL"))
        block.sync(mk("SP"))


N = 2048
DC = 8
TT = 512
NTT = N // TT
TC = 128
NTC = N // TC

_DT_SIZE = {F32: 4, F32R: 4, BF16: 2}
ACOLS = 33536


class Alloc:
    def __init__(self, nc):
        self.nc = nc
        self.per_part = 0

    FORCE_F32R = True

    def __call__(self, name, shape, dtype):
        if self.FORCE_F32R and dtype == F32:
            dtype = F32R
        t = self.nc.alloc_sbuf_tensor(name, shape, dtype)
        self.per_part += shape[-1] * _DT_SIZE[dtype]
        return t


class ArenaAlloc:

    def __init__(self, arena_handle, total_cols):
        self.h = arena_handle
        self.total = total_cols
        self.off = 0
        self.hw = 0

    def reset(self):
        self.hw = max(self.hw, self.off)
        self.off = 0

    def __call__(self, name, shape, dtype):
        req = (shape[-1] * _DT_SIZE[dtype] + 3) // 4
        cols4 = (req + 15) // 16 * 16
        assert self.off + cols4 <= self.total, (name, self.off, cols4, self.total)
        v = self.h[0:shape[0], self.off:self.off + req]
        self.off += cols4
        return v


class C_:
    pass


def make_ctx(nc):
    C = C_()
    C.nc = nc
    C.P = Prog(nc)
    C.al = Alloc(nc)
    C.X = C.al("X", [128, DC * N], F32R)
    C.PS = [nc.alloc_psum_tensor(f"PS{i}", [128, 512], F32) for i in range(8)]
    C.GSCR = C.al("gscr", [128, 2048], F32R)
    C.ba = C.al
    return C


def xkeys(c, tt=None):
    if tt is None:
        return [("X", c, t) for t in range(NTT)]
    return [("X", c, tt)]


def xall():
    return [("X", c, t) for c in range(DC) for t in range(NTT)]


def xsl(C, c, tt=None, lo=0, hi=128):
    if tt is None:
        return C.X[lo:hi, c * N:(c + 1) * N]
    return C.X[lo:hi, c * N + tt * TT: c * N + (tt + 1) * TT]


def emit_consts(C):
    P, al = C.P, C.al
    C.ri = al("c_ri", [128, 128], F32)
    C.ci = al("c_ci", [128, 128], F32)
    C.ident = al("c_ident", [128, 128], F32R)
    C.mask = al("c_mask", [128, 128], F32R)
    C.ones_col = al("c_ones_col", [128, 1], F32R)
    C.ones1 = al("c_ones1", [1, 128], F32R)
    P.op("POOL", lambda e: e.iota(C.ci[:], [[1, 128]], base=0, channel_multiplier=0,
                                  allow_small_or_imprecise_dtypes=True), writes=["c_ci"])
    P.op("POOL", lambda e: e.iota(C.ri[:], [[0, 128]], base=0, channel_multiplier=1,
                                  allow_small_or_imprecise_dtypes=True), writes=["c_ri"])
    P.op("DVE", lambda e: e.tensor_tensor(C.ident[:], C.ri[:], C.ci[:], ALU.is_equal),
         reads=["c_ri", "c_ci"], writes=["c_ident"])
    P.op("DVE", lambda e: e.tensor_tensor(C.mask[:], C.ci[:], C.ri[:], ALU.is_ge),
         reads=["c_ri", "c_ci"], writes=["c_mask"])
    P.op("DVE", lambda e: e.tensor_scalar(C.ones_col[:], C.ri[:, 0:1], -1.0, None, ALU.is_ge),
         reads=["c_ri"], writes=["c_ones_col"])
    P.op("DVE", lambda e: e.tensor_scalar(C.ones1[:], C.ri[0:1, :], -1.0, None, ALU.is_ge),
         reads=["c_ri"], writes=["c_ones1"])
    C.cb6 = C.nc.alloc_sbuf_tensor("c_eps6", [128, 1], F32)
    C.cb12 = C.nc.alloc_sbuf_tensor("c_eps12", [128, 1], F32)
    C.nc.const_aps.register(F32, 1e-6, C.cb6[:])
    C.nc.const_aps.register(F32, 1e-12, C.cb12[:])
    P.op("DVE", lambda e: e.tensor_scalar(C.cb6[:], C.ri[:, 0:1].bitcast(F32), -1.0, None, ALU.is_ge),
         reads=["c_ri"], writes=["c_eps6"])
    P.op("DVE", lambda e: e.tensor_scalar(C.cb6[:], C.cb6[:], 1e-6, None, ALU.mult),
         reads=["c_eps6"], writes=["c_eps6"])
    P.op("DVE", lambda e: e.tensor_scalar(C.cb12[:], C.cb6[:], 1e-6, None, ALU.mult),
         reads=["c_eps6"], writes=["c_eps12"])


def emit_stream_in_fm(C, x_d):
    C.P.dma("SP", lambda e: e.dma_start(
        out=C.X[:].rearrange("p (c t) -> p c t", c=DC),
        in_=x_d.ap().rearrange("(c p) t -> p c t", p=128).bitcast(F32R)),
        writes=xall(), tag="ld_x_fm")


def emit_stream_out_fm(C, y_d):
    C.P.dma("SP", lambda e: e.dma_start(
        out=y_d.ap().rearrange("(c p) t -> p c t", p=128),
        in_=C.X[:].rearrange("p (c t) -> p c t", c=DC).bitcast(F32)),
        reads=xall(), tag="st_x_fm")


def emit_stream_in_tm(C, x_d):
    P, al, nc = C.P, C.al, C.nc
    stg = [C.GSCR[:, 0:1024], C.GSCR[:, 1024:2048]]
    for tc in range(NTC):
        s = tc % 2
        P.dma("SP", (lambda tcc, ss: lambda e: e.dma_start(
            out=stg[ss], in_=x_d[tcc * TC:(tcc + 1) * TC, :].bitcast(F32R)))(tc, s),
            writes=[("gscr", s)], tag=f"ld_x_tm{tc}")
        for c in range(DC):
            bank = C.PS[6 + ((tc * DC + c) % 2)]
            bkey = f"PS{6 + ((tc * DC + c) % 2)}"
            P.op("PE", (lambda ss, cc, b: lambda e: e.transpose(
                b[:, 0:128].bitcast(F32R), stg[ss][:, cc * 128:(cc + 1) * 128], C.ident[:]))(s, c, bank),
                reads=[("gscr", s), "c_ident"], writes=[bkey], tag=f"tr_x{tc}_{c}")
            P.op("ACT", (lambda tcc, cc, b: lambda e: e.copy(
                C.X[:, cc * N + tcc * TC: cc * N + (tcc + 1) * TC], b[:, 0:128].bitcast(F32R)))(tc, c, bank),
                reads=[bkey], writes=[("X", c, tc // 4)], tag=f"ev_x{tc}_{c}")


def emit_stream_out_tm(C, y_d):
    P, al = C.P, C.al
    stg = [C.GSCR[:, 0:1024].bitcast(F32), C.GSCR[:, 1024:2048].bitcast(F32)]
    for tc in range(NTC):
        s = tc % 2
        for c in range(DC):
            bank = C.PS[6 + ((tc * DC + c) % 2)]
            bkey = f"PS{6 + ((tc * DC + c) % 2)}"
            P.op("PE", (lambda tcc, cc, b: lambda e: e.transpose(
                b[:, 0:128].bitcast(F32R),
                C.X[:, cc * N + tcc * TC: cc * N + (tcc + 1) * TC], C.ident[:]))(tc, c, bank),
                reads=[("X", c, tc // 4), "c_ident"], writes=[bkey], tag=f"trO{tc}_{c}")
            P.op("ACT", (lambda ss, cc, b: lambda e: e.copy(
                stg[ss][:, cc * 128:(cc + 1) * 128].bitcast(F32R), b[:, 0:128].bitcast(F32R)))(s, c, bank),
                reads=[bkey], writes=[("gscr", s, c)], tag=f"evO{tc}_{c}")
        P.dma("SP", (lambda tcc, ss: lambda e: e.dma_start(
            out=y_d[tcc * TC:(tcc + 1) * TC, :], in_=stg[ss]))(tc, s),
            reads=[("gscr", s, c2) for c2 in range(DC)] + [("gscr", s)], tag=f"st_y{tc}")


def emit_postln(C, tag):
    P, al, nc = C.P, C.al, C.nc
    pref = "ln" + tag
    B = C.ln_bufs
    AK = getattr(C, "ln_alias", [])

    def row(nm2, tt):
        return B[nm2][0:1, (tt % 2) * 512:(tt % 2) * 512 + 512]

    for tt in range(NTT):
        par = tt % 2
        s1b, s2b = (0, 1) if par == 0 else (4, 5)
        bRb, bMb = (2, 3) if par == 0 else (6, 7)
        for c in range(DC):
            i = (tt * DC + c) % 2
            P.op("ACT", (lambda cc, ttt, ii: lambda e: e.activation(
                B["sq"][ii], xsl(C, cc, ttt), AF.Square))(c, tt, i),
                reads=[("X", c, tt)], writes=[("lnsq", i)] + AK, tag=f"{pref}_sq{tt}_{c}")
            P.op("PE", (lambda b1, cc, ttt, st: lambda e: e.matmul(
                C.PS[b1][0:1, :], C.ones_col[:], xsl(C, cc, ttt),
                start=st, stop=(cc == DC - 1), skip_group_check=True))(s1b, c, tt, c == 0),
                reads=[("X", c, tt), "c_ones_col"], writes=[f"PS{s1b}"], tag=f"{pref}_s1_{tt}_{c}")
            P.op("PE", (lambda b2, cc, ii, st: lambda e: e.matmul(
                C.PS[b2][0:1, :], C.ones_col[:], B["sq"][ii],
                start=st, stop=(cc == DC - 1), skip_group_check=True))(s2b, c, i, c == 0),
                reads=[("lnsq", i), "c_ones_col"], writes=[f"PS{s2b}"], tag=f"{pref}_s2_{tt}_{c}")
        P.op("DVE", (lambda b1, ttt: lambda e: e.tensor_scalar(
            row("ms", ttt), C.PS[b1][0:1, :], 1.0 / 1024, None, ALU.mult))(s1b, tt),
            reads=[f"PS{s1b}"], writes=[("lnms", par)] + AK, tag=f"{pref}_ms{tt}")
        P.op("DVE", (lambda b2, ttt: lambda e: e.tensor_scalar(
            row("m2", ttt), C.PS[b2][0:1, :], 1.0 / 1024, None, ALU.mult))(s2b, tt),
            reads=[f"PS{s2b}"], writes=[("lnm2", par)] + AK, tag=f"{pref}_m2{tt}")
        P.op("DVE", (lambda ttt: lambda e: e.tensor_tensor(
            row("sc", ttt), row("ms", ttt), row("ms", ttt), ALU.mult))(tt),
            reads=[("lnms", par)], writes=[("lnsc", par)] + AK, tag=f"{pref}_msq{tt}")
        P.op("DVE", (lambda ttt: lambda e: e.tensor_tensor(
            row("m2", ttt), row("m2", ttt), row("sc", ttt), ALU.subtract))(tt),
            reads=[("lnm2", par), ("lnsc", par)], writes=[("lnm2", par)], tag=f"{pref}_var{tt}")
        P.op("DVE", (lambda ttt: lambda e: e.tensor_scalar(
            row("m2", ttt), row("m2", ttt), 1e-5, None, ALU.add))(tt),
            reads=[("lnm2", par)], writes=[("lnm2", par)], tag=f"{pref}_vep{tt}")
        P.op("ACT", (lambda ttt: lambda e: e.activation(row("m2", ttt), row("m2", ttt), AF.Ln))(tt),
             reads=[("lnm2", par)], writes=[("lnm2", par)], tag=f"{pref}_ln{tt}")
        P.op("ACT", (lambda ttt: lambda e: e.activation(
            row("rs", ttt), row("m2", ttt), AF.Exp, scale=-0.5))(tt),
            reads=[("lnm2", par)], writes=[("lnrs", par)] + AK, tag=f"{pref}_exp{tt}")
        P.op("DVE", (lambda ttt: lambda e: e.tensor_tensor(
            row("nm", ttt), row("ms", ttt), row("rs", ttt), ALU.mult))(tt),
            reads=[("lnms", par), ("lnrs", par)], writes=[("lnnm", par)] + AK, tag=f"{pref}_nm{tt}")
        P.op("PE", (lambda bR, ttt: lambda e: e.matmul(
            C.PS[bR][:], C.ones1[:], row("rs", ttt), start=True, stop=True))(bRb, tt),
            reads=[("lnrs", par), "c_ones1"], writes=[f"PS{bRb}"], tag=f"{pref}_bR{tt}")
        P.op("PE", (lambda bM, ttt: lambda e: e.matmul(
            C.PS[bM][:], C.ones1[:], row("nm", ttt), start=True, stop=True))(bMb, tt),
            reads=[("lnnm", par), "c_ones1"], writes=[f"PS{bMb}"], tag=f"{pref}_bM{tt}")
        P.op("ACT", (lambda bM, ii: lambda e: e.copy(B["bc"][ii], C.PS[bM][:]))(bMb, par),
             reads=[f"PS{bMb}"], writes=[("lnbc", par)] + AK, tag=f"{pref}_bMe{tt}")
        for c in range(DC):
            P.op("DVE", (lambda bR, cc, ttt: lambda e: e.tensor_tensor(
                xsl(C, cc, ttt), xsl(C, cc, ttt), C.PS[bR][:], ALU.mult))(bRb, c, tt),
                reads=[("X", c, tt), f"PS{bRb}"], writes=[("X", c, tt)], tag=f"{pref}_ap1_{tt}_{c}")
            P.op("POOL", (lambda cc, ttt, ii: lambda e: e.tensor_tensor(
                xsl(C, cc, ttt), xsl(C, cc, ttt), B["bc"][ii], ALU.subtract))(c, tt, par),
                reads=[("X", c, tt), ("lnbc", par)], writes=[("X", c, tt)],
                tag=f"{pref}_ap2_{tt}_{c}")


def emit_l0_attn(C, Wq_d, Wk_d, Wv_d, Wo_d, debug=False, upto=None):
    P, al, nc = C.P, C.ba, C.nc
    H, HD = 4, 64

    mask4 = al("l0a_mask4", [128, 512], F32R)
    co128 = al("l0a_co128", [128, 512], F32R)
    onesq = C.mask[:, 63:64]
    onesk = al("l0a_onesk", [128, 1], F32R)
    ri4 = C.GSCR[:, 0:512]
    P.op("POOL", lambda e: e.iota(
        mask4[:].rearrange("p (q j) -> p q j", q=4), [[0, 4], [1, 128]],
        base=0, channel_multiplier=0, allow_small_or_imprecise_dtypes=True),
        writes=[("l0a_mask4",)], tag="c_ci4")
    P.op("POOL", lambda e: e.iota(ri4, [[0, 512]], base=0, channel_multiplier=1,
                                  allow_small_or_imprecise_dtypes=True),
         reads=[], writes=[("gscr", 0)], tag="c_ri4")
    P.op("DVE", lambda e: e.tensor_tensor(mask4[:], mask4[:], ri4, ALU.is_ge),
         reads=[("l0a_mask4",), ("gscr", 0)], writes=[("l0a_mask4",)], tag=("l0a_mask4",))
    P.op("DVE", lambda e: e.tensor_scalar(co128[:], ri4, -1.0, None, ALU.is_ge),
         reads=[("gscr", 0)], writes=[("l0a_co128",)], tag=("l0a_co128",))
    P.op("DVE", lambda e: e.tensor_scalar(onesk[:], C.mask[:, 63:64], 0.5, None, ALU.is_lt),
         reads=["c_mask"], writes=[("l0a_onesk",)], tag="c_onesk")

    VT = al("l0a_vt", [128, NTC * 256], F32R)
    sraw = [VT[:, 0:2048], VT[:, 2048:4096]]
    arena = al("l0a_warena", [128, 4096], F32R)
    WQK = arena[:, 0:4096]
    WQK4 = WQK.rearrange("p (kc h c) -> p kc h c", kc=8, h=4)
    WTv = al("l0a_wvt", [128, 8 * 256], F32R)
    WOT = arena[0:64, 0:4096]
    tb = [0]

    def wload(nm, src, ii):
        P.dma("ACT", (lambda sr: lambda e: e.dma_start(
            out=sraw[ii].rearrange("p (r d) -> p r d", r=2),
            in_=sr.ap().rearrange("(r p) d -> p r d", p=128).bitcast(F32R)))(src),
            writes=[("l0a_sraw", ii)], tag=f"ld_w{nm}")

    wload("q", Wq_d, 0)
    wload("k", Wk_d, 1)
    P.dma("ACT", lambda e: e.dma_start(
        out=C.GSCR[:].rearrange("p (r j) -> p r j", r=8),
        in_=Wo_d.ap().rearrange("(r p) j -> p r j", p=128).bitcast(F32R)),
        writes=[("gscr", 0), ("gscr", 1)], tag="ld_wo")
    for i, nm in enumerate(["q", "k", "v"]):
        if nm == "v":
            wload("v", Wv_d, 0)
        sr = sraw[i % 2]
        for kc in range(DC):
            for r in range(2):
                bnum = 6 + (tb[0] % 2)
                tb[0] += 1
                bank, bkey = C.PS[bnum], f"PS{bnum}"
                P.op("PE", (lambda rr, kcc, bb, sr2: lambda e: e.transpose(
                    bb[:, 0:128].bitcast(F32R),
                    sr2[:, rr * 1024 + kcc * 128: rr * 1024 + (kcc + 1) * 128],
                    C.ident[:]))(r, kc, bank, sr),
                    reads=[("l0a_sraw", i % 2), "c_ident"], writes=[bkey],
                    tag=f"tr_w{nm}_{kc}_{r}")
                if nm == "v":
                    P.op("ACT", (lambda kcc, rr, bb: lambda e: e.copy(
                        WTv[:, kcc * 256 + rr * 128: kcc * 256 + (rr + 1) * 128],
                        bb[:, 0:128].bitcast(F32R)))(kc, r, bank),
                        reads=[bkey], writes=[("l0a_wt", "v")], tag=f"ev_wv_{kc}_{r}")
                else:
                    off = 0 if nm == "q" else 64
                    P.op("ACT", (lambda kcc, rr, bb, o2: lambda e: e.copy(
                        WQK4[:, kcc:kcc + 1, 2 * rr: 2 * rr + 2, o2:o2 + 64],
                        bb[:, 0:128].rearrange("p (k h c) -> p k h c", k=1, c=64)
                        .bitcast(F32R)))(kc, r, bank, off),
                        reads=[bkey], writes=[("l0a_wqk",)], tag=f"ev_w{nm}_{kc}_{r}")

    q4off = C.ba.off
    Q4 = al("l0a_q4", [64, H * N], F32R)
    NQ = C.ba.h[64:65, q4off:q4off + H * N]
    K4 = al("l0a_k4", [64, H * N], F32R)
    T1 = [al(f"l0a_t1{i}", [128, 512], F32) for i in range(2)]
    UB = [al(f"l0a_u{i}", [128, 512], F32) for i in range(2)]
    MB = [al(f"l0a_m{i}", [128, 512], F32) for i in range(2)]
    PB = [al(f"l0a_pb{i}", [64, 512], F32) for i in range(2)]
    SQ = [T1[i][:] for i in range(2)]
    NK1 = [T1[i][0:1, :] for i in range(2)]
    NK2 = [MB[i][0:1, :] for i in range(2)]
    tiles = [(h, tt) for tt in range(NTT) for h in range(H)]

    def qk_proj(idx):
        h, tt = tiles[idx]
        bank, bkey = C.PS[idx % 4], f"PS{idx % 4}"
        for kc in range(DC):
            P.op("PE", (lambda bb, kcc, hh, ttt, st, sp: lambda e: e.matmul(
                bb[:], WQK[:, kcc * 512 + hh * 128: kcc * 512 + (hh + 1) * 128],
                xsl(C, kcc, ttt), start=st, stop=sp))(bank, kc, h, tt, kc == 0, kc == DC - 1),
                reads=[("X", kc, tt), ("l0a_wqk",)], writes=[bkey],
                tag=f"qk_{h}_{tt}_{kc}")

    def qk_chain(idx):
        h, tt = tiles[idx]
        bank, bkey = C.PS[idx % 4], f"PS{idx % 4}"
        i = idx % 2
        a2, b2 = h * N + tt * TT, h * N + (tt + 1) * TT
        q4keys = [("l0a_q4", h, tt * 4 + j) for j in range(4)]
        k4keys = [("l0a_k4", h, tt * 4 + j) for j in range(4)]
        oadd = P.op("DVE", (lambda bb, ii: lambda e: e.tensor_tensor(
            UB[ii][:], bb[:], co128[:], ALU.add))(bank, i),
            reads=[bkey, ("l0a_co128",)], writes=[("l0a_u", i)], tag=f"pu_add{idx}")
        P.op("ACT", (lambda bb, ii: lambda e: e.activation(T1[ii][:], bb[:], AF.Exp))(bank, i),
             reads=[bkey], writes=[("l0a_t1", i)], dep_ops=[oadd], tag=f"pu_exp{idx}")
        P.op("POOL", (lambda ii: lambda e: e.tensor_scalar_min(
            MB[ii][:], T1[ii][:], 1.0))(i),
            reads=[("l0a_t1", i)], writes=[("l0a_m", i)], tag=f"pu_min{idx}")
        P.op("DVE", (lambda ii: lambda e: e.tensor_tensor(
            Q4[:, a2:b2], MB[ii][0:64, :], UB[ii][0:64, :], ALU.max))(i),
            reads=[("l0a_m", i), ("l0a_u", i)], writes=q4keys, tag=f"pu_phq{idx}")
        P.op("DVE", (lambda ii: lambda e: e.tensor_tensor(
            PB[ii][:], MB[ii][64:128, :], UB[ii][64:128, :], ALU.max))(i),
            reads=[("l0a_m", i), ("l0a_u", i)], writes=[("l0a_pb", i)], tag=f"pu_phk{idx}")
        P.op("POOL", (lambda ii: lambda e: e.tensor_tensor(
            SQ[ii][0:64, :], Q4[:, a2:b2], Q4[:, a2:b2], ALU.mult))(i),
            reads=q4keys, writes=[("l0a_sq", i, 0), ("l0a_t1", i)], tag=f"pu_sqq{idx}")
        P.op("DVE", (lambda ii: lambda e: e.tensor_tensor(
            SQ[ii][64:128, :], PB[ii][:], PB[ii][:], ALU.mult))(i),
            reads=[("l0a_pb", i)], writes=[("l0a_sq", i, 1), ("l0a_t1", i)],
            tag=f"pu_sqk{idx}")
        P.op("PE", (lambda ii: lambda e: e.matmul(
            C.PS[4][0:1, :], onesq, SQ[ii][:], start=True, stop=True))(i),
            reads=[("l0a_sq", i, 0), ("l0a_sq", i, 1), "c_mask"], writes=["PS4"],
            tag=f"pu_ssq{idx}")
        P.op("PE", (lambda ii: lambda e: e.matmul(
            C.PS[5][0:1, :], onesk[:], SQ[ii][:], start=True, stop=True))(i),
            reads=[("l0a_sq", i, 0), ("l0a_sq", i, 1), ("l0a_onesk",)], writes=["PS5"],
            tag=f"pu_ssk{idx}")
        P.op("ACT", (lambda aa, bb: lambda e: e.activation(
            NQ[:, aa:bb], C.PS[4][0:1, :], AF.Sqrt))(a2, b2),
            reads=["PS4"], writes=[("l0a_nq", h, tt)], tag=f"pu_nq{idx}")
        P.op("ACT", (lambda ii: lambda e: e.activation(
            NK1[ii], C.PS[5][0:1, :], AF.Sqrt))(i),
            reads=["PS5", ("l0a_t1", i)], writes=[("l0a_nk1", i), ("l0a_t1", i)],
            tag=f"pu_nk{idx}")
        P.op("ACT", (lambda ii: lambda e: e.activation(
            NK2[ii], NK1[ii], AF.Ln, bias=1e-6))(i),
            reads=[("l0a_nk1", i), ("l0a_m", i)], preads=["c_eps6"],
            writes=[("l0a_nk2", i), ("l0a_m", i)], tag=f"pu_ln{idx}")
        P.op("ACT", (lambda ii: lambda e: e.activation(
            NK2[ii], NK2[ii], AF.Exp, scale=-1.0))(i),
            reads=[("l0a_nk2", i)], writes=[("l0a_nk2", i)], tag=f"pu_rc{idx}")
        bb2 = 6 + (idx % 2)
        bank2, bkey2 = C.PS[bb2], f"PS{bb2}"
        P.op("PE", (lambda b3, ii: lambda e: e.matmul(
            b3[0:64, :], C.ones1[0:1, 0:64], NK2[ii], start=True, stop=True))(bank2, i),
            reads=[("l0a_nk2", i), "c_ones1"], writes=[bkey2], tag=f"pu_bc{idx}")
        P.op("DVE", (lambda ii, b3: lambda e: e.tensor_tensor(
            K4[:, a2:b2], PB[ii][:], b3[0:64, :], ALU.mult))(i, bank2),
            reads=[("l0a_pb", i), bkey2], writes=k4keys, tag=f"pu_wr{idx}")

    if upto == "weights":
        return arena, VT
    qk_proj(0)
    qk_proj(1)
    for t in range(2, len(tiles)):
        qk_proj(t)
        qk_chain(t - 2)
    qk_chain(len(tiles) - 2)
    qk_chain(len(tiles) - 1)
    if upto == "qkv":
        return arena, VT

    for tc in range(NTC):
        bv = tc % 4
        bank, bkey = C.PS[bv], f"PS{bv}"
        for kc in range(DC):
            P.op("PE", (lambda bb, kcc, tcc, st, sp: lambda e: e.matmul(
                bb[:, 0:256], C.X[:, kcc * N + tcc * TC: kcc * N + (tcc + 1) * TC],
                WTv[:, kcc * 256:(kcc + 1) * 256], start=st, stop=sp))(bank, kc, tc, kc == 0, kc == DC - 1),
                reads=[("X", kc, tc // 4), ("l0a_wt", "v")], writes=[bkey], tag=f"v_{tc}_{kc}")
        P.op("ACT", (lambda bb, tcc: lambda e: e.copy(
            VT[:, tcc * 256:(tcc + 1) * 256], bb[:, 0:256].bitcast(F32R)))(bank, tc),
            reads=[bkey],
            writes=[("l0a_vt", tc), ("l0a_sraw", 0 if tc < 8 else 1)], tag=f"v_ev{tc}")

    if upto == "v":
        return arena, VT
    for h in range(H):
        for r in range(8):
            bnum = 6 + (tb[0] % 2)
            tb[0] += 1
            bank, bkey = C.PS[bnum], f"PS{bnum}"
            P.op("PE", (lambda hh, rr, bb: lambda e: e.transpose(
                bb[0:64, 0:128].bitcast(F32R),
                C.GSCR[:, rr * 256 + hh * 64: rr * 256 + (hh + 1) * 64], C.ident[:]))(h, r, bank),
                reads=[("gscr", 0), ("gscr", 1), "c_ident"], writes=[bkey], tag=f"tr_wo{h}_{r}")
            P.op("ACT", (lambda hh, rr, bb: lambda e: e.copy(
                WOT[0:64, hh * 1024 + rr * 128: hh * 1024 + (rr + 1) * 128],
                bb[0:64, 0:128].bitcast(F32R)))(h, r, bank),
                reads=[bkey], writes=[("l0a_wot", h), ("l0a_wqk",)],
                tag=f"ev_wo{h}_{r}")

    if upto == "wostage":
        if debug:
            dk0 = C.nc.dram_tensor("dbg_k4pre", [64, H * N], F32, kind="ExternalOutput")
            dn0 = C.nc.dram_tensor("dbg_nq", [1, H * N], F32, kind="ExternalOutput")
            P.dma("SP", lambda e: e.dma_start(out=dk0.ap(), in_=K4[:].bitcast(F32)),
                  reads=[("l0a_k4", h2, t2) for h2 in range(H) for t2 in range(NTC)], tag="dbgk0")
            P.dma("SP", lambda e: e.dma_start(out=dn0.ap(), in_=NQ.bitcast(F32)),
                  reads=[("l0a_nq", h2, t2) for h2 in range(H) for t2 in range(NTT)], tag="dbgn0")
        return arena, VT
    KVK = al("l0a_kvk", [64, 272], F32R)
    SM4 = [al(f"l0a_sm4{i}", [128, 512], F32R) for i in range(2)]
    KTM4 = al("l0a_ktm4", [128, 256], F32R)
    DENR = [T1[i][0:1, :] for i in range(2)]
    RD = [UB[i][0:1, :] for i in range(2)]
    K4v = K4[:].rearrange("p (h t) -> p h t", h=H)
    NQv = NQ.rearrange("p (h t) -> p h t", h=H)

    def rnorm(tc):
        p = tc % 2
        yb, ybk = (C.PS[2], "PS2") if p == 0 else (C.PS[6], "PS6")
        k4keys = [("l0a_k4", h, tc) for h in range(H)]
        P.op("ACT", (lambda pp: lambda e: e.activation(
            RD[pp], DENR[pp], AF.Ln, bias=1e-12))(p),
            reads=[("l0a_denr", p), ("l0a_u", p)], preads=["c_eps12"],
            writes=[("l0a_rd", p), ("l0a_u", p)], tag=f"at_ln{tc}")
        P.op("ACT", (lambda pp: lambda e: e.activation(
            RD[pp], RD[pp], AF.Exp, scale=-1.0))(p),
            reads=[("l0a_rd", p)], writes=[("l0a_rd", p)], tag=f"at_rc{tc}")
        P.op("PE", (lambda pp: lambda e: e.matmul(
            C.PS[5][0:64, :], C.ones1[0:1, 0:64], RD[pp], start=True, stop=True))(p),
            reads=[("l0a_rd", p), "c_ones1"], writes=["PS5"], tag=f"at_bc{tc}")
        P.op("ACT", (lambda bb, tcc: lambda e: e.copy(
            K4v[:, :, tcc * TC:(tcc + 1) * TC],
            bb[0:64, :].rearrange("p (h t) -> p h t", h=H).bitcast(F32R)))(yb, tc),
            reads=[ybk], writes=k4keys, tag=f"at_yc{tc}")
        P.op("DVE", (lambda tcc: lambda e: e.tensor_tensor(
            K4v[:, :, tcc * TC:(tcc + 1) * TC], K4v[:, :, tcc * TC:(tcc + 1) * TC],
            C.PS[5][0:64, :].rearrange("p (h t) -> p h t", h=H), ALU.mult))(tc),
            reads=["PS5"] + k4keys, writes=k4keys, tag=f"at_ym{tc}")

    for tc in range(NTC):
        p = tc % 2
        sb, sbk = (C.PS[0], "PS0") if p == 0 else (C.PS[1], "PS1")
        yb, ybk = (C.PS[2], "PS2") if p == 0 else (C.PS[6], "PS6")
        qsl = [(h * N + tc * TC, h * N + (tc + 1) * TC) for h in range(H)]
        for h in range(H):
            P.op("PE", (lambda bb, aa, bb2, hh: lambda e: e.matmul(
                bb[:, hh * TC:(hh + 1) * TC], K4[:, aa:bb2], Q4[:, aa:bb2],
                start=(hh == 0), stop=(hh == H - 1), skip_group_check=(hh > 0)))(sb, *qsl[h], h),
                reads=[("l0a_q4", h, tc), ("l0a_k4", h, tc)], writes=[sbk],
                tag=f"at_s{h}_{tc}")
        P.op("DVE", (lambda bb, pp: lambda e: e.tensor_tensor(
            SM4[pp][:], bb[:], mask4[:], ALU.mult))(sb, p),
            reads=[sbk, ("l0a_mask4",)], writes=[("l0a_sm4", p)], tag=f"at_sm{tc}")
        P.op("PE", (lambda pp, solo: lambda e: e.matmul(
            C.PS[3][0:1, :], C.ones_col[:], SM4[pp][:], start=True, stop=solo))(p, tc == 0),
            reads=[("l0a_sm4", p), "c_ones_col"], writes=["PS3"], tag=f"at_d1_{tc}")
        if tc > 0:
            for h in range(H):
                P.op("PE", (lambda hh, aa, bb2: lambda e: e.matmul(
                    C.PS[3][0:1, hh * TC:(hh + 1) * TC], KVK[:, 256 + hh:257 + hh],
                    Q4[:, aa:bb2], start=False, stop=(hh == H - 1),
                    skip_group_check=True))(h, *qsl[h]),
                    reads=[("l0a_q4", h, tc), ("l0a_kvk",)], writes=["PS3"],
                    tag=f"at_d2{h}_{tc}")
        P.op("DVE", (lambda pp, tcc: lambda e: e.scalar_tensor_tensor(
            DENR[pp].rearrange("p (h t) -> p h t", h=H),
            NQv[:, :, tcc * TC:(tcc + 1) * TC], 1e-6,
            C.PS[3][0:1, :].rearrange("p (h t) -> p h t", h=H),
            ALU.mult, ALU.add))(p, tc),
            reads=["PS3"] + [("l0a_nq", h, tc // 4) for h in range(H)],
            writes=[("l0a_denr", p), ("l0a_t1", p)], tag=f"at_de{tc}")
        for h in range(H):
            P.op("PE", (lambda aa, bb2, hh: lambda e: e.transpose(
                C.PS[7][:, hh * 64:(hh + 1) * 64].bitcast(F32R), K4[:, aa:bb2],
                C.ident[0:64, 0:64]))(*qsl[h], h),
                reads=[("l0a_k4", h, tc), "c_ident"], writes=["PS7"], tag=f"at_kt{h}_{tc}")
        P.op("ACT", lambda e: e.copy(KTM4[:], C.PS[7][:, 0:256].bitcast(F32R)),
             reads=["PS7"], writes=[("l0a_ktm4",)], tag=f"at_kte{tc}")
        for h in range(H):
            P.op("PE", (lambda bb, pp, hh, tcc, st, sp: lambda e: e.matmul(
                bb[0:64, hh * TC:(hh + 1) * TC],
                VT[:, tcc * 256 + hh * 64: tcc * 256 + (hh + 1) * 64],
                SM4[pp][:, hh * TC:(hh + 1) * TC], start=st, stop=sp,
                skip_group_check=(hh > 0)))(yb, p, h, tc, h == 0, tc == 0 and h == H - 1),
                reads=[("l0a_vt", tc), ("l0a_sm4", p)], writes=[ybk], tag=f"at_y1{h}_{tc}")
        if tc > 0:
            for h in range(H):
                P.op("PE", (lambda bb, hh, aa, bb2: lambda e: e.matmul(
                    bb[0:64, hh * TC:(hh + 1) * TC], KVK[:, hh * 64:(hh + 1) * 64],
                    Q4[:, aa:bb2], start=False, stop=(hh == H - 1),
                    skip_group_check=True))(yb, h, *qsl[h]),
                    reads=[("l0a_q4", h, tc), ("l0a_kvk",)], writes=[ybk],
                    tag=f"at_y2{h}_{tc}")
        gf = (tc == 0)
        for h in range(H):
            P.op("PE", (lambda pp, hh, tcc, st: lambda e: e.matmul(
                C.PS[4][0:64, hh * 64:(hh + 1) * 64], KTM4[:, hh * 64:(hh + 1) * 64],
                VT[:, tcc * 256 + hh * 64: tcc * 256 + (hh + 1) * 64],
                start=st, stop=(tcc == NTC - 1), skip_group_check=(not st)))(p, h, tc, gf and h == 0),
                reads=[("l0a_ktm4",), ("l0a_vt", tc)], writes=["PS4"], tag=f"at_kv{h}_{tc}")
        for h in range(H):
            P.op("PE", (lambda pp, hh, tcc: lambda e: e.matmul(
                C.PS[4][0:64, 256 + hh:257 + hh], KTM4[:, hh * 64:(hh + 1) * 64].bitcast(F32),
                C.ones_col[:].bitcast(F32), start=False, stop=(tcc == NTC - 1),
                skip_group_check=True))(p, h, tc),
                reads=[("l0a_ktm4",), "c_ones_col"], writes=["PS4"], tag=f"at_ks{h}_{tc}")
        if tc < NTC - 1:
            P.op("ACT", lambda e: e.copy(KVK[:, 0:260], C.PS[4][0:64, 0:260].bitcast(F32R)),
                 reads=["PS4"], writes=[("l0a_kvk",)], tag=f"at_kvev{tc}")
        if tc > 0:
            rnorm(tc - 1)
    rnorm(NTC - 1)

    if upto == "attn":
        return arena, VT
    if debug:
        dq = C.nc.dram_tensor("dbg_q4", [64, H * N], F32, kind="ExternalOutput")
        dk = C.nc.dram_tensor("dbg_k4", [64, H * N], F32, kind="ExternalOutput")
        dv = C.nc.dram_tensor("dbg_vt", [128, NTC * 256], F32, kind="ExternalOutput")
        P.dma("SP", lambda e: e.dma_start(out=dq.ap(), in_=Q4[:].bitcast(F32)),
              reads=[("l0a_q4", h2, t2) for h2 in range(H) for t2 in range(NTC)], tag="dbg_q")
        P.dma("SP", lambda e: e.dma_start(out=dk.ap(), in_=K4[:].bitcast(F32)),
              reads=[("l0a_k4", h2, t2) for h2 in range(H) for t2 in range(NTC)], tag="dbg_k")
        P.dma("SP", lambda e: e.dma_start(out=dv.ap(), in_=VT[:].bitcast(F32)),
              reads=[("l0a_vt", t2) for t2 in range(NTC)], tag="dbg_v")

    for och in range(DC):
        for tt in range(NTT):
            b = (och * NTT + tt) % 4
            bank, bkey = C.PS[b], f"PS{b}"
            for h in range(H):
                P.op("PE", (lambda bb, hh, occ, ttt, st, sp: lambda e: e.matmul(
                    bb[:], WOT[0:64, hh * 1024 + occ * 128: hh * 1024 + (occ + 1) * 128],
                    K4[:, hh * N + ttt * TT: hh * N + (ttt + 1) * TT],
                    start=st, stop=sp))(bank, h, och, tt, h == 0, h == H - 1),
                    reads=[("l0a_wot", h)] + [("l0a_k4", h, tt * 4 + j) for j in range(4)],
                    writes=[bkey], tag=f"wo_{och}_{tt}_{h}")
            P.op("DVE", (lambda bb, occ, ttt: lambda e: e.tensor_tensor(
                xsl(C, occ, ttt), xsl(C, occ, ttt), bb[:], ALU.add))(bank, och, tt),
                reads=[("X", och, tt), bkey], writes=[("X", och, tt)], tag=f"wo_res{och}_{tt}")
    return arena, VT


def build_prog_l0_attn(first=True, debug=False, upto=None):
    nc = bass.Bass("TRN2", target_bir_lowering=False)
    x_d = nc.dram_tensor("x", [N, 1024], F32, kind="ExternalInput")
    wq = nc.dram_tensor("wq", [256, 1024], F32, kind="ExternalInput")
    wk = nc.dram_tensor("wk", [256, 1024], F32, kind="ExternalInput")
    wv = nc.dram_tensor("wv", [256, 1024], F32, kind="ExternalInput")
    wo = nc.dram_tensor("wo", [1024, 256], F32, kind="ExternalInput")
    xo = nc.dram_tensor("xo", [1024, N], F32, kind="ExternalOutput")
    C = make_ctx(nc)
    C.ba = ArenaAlloc(C.al("arena", [128, ACOLS], F32R), ACOLS)
    emit_consts(C)
    emit_stream_in_tm(C, x_d)
    arena, VT = emit_l0_attn(C, wq, wk, wv, wo, debug=debug, upto=upto)
    C.ln_bufs = {
        "sq": [arena[:, 0:512], arena[:, 512:1024]],
        "bc": [arena[:, 1024:1536], arena[:, 1536:2048]],
        "ms": arena[0:1, 2048:3072],
        "m2": arena[0:1, 3072:4096],
        "sc": VT[0:1, 0:1024],
        "rs": VT[0:1, 1024:2048],
        "nm": VT[0:1, 2048:3072],
    }
    C.ln_alias = ([("l0a_wot", h) for h in range(4)] +
                  [("l0a_wqk",), ("l0a_sraw", 0), ("l0a_sraw", 1)] +
                  [("l0a_vt", t) for t in range(12)])
    emit_postln(C, "0a")
    emit_stream_out_fm(C, xo)
    with ExitStack() as stack:
        C.P.build(stack)
    return nc, C


_PROG_CACHE = {}


def _get_prog(name, builder):
    if name not in _PROG_CACHE:
        _PROG_CACHE[name] = builder()
    return _PROG_CACHE[name]


def emit_l0_ffn(C, router_d, wg_d, wu_d, wd_d, wsg_d, wsu_d, wsd_d, debug=False, vex_filter=None):
    P, al, nc = C.P, C.ba, C.nc
    E = 8

    RING = al("ffn_ring", [128, 3 * 4096], F32R)
    RW = al("ffn_rw", [128, 64], F32R)
    SFMs = al("ffn_sfms", [8, 512], F32R)
    STM = al("ffn_stm", [128, 128], F32)
    SCR = al("ffn_scr", [128, 128], F32)
    CNT = al("ffn_cnt", [128, 128], F32)
    MWF = al("ffn_mwf", [8, N], F32R)
    MW8 = al("ffn_mw8", [128, 128], F32R)
    RI8 = al("ffn_ri8", [8, 1024], F32)
    OH8 = al("ffn_oh8", [8, 1024], F32R)
    BLK = OH8
    SX = al("ffn_sx", [128, 8], F32)
    RMX = al("ffn_rmx", [128, 1], F32)
    RSM = al("ffn_rsm", [128, 1], F32)
    CEQ = al("ffn_ceq", [128, 1], F32)
    COLB = [al(f"ffn_col{j}", [128, 1], F32) for j in range(8)]
    SG = [al(f"ffn_sg{i}", [128, 512], F32) for i in range(2)]
    HA = [al(f"ffn_ha{i}", [128, 512], F32) for i in range(2)]
    H2 = [al(f"ffn_h2{i}", [128, 4 * 512], F32R) for i in range(2)]
    BCW = [al(f"ffn_bcw{i}", [128, 512], F32R) for i in range(2)]
    YH = al("ffn_yh", [128, DC * 1024], F32R)

    P.op("POOL", lambda e: e.iota(RI8[:], [[0, 1024]], base=0, channel_multiplier=1,
                                  allow_small_or_imprecise_dtypes=True), writes=["ffn_ri8"])
    P.op("POOL", lambda e: e.iota(BLK[:].rearrange("p (b j) -> p b j", b=8), [[1, 8], [0, 128]],
                                  base=0, channel_multiplier=0,
                                  allow_small_or_imprecise_dtypes=True), writes=["ffn_oh8"])
    P.op("DVE", lambda e: e.tensor_tensor(OH8[:], RI8[:], OH8[:], ALU.is_equal),
         reads=["ffn_ri8", "ffn_oh8"], writes=["ffn_oh8"])

    P.dma("SP", lambda e: e.dma_start(
        out=RW[:].rearrange("p (k e) -> p k e", k=8),
        in_=router_d.ap().rearrange("(k p) e -> p k e", p=128).bitcast(F32R)),
        writes=["ffn_rw"], tag="ld_rw")
    tk_pending = []
    for tt in range(NTT):
        b = tt % 2
        bank, bkey = C.PS[b], f"PS{b}"
        for kc in range(DC):
            P.op("PE", (lambda bb, kcc, ttt, st, sp: lambda e: e.matmul(
                bb[0:8, :], RW[:, kcc * 8:(kcc + 1) * 8], xsl(C, kcc, ttt),
                start=st, stop=sp))(bank, kc, tt, kc == 0, kc == DC - 1),
                reads=[("X", kc, tt), "ffn_rw"], writes=[bkey], tag=f"rt_{tt}_{kc}")
        P.op("DVE", (lambda bb: lambda e: e.tensor_copy(SFMs[:], bb[0:8, :]))(bank),
            reads=[bkey], writes=["ffn_sfms"], tag=f"rt_ev{tt}")
        for tcl in range(4):
            tc = tt * 4 + tcl
            b2 = 2 + (tcl % 2)
            bank2, bkey2 = C.PS[b2], f"PS{b2}"
            P.op("PE", (lambda bb, tcll: lambda e: e.transpose(
                bb[0:128, 0:8].bitcast(F32R), SFMs[:, tcll * TC:(tcll + 1) * TC],
                C.ident[0:8, 0:8]))(bank2, tcl),
                reads=["ffn_sfms", "c_ident"], writes=[bkey2], tag=f"rtT{tc}")
            P.op("DVE", (lambda bb, tcc: lambda e: e.tensor_copy(
                STM[:, tcc * 8:(tcc + 1) * 8], bb[0:128, 0:8].bitcast(F32R)))(bank2, tc),
                reads=[bkey2], writes=[("ffn_stm", tc)], tag=f"rtTe{tc}")
            sl = (tc * 8, (tc + 1) * 8)
            P.op("DVE", (lambda a, bb2: lambda e: e.reduce_max(RMX[:], STM[:, a:bb2], axis=AX.X))(*sl),
                 preads=[("ffn_stm", tc)], writes=["ffn_rmx"], tag=f"sm_mx{tc}", fence_out=True)
            P.op("DVE", (lambda a, bb2: lambda e: e.tensor_scalar(SX[:], STM[:, a:bb2], RMX[:].bitcast(F32), None,
                                                                  ALU.subtract))(*sl),
                 reads=[("ffn_stm", tc)], preads=["ffn_rmx"], writes=["ffn_sx"], tag=f"sm_sb{tc}")
            P.op("ACT", lambda e: e.activation(SX[:], SX[:], AF.Exp),
                 reads=["ffn_sx"], writes=["ffn_sx"], tag=f"sm_ex{tc}")
            P.op("DVE", lambda e: e.reduce_sum(RSM[:], SX[:], axis=AX.X),
                 preads=["ffn_sx"], writes=["ffn_rsm"], tag=f"sm_su{tc}", fence_out=True)
            P.op("DVE", lambda e: e.reciprocal(RMX[:], RSM[:]),
                 preads=["ffn_rsm"], writes=["ffn_rmx"], tag=f"sm_rc{tc}")
            P.op("DVE", (lambda a, bb2: lambda e: e.tensor_scalar(SCR[:, a:bb2], SX[:], RMX[:].bitcast(F32), None,
                                                                  ALU.mult))(*sl),
                 reads=["ffn_sx"], preads=["ffn_rmx"], writes=[("ffn_scr", tc)], tag=f"sm_ml{tc}")
            for ex in range(8):
                P.op("POOL", (lambda a, exx: lambda e: e.tensor_copy(
                    COLB[exx][:], SCR[:, a + exx:a + exx + 1]))(tc * 8, ex),
                    reads=[("ffn_scr", tc)], writes=[("ffn_colb", ex)], tag=f"tk_c{tc}_{ex}")
            for ex in range(8):
                P.op("DVE", (lambda a, bb2, exx: lambda e: e.tensor_scalar(
                    SX[:], SCR[:, a:bb2], COLB[exx][:].bitcast(F32), None, ALU.is_gt))(*sl, ex),
                    reads=[("ffn_scr", tc)], preads=[("ffn_colb", ex)], writes=["ffn_sx"],
                    tag=f"tk_gt{tc}_{ex}")
                if ex > 0:
                    P.op("DVE", (lambda a, exx: lambda e: e.tensor_scalar(
                        SX[:, 0:exx], SCR[:, a:a + exx], COLB[exx][:].bitcast(F32), None, ALU.is_ge))(tc * 8, ex),
                        reads=[("ffn_scr", tc)], preads=[("ffn_colb", ex)], writes=["ffn_sx"],
                        tag=f"tk_eq{tc}_{ex}")
                P.op("DVE", (lambda a, exx: lambda e: e.reduce_sum(
                    CNT[:, a + exx:a + exx + 1], SX[:], axis=AX.X))(tc * 8, ex),
                    preads=["ffn_sx"], writes=[("ffn_cnt", tc)], tag=f"tk_s{tc}_{ex}")
            P.op("DVE", (lambda a, bb2: lambda e: e.tensor_scalar(
                CNT[:, a:bb2], CNT[:, a:bb2], 5.5, None, ALU.is_le))(*sl),
                reads=[], preads=[("ffn_cnt", tc)], writes=[("ffn_cnt", tc)], tag=f"tk_le{tc}")
            P.op("DVE", (lambda a, bb2: lambda e: e.tensor_tensor(
                MW8[:, a:bb2], SCR[:, a:bb2], CNT[:, a:bb2], ALU.mult))(*sl),
                reads=[("ffn_scr", tc), ("ffn_cnt", tc)], writes=[("ffn_mw8", tc)], tag=f"tk_mw{tc}")
            def mk_tkT(tc2, a2, b4, b3):
                bank3, bkey3 = C.PS[b3], f"PS{b3}"

                def emit():
                    P.op("PE", (lambda bk: lambda e: e.transpose(
                        bk[0:8, 0:128].bitcast(F32R), MW8[:, a2:b4], C.ident[:]))(bank3),
                        reads=[("ffn_mw8", tc2), "c_ident"], writes=[bkey3], tag=f"tk_T{tc2}")
                    P.op("DVE", (lambda bk: lambda e: e.tensor_copy(
                        MWF[:, tc2 * TC:(tc2 + 1) * TC], bk[0:8, 0:128].bitcast(F32R)))(bank3),
                        reads=[bkey3], writes=[("ffn_mwf", tc2)], tag=f"tk_Te{tc2}")
                return emit
            tk_pending.append(mk_tkT(tc, sl[0], sl[1], 4 + (tcl % 2)))

    if debug:
        dmw = nc.dram_tensor("dbg_mwf", [8, N], F32, kind="ExternalOutput")
        P.dma("SP", lambda e: e.dma_start(out=dmw.ap(), in_=MWF[:].bitcast(F32)),
              reads=[("ffn_mwf", t2) for t2 in range(NTC)], tag="dbg_mwf")

    vex = []
    for fs in range(2):
        vex.append((wsg_d[:, fs * 512:(fs + 1) * 512], wsu_d[:, fs * 512:(fs + 1) * 512],
                    wsd_d[fs * 512:(fs + 1) * 512, :], False, f"s{fs}"))
    for e in range(E):
        vex.append((wg_d[e], wu_d[e], wd_d[e], True, f"e{e}"))
    if vex_filter is not None:
        vex = [v for v in vex if v[4] in vex_filter]

    rp = [0]

    def ring_load(src_ap, kparts, tag):
        i = rp[0] % 3
        rp[0] += 1
        cols = src_ap.shape[-1]
        view = RING[:, i * 4096: i * 4096 + kparts * cols]
        P.dma("SP", (lambda v, sa, kp: lambda e: e.dma_start(
            out=v.rearrange("p (k c) -> p k c", k=kp),
            in_=sa.rearrange("(k p) c -> p k c", p=128).bitcast(F32R)))(view, src_ap, kparts),
            writes=[("ring", i)], tag="rl_" + tag)
        return view, ("ring", i)

    dcnt = [0]
    for th in range(2):
        for vi, (g_ap, u_ap, d_ap, scaled, nm) in enumerate(vex):
            GV, gk = ring_load(g_ap, 8, f"{nm}g{th}")
            UV, uk = ring_load(u_ap, 8, f"{nm}u{th}")
            DV, dk = ring_load(d_ap, 4, f"{nm}d{th}")
            ei = int(nm[1]) if nm[0] == "e" else None
            for ttl in range(2):
                tt = th * 2 + ttl
                for fch in range(4):
                    gb, ub = fch % 2, 2 + (fch % 2)
                    gbank, gbkey = C.PS[gb], f"PS{gb}"
                    ubank, ubkey = C.PS[ub], f"PS{ub}"
                    for kc in range(DC):
                        P.op("PE", (lambda bb, gv, kcc, fc, ttt, st, sp: lambda e: e.matmul(
                            bb[:], gv[:, kcc * 512 + fc * 128: kcc * 512 + (fc + 1) * 128],
                            xsl(C, kcc, ttt), start=st, stop=sp))(gbank, GV, kc, fch, tt,
                                                                 kc == 0, kc == DC - 1),
                            reads=[("X", kc, tt), gk], writes=[gbkey], tag=f"g{nm}_{tt}_{fch}_{kc}")
                    for kc in range(DC):
                        P.op("PE", (lambda bb, uv, kcc, fc, ttt, st, sp: lambda e: e.matmul(
                            bb[:], uv[:, kcc * 512 + fc * 128: kcc * 512 + (fc + 1) * 128],
                            xsl(C, kcc, ttt), start=st, stop=sp))(ubank, UV, kc, fch, tt,
                                                                 kc == 0, kc == DC - 1),
                            reads=[("X", kc, tt), uk], writes=[ubkey], tag=f"u{nm}_{tt}_{fch}_{kc}")
                    i = fch % 2
                    P.op("ACT", (lambda bb, ii: lambda e: e.activation(SG[ii][:], bb[:], AF.Silu))(gbank, i),
                         reads=[gbkey], writes=[("ffn_sg", i)], tag=f"sl{nm}_{tt}_{fch}")
                    P.op("DVE", (lambda bb, ii, fc, wi: lambda e: e.tensor_tensor(
                        H2[wi][:, fc * 512:(fc + 1) * 512], SG[ii][:], bb[:],
                        ALU.mult))(ubank, i, fch, ttl),
                        reads=[("ffn_sg", i), ubkey],
                        writes=[("ffn_h2", ttl, fch)], tag=f"hm{nm}_{tt}_{fch}")
                    if not scaled and tk_pending:
                        tk_pending.pop(0)()
                        if tk_pending:
                            tk_pending.pop(0)()
                if scaled:
                    while tk_pending:
                        tk_pending.pop(0)()
                    bcb = 4 + (dcnt[0] % 4)
                    dcnt[0] += 1
                    bank, bkey = C.PS[bcb], f"PS{bcb}"
                    P.op("PE", (lambda bb, eee, ttt: lambda e: e.matmul(
                        bb[:], OH8[:, eee * 128:(eee + 1) * 128],
                        MWF[:, ttt * TT:(ttt + 1) * TT], start=True, stop=True))(bank, ei, tt),
                        reads=["ffn_oh8"] + [("ffn_mwf", tt * 4 + j) for j in range(4)],
                        writes=[bkey], tag=f"bcw{nm}_{tt}")
                    P.op("ACT", (lambda bb, ii: lambda e: e.copy(BCW[ii][:], bb[:]))(bank, ttl),
                         reads=[bkey], writes=[("ffn_bcw", ttl)], tag=f"bcwe{nm}_{tt}")
                    for fch in range(4):
                        P.op("POOL", (lambda fc, wi: lambda e: e.tensor_tensor(
                            H2[wi][:, fc * 512:(fc + 1) * 512],
                            H2[wi][:, fc * 512:(fc + 1) * 512], BCW[wi][:],
                            ALU.mult))(fch, ttl),
                            reads=[("ffn_h2", ttl, fch), ("ffn_bcw", ttl)],
                            writes=[("ffn_h2", ttl, fch)], tag=f"hw{nm}_{tt}_{fch}")
                for och in range(DC):
                    db = 4 + (dcnt[0] % 4)
                    dcnt[0] += 1
                    bank, bkey = C.PS[db], f"PS{db}"
                    for kc2 in range(4):
                        P.op("PE", (lambda bb, dv, kcc, occ, wi, st, sp: lambda e: e.matmul(
                            bb[:], dv[:, kcc * 1024 + occ * 128: kcc * 1024 + (occ + 1) * 128],
                            H2[wi][:, kcc * 512:(kcc + 1) * 512], start=st, stop=sp))(
                                bank, DV, kc2, och, ttl, kc2 == 0, kc2 == 3),
                            reads=[("ffn_h2", ttl, kc2), dk], writes=[bkey],
                            tag=f"d{nm}_{tt}_{och}_{kc2}")
                    yh = YH[:, och * 1024 + ttl * 512: och * 1024 + (ttl + 1) * 512]
                    if vi == 0:
                        P.op("DVE", (lambda bb, y2: lambda e: e.tensor_copy(y2, bb[:]))(bank, yh),
                            reads=[bkey], writes=[("ffn_yh", och, ttl)], tag=f"dy{nm}_{tt}_{och}")
                    else:
                        P.op("DVE", (lambda bb, y2: lambda e: e.tensor_tensor(
                            y2, y2, bb[:], ALU.add))(bank, yh),
                            reads=[bkey, ("ffn_yh", och, ttl)], writes=[("ffn_yh", och, ttl)],
                            tag=f"dy{nm}_{tt}_{och}")
        for och in range(DC):
            for ttl in range(2):
                tt = th * 2 + ttl
                yh = YH[:, och * 1024 + ttl * 512: och * 1024 + (ttl + 1) * 512]
                P.op("DVE", (lambda occ, ttt, y2: lambda e: e.tensor_tensor(
                    xsl(C, occ, ttt), xsl(C, occ, ttt), y2, ALU.add))(och, tt, yh),
                    reads=[("X", och, tt), ("ffn_yh", och, ttl)], writes=[("X", och, tt)],
                    tag=f"xadd{th}_{och}_{tt}")
    if debug:
        dx = nc.dram_tensor("dbg_x", [1024, N], F32, kind="ExternalOutput")
        P.dma("SP", lambda e: e.dma_start(
            out=dx.ap().rearrange("(c p) t -> p c t", p=128),
            in_=C.X[:].rearrange("p (c t) -> p c t", c=DC).bitcast(F32)),
            reads=xall(), tag="dbg_x")
    return RING


def build_prog_l0_ffn(debug=False, vex_filter=None):
    nc = bass.Bass("TRN2", target_bir_lowering=False)
    x_d = nc.dram_tensor("x", [1024, N], F32, kind="ExternalInput")
    rw = nc.dram_tensor("rw", [1024, 8], F32, kind="ExternalInput")
    wg = nc.dram_tensor("wg", [8, 1024, 512], F32, kind="ExternalInput")
    wu = nc.dram_tensor("wu", [8, 1024, 512], F32, kind="ExternalInput")
    wd = nc.dram_tensor("wd", [8, 512, 1024], F32, kind="ExternalInput")
    wsg = nc.dram_tensor("wsg", [1024, 1024], F32, kind="ExternalInput")
    wsu = nc.dram_tensor("wsu", [1024, 1024], F32, kind="ExternalInput")
    wsd = nc.dram_tensor("wsd", [1024, 1024], F32, kind="ExternalInput")
    xo = nc.dram_tensor("xo", [1024, N], F32, kind="ExternalOutput")
    C = make_ctx(nc)
    C.ba = ArenaAlloc(C.al("arena", [128, ACOLS], F32R), ACOLS)
    emit_consts(C)
    emit_stream_in_fm(C, x_d)
    RING = emit_l0_ffn(C, rw, wg.ap(), wu.ap(), wd.ap(), wsg.ap(), wsu.ap(), wsd.ap(),
                       debug=debug, vex_filter=vex_filter)
    C.ln_bufs = {
        "sq": [RING[:, 0:512], RING[:, 512:1024]],
        "bc": [RING[:, 1024:1536], RING[:, 1536:2048]],
        "ms": RING[0:1, 2048:3072],
        "m2": RING[0:1, 3072:4096],
        "sc": RING[0:1, 4096:5120],
        "rs": RING[0:1, 5120:6144],
        "nm": RING[0:1, 6144:7168],
    }
    C.ln_alias = [("ring", 0), ("ring", 1)]
    emit_postln(C, "0f")
    emit_stream_out_fm(C, xo)
    with ExitStack() as stack:
        C.P.build(stack)
    return nc, C


def emit_l1_attn(C, Wq_d, bq_d, Wk_d, bk_d, Wv_d, bv_d, Wp_d, bp_d, wqf_d, wkf_d):
    P, al, nc = C.P, C.ba, C.nc

    RINGW = al("l1a_ringw", [128, 3 * 1024], F32R)
    PAIRBUF = al("l1a_pairbuf", [128, 6 * 2048], F32R)
    KHP = [PAIRBUF[:, (3 * i + 0) * 2048:(3 * i + 1) * 2048] for i in range(2)]
    VHP = [PAIRBUF[:, (3 * i + 1) * 2048:(3 * i + 2) * 2048] for i in range(2)]
    QHP = [PAIRBUF[:, (3 * i + 2) * 2048:(3 * i + 3) * 2048] for i in range(2)]
    KP = al("l1a_kp", [128, NTC * 512], F32R)
    VTM = al("l1a_vtm", [128, NTC * 128], F32R)
    OUTP = VTM
    QPT = [al(f"l1a_qpt{i}", [128, 4 * 512], F32R) for i in range(2)]
    CTXP = [al(f"l1a_ctxp{i}", [128, 512], F32R) for i in range(2)]
    WQF = al("l1a_wqf", [128, 512], F32R)
    WKF = al("l1a_wkf", [128, 512], F32R)
    BQA = al("l1a_bqa", [128, 8], F32R)
    BKA = al("l1a_bka", [128, 8], F32R)
    BVA = al("l1a_bva", [128, 8], F32R)
    BPA = al("l1a_bpa", [128, 8], F32R)
    BC = {k: al(f"l1a_bc{k}", [128, 1], F32R) for k in ("q", "k", "v", "p")}
    BC2 = {k: al(f"l1a_bc2{k}", [128, 1], F32R) for k in ("q", "k", "v")}
    KSR = [al("l1a_ksr0", [1, 512], F32R)] * 2
    KSP = [al(f"l1a_ksp{i}", [128, 4], F32R) for i in range(2)]
    _den = al("l1a_den0", [1, 512], F32R)
    _rd = al("l1a_rd10", [1, 512], F32R)
    DEN = [_den, _den]
    RD1 = [_rd, _rd]
    BST = KP[0:8, 0:128]
    outf = nc.dram_tensor("l1a_outf", [1024, N], F32, kind="Internal")

    WPJ = KP[:, 0:8 * 1024]
    OFV = PAIRBUF[:, 0:8 * 1024]

    for (T_, nm, src) in [(WQF, "l1a_wqf", wqf_d), (WKF, "l1a_wkf", wkf_d)]:
        P.op("POOL", (lambda t: lambda e: e.iota(t[:], [[0, 512]], base=0, channel_multiplier=0,
                                                 allow_small_or_imprecise_dtypes=True))(T_),
             writes=[nm], tag=nm + "_z")
        P.dma("SP", (lambda t, sa: lambda e: e.dma_start(out=t[0:64, 0:256],
                                                         in_=sa.ap().bitcast(F32R)))(T_, src),
              writes=[nm], tag=nm + "_a")
        P.dma("SP", (lambda t, sa: lambda e: e.dma_start(out=t[64:128, 256:512],
                                                         in_=sa.ap().bitcast(F32R)))(T_, src),
              writes=[nm], tag=nm + "_b")
    for (Tb, nm, src) in [(BQA, "l1a_bqa", bq_d), (BKA, "l1a_bka", bk_d),
                          (BVA, "l1a_bva", bv_d), (BPA, "l1a_bpa", bp_d)]:
        P.dma("SP", (lambda sa: lambda e: e.dma_start(
            out=BST, in_=sa.ap().rearrange("(c p) -> c p", p=128).bitcast(F32R)))(src),
            writes=["l1a_bst", ("l1a_kp", 0)], tag=nm + "_ld")
        P.op("PE", lambda e: e.transpose(C.PS[6][:, 0:8].bitcast(F32R), BST, C.ident[0:8, 0:8]),
             reads=["l1a_bst", "c_ident"], writes=["PS6"], tag=nm + "_tr")
        P.op("DVE", (lambda t: lambda e: e.tensor_copy(t[:], C.PS[6][:, 0:8].bitcast(F32R)))(Tb),
             reads=["PS6"], writes=[nm], tag=nm + "_ev")

    rp = [0]

    def wload(src_col_ap, tag):
        i = rp[0] % 3
        rp[0] += 1
        view = RINGW[:, i * 1024:(i + 1) * 1024]
        P.dma("SP", (lambda v, sa: lambda e: e.dma_start(
            out=v.rearrange("p (k j) -> p k j", k=8),
            in_=sa.rearrange("(k p) j -> p k j", p=128).bitcast(F32R)))(view, src_col_ap),
            writes=[("l1a_ring", i)], tag="wl_" + tag)
        return view, ("l1a_ring", i)

    def stage_A(p):
        pi = p % 2
        csl = (p * 128, (p + 1) * 128)
        WKV, wkk = wload(Wk_d[:, csl[0]:csl[1]], f"k{p}")
        WVV, wvk = wload(Wv_d[:, csl[0]:csl[1]], f"v{p}")
        WQV, wqk = wload(Wq_d[:, csl[0]:csl[1]], f"q{p}")
        bcs = BC if pi == 0 else BC2
        for (dstk, srcb, nmb, key) in [("k", BKA, f"l1a_bck{pi}", "l1a_bka"),
                                       ("v", BVA, f"l1a_bcv{pi}", "l1a_bva"),
                                       ("q", BQA, f"l1a_bcq{pi}", "l1a_bqa")]:
            P.op("DVE", (lambda d, s2, pp: lambda e: e.tensor_copy(d[:], s2[:, pp:pp + 1]))(bcs[dstk], srcb, p),
                 reads=[key], writes=[nmb], tag=f"{nmb}_{p}")
        for (dst, dkey, WV, wk2, bkey, bcol) in [
                (KHP[pi], f"l1a_khp{pi}", WKV, wkk, f"l1a_bck{pi}", bcs["k"]),
                (VHP[pi], f"l1a_vhp{pi}", WVV, wvk, f"l1a_bcv{pi}", bcs["v"]),
                (QHP[pi], f"l1a_qhp{pi}", WQV, wqk, f"l1a_bcq{pi}", bcs["q"])]:
            for tt in range(NTT):
                b2 = tt % 2
                bank, bkey2 = C.PS[b2], f"PS{b2}"
                for kc in range(DC):
                    P.op("PE", (lambda bb, wv, kcc, ttt, st, sp: lambda e: e.matmul(
                        bb[:], wv[:, kcc * 128:(kcc + 1) * 128], xsl(C, kcc, ttt),
                        start=st, stop=sp))(bank, WV, kc, tt, kc == 0, kc == DC - 1),
                        reads=[("X", kc, tt), wk2], writes=[bkey2], tag=f"l1p{p}_{dkey}_{tt}_{kc}")
                P.op("ACT", (lambda bb, d, ttt, bcc: lambda e: e.activation(
                    d[:, ttt * TT:(ttt + 1) * TT], bb[:], AF.Identity,
                    bias=bcc[:].bitcast(F32)))(bank, dst, tt, bcol),
                    reads=[bkey2, bkey], writes=[(dkey, tt)], tag=f"l1pe{p}_{dkey}_{tt}")

    def stage_B(p):
        pi = p % 2
        for tc in range(NTC):
            b2 = 6 + (tc % 2)
            bank, bkey2 = C.PS[b2], f"PS{b2}"
            P.op("PE", (lambda bb, tcc: lambda e: e.transpose(
                bb[:, 0:128].bitcast(F32R), VHP[pi][:, tcc * TC:(tcc + 1) * TC],
                C.ident[:]))(bank, tc),
                reads=[(f"l1a_vhp{pi}", tc // 4), "c_ident"], writes=[bkey2], tag=f"l1vt{p}_{tc}")
            P.op("DVE", (lambda bb, tcc: lambda e: e.tensor_copy(
                VTM[:, tcc * TC:(tcc + 1) * TC], bb[:, 0:128].bitcast(F32R)))(bank, tc),
                reads=[bkey2], writes=[("l1a_vtm", tc)], tag=f"l1vte{p}_{tc}")
        for tc in range(NTC):
            b2 = 2 + (tc % 2)
            bank, bkey2 = C.PS[b2], f"PS{b2}"
            P.op("PE", (lambda bb, tcc: lambda e: e.matmul(
                bb[:], KHP[pi][:, tcc * TC:(tcc + 1) * TC], WKF[:], start=True, stop=True))(bank, tc),
                reads=[(f"l1a_khp{pi}", tc // 4), "l1a_wkf"], writes=[bkey2], tag=f"l1kp{p}_{tc}")
            P.op("ACT", (lambda bb, tcc: lambda e: e.activation(
                KP[:, tcc * 512:(tcc + 1) * 512], bb[:], AF.Exp))(bank, tc),
                reads=[bkey2], writes=[("l1a_kp", tc)], tag=f"l1kpe{p}_{tc}")
            P.op("PE", (lambda tcc, st, sp: lambda e: e.matmul(
                C.PS[5][0:1, :], C.ones_col[:], KP[:, tcc * 512:(tcc + 1) * 512],
                start=st, stop=sp, skip_group_check=True))(tc, tc == 0, tc == NTC - 1),
                reads=[("l1a_kp", tc), "c_ones_col"], writes=["PS5"], tag=f"l1ks{p}_{tc}")
        P.op("DVE", (lambda pi2: lambda e: e.tensor_copy(KSR[pi2][:], C.PS[5][0:1, :]))(pi),
             reads=["PS5"], writes=[("l1a_ksr",)], tag=f"l1ksr{p}")
        for fc in range(4):
            P.op("PE", (lambda pi2, fcc: lambda e: e.matmul(
                C.PS[4][0:128, fcc:fcc + 1].bitcast(F32),
                KSR[pi2][0:1, fcc * 128:(fcc + 1) * 128].bitcast(F32),
                C.ones1[0:1, 0:1].bitcast(F32), start=True, stop=True,
                skip_group_check=True))(pi, fc),
                reads=[("l1a_ksr",), "c_ones1"], writes=["PS4"], tag=f"l1kst{p}_{fc}")
        P.op("DVE", (lambda pi2: lambda e: e.tensor_copy(KSP[pi2][:], C.PS[4][:, 0:4]))(pi),
             reads=["PS4"], writes=[(f"l1a_ksp{pi}",)], tag=f"l1kspe{p}")

    def stage_C(p):
        pi = p % 2
        P.op("POOL", (lambda pi2: lambda e: e.iota(CTXP[pi2][:], [[0, 512]], base=0,
                                                   channel_multiplier=0,
                                                   allow_small_or_imprecise_dtypes=True))(pi),
             writes=[(f"l1a_ctxp{pi}",)], tag=f"l1cz{p}")
        for gi, (hd, fc) in enumerate([(h2, f2) for h2 in range(2) for f2 in range(2)]):
            cb = 5 if gi % 2 == 0 else 3
            for tc in range(NTC):
                P.op("PE", (lambda c2, hdd, fcc, tcc, st, sp: lambda e: e.matmul(
                    C.PS[c2][:, 0:64], KP[:, tcc * 512 + hdd * 256 + fcc * 128:
                                          tcc * 512 + hdd * 256 + (fcc + 1) * 128],
                    VTM[:, tcc * TC + hdd * 64: tcc * TC + (hdd + 1) * 64],
                    start=st, stop=sp, skip_group_check=True))(cb, hd, fc, tc, tc == 0, tc == NTC - 1),
                    reads=[("l1a_kp", tc), ("l1a_vtm", tc)], writes=[f"PS{cb}"],
                    tag=f"l1cx{p}_{hd}_{fc}_{tc}")
            P.op("DVE", (lambda pi2, c2, hdd, fcc: lambda e: e.tensor_copy(
                CTXP[pi2][:, (hdd * 2 + fcc) * 128 + hdd * 64:
                          (hdd * 2 + fcc) * 128 + (hdd + 1) * 64],
                C.PS[c2][:, 0:64].bitcast(F32R)))(pi, cb, hd, fc),
                reads=[f"PS{cb}"], writes=[(f"l1a_ctxp{pi}",)], tag=f"l1cxe{p}_{hd}_{fc}")

    def stage_D(p):
        pi = p % 2
        for tt in range(NTT):
            qi = tt % 2
            for fch in range(4):
                b2 = 6 + (fch % 2)
                bank, bkey2 = C.PS[b2], f"PS{b2}"
                P.op("PE", (lambda bb, pi2, fcc, ttt: lambda e: e.matmul(
                    bb[:], WQF[:, fcc * 128:(fcc + 1) * 128],
                    QHP[pi2][:, ttt * TT:(ttt + 1) * TT], start=True, stop=True))(bank, pi, fch, tt),
                    reads=[(f"l1a_qhp{pi}", tt), "l1a_wqf"], writes=[bkey2], tag=f"l1qp{p}_{fch}_{tt}")
                P.op("ACT", (lambda bb, qi2, fcc: lambda e: e.activation(
                    QPT[qi2][:, fcc * 512:(fcc + 1) * 512], bb[:], AF.Exp))(bank, qi, fch),
                    reads=[bkey2], writes=[("l1a_qpt", qi, fch)], tag=f"l1qpe{p}_{fch}_{tt}")
            for hd in range(2):
                db = 2 + hd
                for fc in range(2):
                    fch = hd * 2 + fc
                    P.op("PE", (lambda d2, pi2, qi2, fcc, st, sp: lambda e: e.matmul(
                        C.PS[d2][0:1, :], KSP[pi2][:, fcc:fcc + 1],
                        QPT[qi2][:, fcc * 512:(fcc + 1) * 512],
                        start=st, stop=sp))(db, pi, qi, fch, fc == 0, fc == 1),
                        reads=[(f"l1a_ksp{pi}",), ("l1a_qpt", qi, fch)], writes=[f"PS{db}"],
                        tag=f"l1dn{p}_{hd}_{tt}_{fc}")
                P.op("DVE", (lambda d2, ii: lambda e: e.tensor_scalar(
                    DEN[ii][:], C.PS[d2][0:1, :], 1e-6, None, ALU.add))(db, hd),
                    reads=[f"PS{db}"], writes=[("l1a_den",)], tag=f"l1de{p}_{hd}_{tt}")
                P.op("DVE", (lambda ii: lambda e: e.reciprocal(RD1[ii][:], DEN[ii][:]))(hd),
                     preads=[("l1a_den",)], writes=[("l1a_rd1",)], tag=f"l1dr{p}_{hd}_{tt}")
                P.op("PE", (lambda b3, ii: lambda e: e.matmul(
                    C.PS[b3][:], C.ones1[:], RD1[ii][:], start=True, stop=True))(4 + hd // 2 * 0, hd),
                    reads=[("l1a_rd1",), "c_ones1"], writes=["PS4"], tag=f"l1db{p}_{hd}_{tt}")
                for fc in range(2):
                    fch = hd * 2 + fc
                    P.op("DVE", (lambda qi2, fcc: lambda e: e.tensor_tensor(
                        QPT[qi2][:, fcc * 512:(fcc + 1) * 512],
                        QPT[qi2][:, fcc * 512:(fcc + 1) * 512],
                        C.PS[4][:], ALU.mult))(qi, fch),
                        reads=[("l1a_qpt", qi, fch), "PS4"], writes=[("l1a_qpt", qi, fch)],
                        tag=f"l1ds{p}_{hd}_{tt}_{fc}")
            ob = 5
            for kc in range(4):
                P.op("PE", (lambda pi2, qi2, kcc, st, sp: lambda e: e.matmul(
                    C.PS[ob][:], CTXP[pi2][:, kcc * 128:(kcc + 1) * 128],
                    QPT[qi2][:, kcc * 512:(kcc + 1) * 512],
                    start=st, stop=sp))(pi, qi, kc, kc == 0, kc == 3),
                    reads=[(f"l1a_ctxp{pi}",), ("l1a_qpt", qi, kc)], writes=[f"PS{ob}"],
                    tag=f"l1o{p}_{tt}_{kc}")
            P.op("ACT", (lambda ttt: lambda e: e.copy(
                OUTP[:, ttt * TT:(ttt + 1) * TT], C.PS[ob][:]))(tt),
                reads=[f"PS{ob}"],
                writes=[("l1a_outp", tt)] + [("l1a_vtm", tt * 4 + j) for j in range(4)],
                tag=f"l1oe{p}_{tt}")
        P.dma("SP", (lambda pp: lambda e: e.dma_start(
            out=outf[pp * 128:(pp + 1) * 128, :], in_=OUTP[:].bitcast(F32)))(p),
            reads=([("l1a_outp", t2) for t2 in range(NTT)]
                   + [("l1a_vtm", t2) for t2 in range(NTC)]),
            writes=[("l1a_outf", p)], tag=f"st_outf{p}")

    stage_A(0)
    stage_B(0)
    for p in range(8):
        if p < 7:
            stage_A(p + 1)
        stage_C(p)
        stage_D(p)
        if p < 7:
            stage_B(p + 1)

    for kc in range(DC):
        kp_alias = [("l1a_kp", t2) for t2 in range(kc * 2, kc * 2 + 2)]
        P.dma("SP", (lambda kcc: lambda e: e.dma_start(
            out=WPJ[:, kcc * 1024:(kcc + 1) * 1024].rearrange("p (o j) -> p o j", o=8),
            in_=Wp_d[kcc * 128:(kcc + 1) * 128, :].rearrange("p (o j) -> p o j", o=8).bitcast(F32R)))(kc),
            writes=[("l1a_wpj", kc)] + kp_alias, tag=f"ld_wpj{kc}")
    pb_alias = ([(f"l1a_khp{i2}", t2) for i2 in range(2) for t2 in range(NTT)]
                + [(f"l1a_vhp{i2}", t2) for i2 in range(2) for t2 in range(NTT)]
                + [("l1a_qhp0", t2) for t2 in range(NTT)])
    for th in range(2):
        for kc in range(DC):
            P.dma("SP", (lambda kcc, thh: lambda e: e.dma_start(
                out=OFV[:, kcc * 1024:(kcc + 1) * 1024],
                in_=outf[kcc * 128:(kcc + 1) * 128, thh * 1024:(thh + 1) * 1024].bitcast(F32R)))(kc, th),
                reads=[("l1a_outf", kc)], writes=[("l1a_of", kc)] + (pb_alias if th == 0 else []),
                tag=f"ld_of{th}_{kc}")
        for och in range(DC):
            for ttl in range(2):
                tt = th * 2 + ttl
                b2 = 6 + ((och * 2 + ttl) % 2)
                bank, bkey2 = C.PS[b2], f"PS{b2}"
                for kc in range(DC):
                    P.op("PE", (lambda bb, kcc, occ, thh, ttt2, st, sp: lambda e: e.matmul(
                        bb[:], WPJ[:, kcc * 1024 + occ * 128: kcc * 1024 + (occ + 1) * 128],
                        OFV[:, kcc * 1024 + ttt2 * TT: kcc * 1024 + (ttt2 + 1) * TT],
                        start=st, stop=sp))(bank, kc, och, th, ttl, kc == 0, kc == DC - 1),
                        reads=[("l1a_wpj", kc), ("l1a_of", kc)], writes=[bkey2],
                        tag=f"l1z{th}_{och}_{ttl}_{kc}")
                P.op("DVE", (lambda occ: lambda e: e.tensor_copy(BC["p"][:], BPA[:, occ:occ + 1]))(och),
                     reads=["l1a_bpa"], writes=["l1a_bcp"], tag=f"l1zb{th}_{och}_{ttl}")
                P.op("DVE", (lambda bb, occ, ttt: lambda e: e.scalar_tensor_tensor(
                    xsl(C, occ, ttt), bb[:], BC["p"][:].bitcast(F32), xsl(C, occ, ttt),
                    ALU.add, ALU.add))(bank, och, tt),
                    reads=[bkey2, ("X", och, tt)], preads=["l1a_bcp"], writes=[("X", och, tt)],
                    tag=f"l1zr{th}_{och}_{ttl}")
    C.QHPv = QHP[1]
    C.KPv = KP
    return KHP[0], VHP[1]


def build_prog_l1_attn():
    nc = bass.Bass("TRN2", target_bir_lowering=False)
    x_d = nc.dram_tensor("x", [1024, N], F32, kind="ExternalInput")
    wq = nc.dram_tensor("wq", [1024, 1024], F32, kind="ExternalInput")
    bq = nc.dram_tensor("bq", [1024], F32, kind="ExternalInput")
    wk = nc.dram_tensor("wk", [1024, 1024], F32, kind="ExternalInput")
    bk = nc.dram_tensor("bk", [1024], F32, kind="ExternalInput")
    wv = nc.dram_tensor("wv", [1024, 1024], F32, kind="ExternalInput")
    bv = nc.dram_tensor("bv", [1024], F32, kind="ExternalInput")
    wp = nc.dram_tensor("wp", [1024, 1024], F32, kind="ExternalInput")
    bp = nc.dram_tensor("bp", [1024], F32, kind="ExternalInput")
    wqf = nc.dram_tensor("wqf", [64, 256], F32, kind="ExternalInput")
    wkf = nc.dram_tensor("wkf", [64, 256], F32, kind="ExternalInput")
    xo = nc.dram_tensor("xo", [1024, N], F32, kind="ExternalOutput")
    C = make_ctx(nc)
    C.ba = ArenaAlloc(C.al("arena", [128, ACOLS], F32R), ACOLS)
    emit_consts(C)
    emit_stream_in_fm(C, x_d)
    KHP = emit_l1_attn(C, wq.ap(), bq, wk.ap(), bk, wv.ap(), bv, wp.ap(), bp, wqf, wkf)
    KHPv, VHPv = KHP
    C.ln_bufs = {
        "sq": [KHPv[:, 0:512], KHPv[:, 512:1024]],
        "bc": [KHPv[:, 1024:1536], KHPv[:, 1536:2048]],
        "ms": VHPv[0:1, 0:1024],
        "m2": VHPv[0:1, 1024:2048],
        "sc": C.QHPv[0:1, 0:1024],
        "rs": C.QHPv[0:1, 1024:2048],
        "nm": C.KPv[0:1, 0:1024],
    }
    C.ln_alias = ([("l1a_khp", t) for t in range(NTT)] + [("l1a_vhp", t) for t in range(NTT)]
                  + [("l1a_qhp", t) for t in range(NTT)]
                  + [("l1a_kp", 0), ("l1a_kp", 1), ("l1a_of", 0)])
    emit_postln(C, "1a")
    emit_stream_out_fm(C, xo)
    with ExitStack() as stack:
        C.P.build(stack)
    return nc, C


def emit_l1_ffn(C, Wr_d, Wk_d, Wv_d, mur_d, muk_d):
    P, al, nc = C.P, C.ba, C.nc

    RING = al("f1_ring", [128, 8 * 1024], F32R)
    TBUF = [al(f"f1_tb{i}", [128, 1024], F32R) for i in range(2)]
    XK = al("f1_xk", [128, DC * 512], F32R)
    XR = al("f1_xr", [128, DC * 512], F32R)
    KACT = al("f1_kact", [128, DC * 512], F32R)
    XPREV = al("f1_xprev", [128, 512], F32R)
    XLAST = al("f1_xlast", [128, 8], F32R)
    RT = [al(f"f1_rt{i}", [128, 512], F32) for i in range(2)]
    TMP = [al(f"f1_tmp{i}", [128, 512], F32) for i in range(2)]
    MUA = {}
    for nm in ("r", "k", "r1", "k1"):
        MUA[nm] = al(f"f1_mu{nm}", [128, 8], F32R)
    MC = {nm: al(f"f1_mc{nm}", [128, 1], F32R) for nm in ("r", "k")}
    BST = al("f1_bst", [8, 128], F32R)
    wt_dram = {nm: nc.dram_tensor(f"f1_wt_{nm}", [1024, 1024], F32, kind="Internal")
               for nm in ("r", "k", "v")}

    for nm, src in (("r", mur_d), ("k", muk_d)):
        P.dma("SP", (lambda sa: lambda e: e.dma_start(
            out=BST[:], in_=sa.ap().rearrange("(c p) -> c p", p=128).bitcast(F32R)))(src),
            writes=["f1_bst"], tag=f"mu{nm}_ld")
        P.op("PE", lambda e: e.transpose(C.PS[6][:, 0:8].bitcast(F32R), BST[:], C.ident[0:8, 0:8]),
             reads=["f1_bst", "c_ident"], writes=["PS6"], tag=f"mu{nm}_tr")
        P.op("DVE", (lambda n2: lambda e: e.tensor_copy(MUA[n2][:], C.PS[6][:, 0:8].bitcast(F32R)))(nm),
             reads=["PS6"], writes=[f"f1_mu{nm}"], tag=f"mu{nm}_ev")
        P.op("DVE", (lambda n2: lambda e: e.tensor_scalar(
            MUA[n2 + "1"][:], MUA[n2][:], 1.0, -1.0, ALU.subtract, ALU.mult))(nm),
            reads=[f"f1_mu{nm}"], writes=[f"f1_mu{nm}1"], tag=f"mu{nm}_1m")

    ldq = [0]
    for nm, src in (("k", Wk_d), ("r", Wr_d), ("v", Wv_d)):
        for r in range(8):
            sl = r % 4
            tb = r % 2
            P.dma("SP" if ldq[0] % 2 == 0 else "POOL", (lambda sa, rr, sll: lambda e: e.dma_start(
                out=RING[:, sll * 1024:(sll + 1) * 1024],
                in_=sa[rr * 128:(rr + 1) * 128, :].bitcast(F32R)))(src, r, sl),
                writes=[("f1_ring", sl)], tag=f"wt{nm}_ld{r}")
            ldq[0] += 1
            for j in range(8):
                b = 6 + (j % 2)
                bank, bkey = C.PS[b], f"PS{b}"
                P.op("PE", (lambda sll, jj, bb: lambda e: e.transpose(
                    bb[:, 0:128].bitcast(F32R),
                    RING[:, sll * 1024 + jj * 128: sll * 1024 + (jj + 1) * 128],
                    C.ident[:]))(sl, j, bank),
                    reads=[("f1_ring", sl), "c_ident"], writes=[bkey], tag=f"wt{nm}_tr{r}_{j}")
                P.op("ACT", (lambda tbb, jj, bb: lambda e: e.copy(
                    TBUF[tbb][:, jj * 128:(jj + 1) * 128], bb[:, 0:128].bitcast(F32R)))(tb, j, bank),
                    reads=[bkey], writes=[("f1_tb", tb)], tag=f"wt{nm}_ev{r}_{j}")
            P.dma("ACT", (lambda n2, rr, tbb: lambda e: e.dma_start(
                out=wt_dram[n2].ap().rearrange("(j p) d -> p j d", p=128)[:, :, rr * 128:(rr + 1) * 128],
                in_=TBUF[tbb][:].rearrange("p (j d) -> p j d", j=8).bitcast(F32)))(nm, r, tb),
                reads=[("f1_tb", tb)], writes=[("f1_wt", nm, r)], tag=f"wt{nm}_st{r}")

    rp = [0]

    def wload(nm, och, tag):
        i = rp[0] % 8
        rp[0] += 1
        view = RING[:, i * 1024:(i + 1) * 1024]
        P.dma("SP", (lambda n2, occ, v: lambda e: e.dma_start(
            out=v.rearrange("p (k j) -> p k j", k=8),
            in_=wt_dram[n2].ap()[:, occ * 128:(occ + 1) * 128]
                .rearrange("(k p) j -> p k j", p=128).bitcast(F32R)))(nm, och, view),
            reads=[("f1_wt", nm, r2) for r2 in range(8)],
            writes=[("f1_ring", i)], tag="wl_" + tag)
        return view, ("f1_ring", i)

    for tt in range(NTT):
        for kc in range(DC):
            if tt == 0:
                P.op("POOL", lambda e: e.iota(XPREV[:, 0:1], [[0, 1]], base=0, channel_multiplier=0,
                                              allow_small_or_imprecise_dtypes=True),
                     writes=["f1_xprev"], tag=f"f1z{tt}_{kc}")
                P.op("DVE", (lambda kcc: lambda e: e.tensor_copy(
                    XPREV[:, 1:512], C.X[:, kcc * N: kcc * N + 511]))(kc),
                    reads=[("X", kc, 0)], writes=["f1_xprev"], tag=f"f1xp{tt}_{kc}")
            else:
                P.op("DVE", (lambda kcc, ttt: lambda e: e.tensor_copy(
                    XPREV[:, 0:1], XLAST[:, kcc:kcc + 1]))(kc, tt),
                    reads=[("f1_xlast", kc)], writes=["f1_xprev"], tag=f"f1xp0{tt}_{kc}")
                P.op("DVE", (lambda kcc, ttt: lambda e: e.tensor_copy(
                    XPREV[:, 1:512], C.X[:, kcc * N + ttt * TT: kcc * N + ttt * TT + 511]))(kc, tt),
                    reads=[("X", kc, tt)], writes=["f1_xprev"], tag=f"f1xp{tt}_{kc}")
            P.op("DVE", (lambda kcc, ttt: lambda e: e.tensor_tensor(
                TMP[0][:], xsl(C, kcc, ttt), XPREV[:], ALU.subtract))(kc, tt),
                reads=[("X", kc, tt), "f1_xprev"], writes=[("f1_tmp", 0)], tag=f"f1d1{tt}_{kc}")
            for n2 in ("r", "k"):
                P.op("DVE", (lambda n3, kcc: lambda e: e.tensor_copy(
                    MC[n3][:], MUA[n3][:, kcc:kcc + 1]))(n2, kc),
                    reads=[f"f1_mu{n2}"], writes=[f"f1_mc{n2}"], tag=f"f1mc{n2}{tt}_{kc}")
                dst = XR if n2 == "r" else XK
                dkey = "f1_xr" if n2 == "r" else "f1_xk"
                P.op("DVE", (lambda d, n3, kcc: lambda e: e.scalar_tensor_tensor(
                    d[:, kcc * 512:(kcc + 1) * 512], TMP[0][:], MC[n3][:].bitcast(F32), XPREV[:],
                    ALU.mult, ALU.add))(dst, n2, kc),
                    reads=[("f1_tmp", 0), "f1_xprev"], preads=[f"f1_mc{n2}"],
                    writes=[(dkey, kc)], tag=f"f1mx{n2}{tt}_{kc}")
        for kc in range(DC):
            P.op("DVE", (lambda kcc, ttt: lambda e: e.tensor_copy(
                XLAST[:, kcc:kcc + 1], C.X[:, kcc * N + ttt * TT + TT - 1: kcc * N + ttt * TT + TT]))(kc, tt),
                reads=[("X", kc, tt), ("f1_xlast", kc)], writes=[("f1_xlast", kc)],
                tag=f"f1xl{tt}_{kc}")
        for och in range(DC):
            WV, wk2 = wload("k", och, f"k{tt}_{och}")
            b = och % 2
            bank, bkey = C.PS[b], f"PS{b}"
            for kc in range(DC):
                P.op("PE", (lambda bb, v, kcc, st, sp: lambda e: e.matmul(
                    bb[:], v[:, kcc * 128:(kcc + 1) * 128], XK[:, kcc * 512:(kcc + 1) * 512],
                    start=st, stop=sp))(bank, WV, kc, kc == 0, kc == DC - 1),
                    reads=[("f1_xk", kc), wk2], writes=[bkey], tag=f"f1k{tt}_{och}_{kc}")
            i = och % 2
            P.op("DVE", (lambda bb, ii: lambda e: e.tensor_scalar_max(TMP[ii][:], bb[:], 0.0))(bank, i),
                 reads=[bkey], writes=[("f1_tmp", i)], tag=f"f1kr{tt}_{och}")
            P.op("ACT", (lambda ii, occ: lambda e: e.activation(
                KACT[:, occ * 512:(occ + 1) * 512], TMP[ii][:], AF.Square))(i, och),
                reads=[("f1_tmp", i)], writes=[("f1_kact", och)], tag=f"f1ks{tt}_{och}")
        for och in range(DC):
            WVr, wrk = wload("r", och, f"r{tt}_{och}")
            WVv, wvk = wload("v", och, f"v{tt}_{och}")
            b = och % 2
            bank, bkey = C.PS[b], f"PS{b}"
            for kc in range(DC):
                P.op("PE", (lambda bb, v, kcc, st, sp: lambda e: e.matmul(
                    bb[:], v[:, kcc * 128:(kcc + 1) * 128], XR[:, kcc * 512:(kcc + 1) * 512],
                    start=st, stop=sp))(bank, WVr, kc, kc == 0, kc == DC - 1),
                    reads=[("f1_xr", kc), wrk], writes=[bkey], tag=f"f1r{tt}_{och}_{kc}")
            i = och % 2
            P.op("ACT", (lambda bb, ii: lambda e: e.activation(RT[ii][:], bb[:], AF.Sigmoid))(bank, i),
                 reads=[bkey], writes=[("f1_rt", i)], tag=f"f1rs{tt}_{och}")
            b2 = 2 + (och % 2)
            bank2, bkey2 = C.PS[b2], f"PS{b2}"
            for kc in range(DC):
                P.op("PE", (lambda bb, v, kcc, st, sp: lambda e: e.matmul(
                    bb[:], v[:, kcc * 128:(kcc + 1) * 128], KACT[:, kcc * 512:(kcc + 1) * 512],
                    start=st, stop=sp))(bank2, WVv, kc, kc == 0, kc == DC - 1),
                    reads=[("f1_kact", kc), wvk], writes=[bkey2], tag=f"f1v{tt}_{och}_{kc}")
            P.op("DVE", (lambda bb, ii: lambda e: e.tensor_tensor(
                TMP[ii][:], RT[ii][:], bb[:], ALU.mult))(bank2, i),
                reads=[("f1_rt", i), bkey2], writes=[("f1_tmp", i)], tag=f"f1rv{tt}_{och}")
            P.op("POOL", (lambda ii, occ, ttt: lambda e: e.tensor_tensor(
                xsl(C, occ, ttt), xsl(C, occ, ttt), TMP[ii][:], ALU.add))(i, och, tt),
                reads=[("X", och, tt), ("f1_tmp", i)], writes=[("X", och, tt)],
                tag=f"f1xa{tt}_{och}")
    return XK, XR, KACT


def build_prog_l1_ffn():
    nc = bass.Bass("TRN2", target_bir_lowering=False)
    x_d = nc.dram_tensor("x", [1024, N], F32, kind="ExternalInput")
    wr = nc.dram_tensor("wr", [1024, 1024], F32, kind="ExternalInput")
    wk = nc.dram_tensor("wk", [1024, 1024], F32, kind="ExternalInput")
    wv = nc.dram_tensor("wv", [1024, 1024], F32, kind="ExternalInput")
    mur = nc.dram_tensor("mur", [1024], F32, kind="ExternalInput")
    muk = nc.dram_tensor("muk", [1024], F32, kind="ExternalInput")
    yo = nc.dram_tensor("yo", [N, 1024], F32, kind="ExternalOutput")
    C = make_ctx(nc)
    C.ba = ArenaAlloc(C.al("arena", [128, ACOLS], F32R), ACOLS)
    emit_consts(C)
    emit_stream_in_fm(C, x_d)
    XK, XR, KACT = emit_l1_ffn(C, wr.ap().bitcast(F32R), wk.ap().bitcast(F32R),
                               wv.ap().bitcast(F32R), mur, muk)
    C.ln_bufs = {
        "sq": [XK[:, 0:512], XK[:, 512:1024]],
        "bc": [XK[:, 1024:1536], XK[:, 1536:2048]],
        "ms": XK[0:1, 2048:3072],
        "m2": XK[0:1, 3072:4096],
        "sc": XR[0:1, 0:1024],
        "rs": XR[0:1, 1024:2048],
        "nm": KACT[0:1, 0:1024],
    }
    C.ln_alias = ([("f1_xk", k) for k in range(DC)] + [("f1_xr", k) for k in range(DC)]
                  + [("f1_kact", k) for k in range(DC)])
    emit_postln(C, "1f")
    emit_stream_out_tm(C, yo.ap())
    with ExitStack() as stack:
        C.P.build(stack)
    return nc, C




def build_prog_fused():
    nc = bass.Bass("TRN2", target_bir_lowering=False)
    I = {}
    I["x"] = nc.dram_tensor("x", [N, 1024], F32, kind="ExternalInput")
    for nm, shp in [("l0a_wq", [256, 1024]), ("l0a_wk", [256, 1024]), ("l0a_wv", [256, 1024]),
                    ("l0a_wo", [1024, 256]), ("l0f_rw", [1024, 8]), ("l0f_wg", [8, 1024, 512]),
                    ("l0f_wu", [8, 1024, 512]), ("l0f_wd", [8, 512, 1024]),
                    ("l0f_wsg", [1024, 1024]), ("l0f_wsu", [1024, 1024]), ("l0f_wsd", [1024, 1024]),
                    ("l1a_wq", [1024, 1024]), ("l1a_bq", [1024]), ("l1a_wk", [1024, 1024]),
                    ("l1a_bk", [1024]), ("l1a_wv", [1024, 1024]), ("l1a_bv", [1024]),
                    ("l1a_wp", [1024, 1024]), ("l1a_bp", [1024]), ("l1a_wqf", [64, 256]),
                    ("l1a_wkf", [64, 256]), ("l1f_wr", [1024, 1024]), ("l1f_wk", [1024, 1024]),
                    ("l1f_wv", [1024, 1024]), ("l1f_mur", [1024]), ("l1f_muk", [1024])]:
        I[nm] = nc.dram_tensor(nm, shp, F32, kind="ExternalInput")
    yo = nc.dram_tensor("yo", [N, 1024], F32, kind="ExternalOutput")

    C = make_ctx(nc)
    ARENA = C.al("arena", [128, ACOLS], F32R)
    C.ba = ArenaAlloc(ARENA, ACOLS)
    emit_consts(C)
    emit_stream_in_tm(C, I["x"])

    C.ba.reset()
    arena, VT = emit_l0_attn(C, I["l0a_wq"], I["l0a_wk"], I["l0a_wv"], I["l0a_wo"])
    C.ln_bufs = {
        "sq": [arena[:, 0:512], arena[:, 512:1024]],
        "bc": [arena[:, 1024:1536], arena[:, 1536:2048]],
        "ms": arena[0:1, 2048:3072],
        "m2": arena[0:1, 3072:4096],
        "sc": VT[0:1, 0:1024],
        "rs": VT[0:1, 1024:2048],
        "nm": VT[0:1, 2048:3072],
    }
    C.ln_alias = ([("l0a_wot", h) for h in range(4)] +
                  [("l0a_wqk",), ("l0a_sraw", 0), ("l0a_sraw", 1)] +
                  [("l0a_vt", t) for t in range(12)])
    emit_postln(C, "0a")
    C.P.barrier()

    C.ba.reset()
    RING = emit_l0_ffn(C, I["l0f_rw"], I["l0f_wg"].ap(), I["l0f_wu"].ap(), I["l0f_wd"].ap(),
                       I["l0f_wsg"].ap(), I["l0f_wsu"].ap(), I["l0f_wsd"].ap())
    C.ln_bufs = {
        "sq": [RING[:, 0:512], RING[:, 512:1024]],
        "bc": [RING[:, 1024:1536], RING[:, 1536:2048]],
        "ms": RING[0:1, 2048:3072],
        "m2": RING[0:1, 3072:4096],
        "sc": RING[0:1, 4096:5120],
        "rs": RING[0:1, 5120:6144],
        "nm": RING[0:1, 6144:7168],
    }
    C.ln_alias = [("ring", 0), ("ring", 1)]
    emit_postln(C, "0f")
    C.P.barrier()

    C.ba.reset()
    KHPv, VHPv = emit_l1_attn(C, I["l1a_wq"].ap(), I["l1a_bq"], I["l1a_wk"].ap(), I["l1a_bk"],
                              I["l1a_wv"].ap(), I["l1a_bv"], I["l1a_wp"].ap(), I["l1a_bp"],
                              I["l1a_wqf"], I["l1a_wkf"])
    C.ln_bufs = {
        "sq": [KHPv[:, 0:512], KHPv[:, 512:1024]],
        "bc": [KHPv[:, 1024:1536], KHPv[:, 1536:2048]],
        "ms": VHPv[0:1, 0:1024],
        "m2": VHPv[0:1, 1024:2048],
        "sc": C.QHPv[0:1, 0:1024],
        "rs": C.QHPv[0:1, 1024:2048],
        "nm": C.KPv[0:1, 0:1024],
    }
    C.ln_alias = ([("l1a_khp", t) for t in range(NTT)] + [("l1a_vhp", t) for t in range(NTT)]
                  + [("l1a_qhp", t) for t in range(NTT)]
                  + [("l1a_kp", 0), ("l1a_kp", 1), ("l1a_of", 0)])
    emit_postln(C, "1a")
    C.P.barrier()

    C.ba.reset()
    XK, XR, KACT = emit_l1_ffn(C, I["l1f_wr"].ap().bitcast(F32R), I["l1f_wk"].ap().bitcast(F32R),
                               I["l1f_wv"].ap().bitcast(F32R), I["l1f_mur"], I["l1f_muk"])
    C.ln_bufs = {
        "sq": [XK[:, 0:512], XK[:, 512:1024]],
        "bc": [XK[:, 1024:1536], XK[:, 1536:2048]],
        "ms": XK[0:1, 2048:3072],
        "m2": XK[0:1, 3072:4096],
        "sc": XR[0:1, 0:1024],
        "rs": XR[0:1, 1024:2048],
        "nm": KACT[0:1, 0:1024],
    }
    C.ln_alias = ([("f1_xk", k) for k in range(DC)] + [("f1_xr", k) for k in range(DC)]
                  + [("f1_kact", k) for k in range(DC)])
    emit_postln(C, "1f")
    emit_stream_out_tm(C, yo.ap())
    with ExitStack() as stack:
        C.P.build(stack)
    return nc, C


def kernel(**inputs):
    I = inputs
    x0 = np.ascontiguousarray(np.asarray(I["L0_attn_x"], dtype=np.float32))
    nc, _ = _get_prog("fused", build_prog_fused)
    im = [dict(x=np.ascontiguousarray(x0[b]),
               l0a_wq=I["L0_attn_Wq"], l0a_wk=I["L0_attn_Wk"], l0a_wv=I["L0_attn_Wv"],
               l0a_wo=I["L0_attn_Wo"], l0f_rw=I["L0_ffn_router_w"], l0f_wg=I["L0_ffn_w_gate"],
               l0f_wu=I["L0_ffn_w_up"], l0f_wd=I["L0_ffn_w_down"], l0f_wsg=I["L0_ffn_ws_gate"],
               l0f_wsu=I["L0_ffn_ws_up"], l0f_wsd=I["L0_ffn_ws_down"],
               l1a_wq=I["L1_attn_Wq"], l1a_bq=I["L1_attn_bq"], l1a_wk=I["L1_attn_Wk"],
               l1a_bk=I["L1_attn_bk"], l1a_wv=I["L1_attn_Wv"], l1a_bv=I["L1_attn_bv"],
               l1a_wp=I["L1_attn_Wproj"], l1a_bp=I["L1_attn_bproj"],
               l1a_wqf=I["L1_attn_wq_feat"], l1a_wkf=I["L1_attn_wk_feat"],
               l1f_wr=I["L1_ffn_W_r"], l1f_wk=I["L1_ffn_W_k"], l1f_wv=I["L1_ffn_W_v"],
               l1f_mur=I["L1_ffn_mu_r"], l1f_muk=I["L1_ffn_mu_k"]) for b in range(8)]
    r = run_bass_kernel_spmd(nc, im, core_ids=list(range(8)))
    return np.stack([np.asarray(r.results[b]["yo"]) for b in range(8)], axis=0)


# version 17 (final)
# speedup vs baseline: 1.0425x; 1.0425x over previous
import numpy as np
from contextlib import ExitStack

import concourse.bass as bass
import concourse.mybir as mybir
from concourse.bass_utils import run_bass_kernel_spmd

F32 = mybir.dt.float32
F32R = mybir.dt.float32r
BF16 = mybir.dt.bfloat16
AF = mybir.ActivationFunctionType
ALU = mybir.AluOpType
AX = mybir.AxisListType

ENGINES = ("PE", "ACT", "DVE", "POOL", "SP")
QUEUE_OF = {"SP": "QSP", "ACT": "QACT", "POOL": "QPOOL", "DVE": "QDVE", "PE": "QPE"}
ALL_STREAMS = ENGINES + tuple(QUEUE_OF.values())


class _Op:
    __slots__ = ("idx", "stream", "issuer", "fn", "reads", "writes", "inc_amt",
                 "needs_inc", "inc_val", "deps", "waits", "tag", "wait_queues",
                 "fence_out")


class Prog:

    def __init__(self, nc):
        self.nc = nc
        self.ops = []
        self._writer = {}
        self._readers = {}

    def op(self, engine, fn, reads=(), writes=(), tag="", preads=(), dep_ops=(),
           fence_out=False):
        assert engine in ENGINES, engine
        return self._add(engine, engine, fn, reads, writes, 1, False, tag, preads,
                         dep_ops, fence_out)

    def dma(self, issuer, fn, reads=(), writes=(), tag=""):
        assert issuer in ENGINES, issuer
        return self._add(QUEUE_OF[issuer], issuer, fn, reads, writes, 16, True, tag, (), (), False)

    def _add(self, stream, issuer, fn, reads, writes, inc_amt, force_inc, tag, preads=(),
             dep_ops=(), fence_out=False):
        o = _Op()
        o.idx = len(self.ops)
        o.stream = stream
        o.issuer = issuer
        o.fn = fn
        o.reads = list(reads)
        o.writes = list(writes)
        o.inc_amt = inc_amt
        o.needs_inc = force_inc
        o.inc_val = None
        o.tag = tag
        o.wait_queues = False
        o.fence_out = fence_out
        deps = {}

        def add_pdep(d):
            if d is None:
                return
            k = d.stream
            if k not in deps or deps[k].idx < d.idx:
                deps[k] = d

        def add_dep(d):
            if d is None:
                return
            if d.stream == stream == "PE":
                return
            if d.stream == stream and stream.startswith("Q"):
                return
            k = d.stream
            if k not in deps or deps[k].idx < d.idx:
                deps[k] = d

        for b in preads:
            add_pdep(self._writer.get(b))
            o.reads.append(b)
        for d0 in dep_ops:
            add_pdep(d0)
        for b in o.reads:
            add_dep(self._writer.get(b))
        for b in o.writes:
            add_dep(self._writer.get(b))
            for r in self._readers.get(b, ()):
                add_dep(r)
        o.deps = [d.idx for d in deps.values()]
        for b in o.reads:
            self._readers.setdefault(b, []).append(o)
        for b in o.writes:
            self._writer[b] = o
            self._readers[b] = []
        self.ops.append(o)
        return o

    def finalize(self):
        for o in self.ops:
            for di in o.deps:
                self.ops[di].needs_inc = True
        counts = {s: 0 for s in ALL_STREAMS}
        qprefix = []
        qtot = {s: 0 for s in ALL_STREAMS if s.startswith("Q")}
        for o in self.ops:
            qprefix.append(dict(qtot))
            if o.needs_inc:
                counts[o.stream] += o.inc_amt
                o.inc_val = counts[o.stream]
                if o.stream.startswith("Q"):
                    qtot[o.stream] = o.inc_val
        known = {e: {} for e in ENGINES}
        for o in self.ops:
            w = {}
            for di in o.deps:
                d = self.ops[di]
                assert d.inc_val is not None
                v = qprefix[o.idx][d.stream] if d.stream.startswith("Q") else d.inc_val
                if w.get(d.stream, 0) < v:
                    w[d.stream] = v
            if o.wait_queues:
                for qs, qv in qprefix[o.idx].items():
                    if qv > 0 and w.get(qs, 0) < qv:
                        w[qs] = qv
            ks = known[o.issuer]
            o.waits = []
            for s in sorted(w):
                v = w[s]
                if ks.get(s, 0) < v:
                    o.waits.append((s, v))
                    ks[s] = v
        self.final_counts = counts

    def n_waits(self):
        return sum(len(o.waits) for o in self.ops)

    _barn = 0

    def barrier(self):
        n = Prog._barn
        Prog._barn += 1
        last = {}
        for o in self.ops:
            if o.stream in ENGINES:
                last[o.stream] = o
        for E in ENGINES:
            self.op(E, lambda e: e.nop(), writes=[("bar1", n, E)], tag=f"bar1_{E}_{n}",
                    dep_ops=[last[E]] if E in last else [])
        for E in ENGINES:
            o = self.op(E, lambda e: e.nop(),
                        reads=[("bar1", n, E2) for E2 in ENGINES],
                        writes=[("bar2", n, E)], tag=f"bar2_{E}_{n}")
            o.wait_queues = True

    def build(self, stack):
        self.finalize()
        nc = self.nc
        sems = {}
        for s in ALL_STREAMS:
            sems[s] = stack.enter_context(nc.semaphore("sem_" + s))
        per_eng = {e: [] for e in ENGINES}
        for o in self.ops:
            per_eng[o.issuer].append(o)

        block = stack.enter_context(nc.Block())

        def mk(eng_name):
            ops = per_eng[eng_name]

            def body(eng):
                with nc.allow_low_precision("f32r operands/stream by design"):
                    for o in ops:
                        for (s, v) in o.waits:
                            eng.wait_ge(sems[s], v)
                        inst = o.fn(eng)
                        if o.needs_inc:
                            inst.then_inc(sems[o.stream], o.inc_amt)
                if eng_name == "SP":
                    for s in ALL_STREAMS:
                        if s != "SP" and self.final_counts[s] > 0:
                            eng.wait_ge(sems[s], self.final_counts[s])
            return body

        block.tensor(mk("PE"))
        block.scalar(mk("ACT"))
        block.vector(mk("DVE"))
        block.gpsimd(mk("POOL"))
        block.sync(mk("SP"))


N = 2048
DC = 8
TT = 512
NTT = N // TT
TC = 128
NTC = N // TC

_DT_SIZE = {F32: 4, F32R: 4, BF16: 2}
ACOLS = 33536


class Alloc:
    def __init__(self, nc):
        self.nc = nc
        self.per_part = 0

    FORCE_F32R = True

    def __call__(self, name, shape, dtype):
        if self.FORCE_F32R and dtype == F32:
            dtype = F32R
        t = self.nc.alloc_sbuf_tensor(name, shape, dtype)
        self.per_part += shape[-1] * _DT_SIZE[dtype]
        return t


class ArenaAlloc:

    def __init__(self, arena_handle, total_cols):
        self.h = arena_handle
        self.total = total_cols
        self.off = 0
        self.hw = 0

    def reset(self):
        self.hw = max(self.hw, self.off)
        self.off = 0

    def __call__(self, name, shape, dtype):
        req = (shape[-1] * _DT_SIZE[dtype] + 3) // 4
        cols4 = (req + 15) // 16 * 16
        assert self.off + cols4 <= self.total, (name, self.off, cols4, self.total)
        v = self.h[0:shape[0], self.off:self.off + req]
        self.off += cols4
        return v


class C_:
    pass


def make_ctx(nc):
    C = C_()
    C.nc = nc
    C.P = Prog(nc)
    C.al = Alloc(nc)
    C.X = C.al("X", [128, DC * N], F32R)
    C.PS = [nc.alloc_psum_tensor(f"PS{i}", [128, 512], F32) for i in range(8)]
    C.GSCR = C.al("gscr", [128, 2048], F32R)
    C.ba = C.al
    return C


def xkeys(c, tt=None):
    if tt is None:
        return [("X", c, t) for t in range(NTT)]
    return [("X", c, tt)]


def xall():
    return [("X", c, t) for c in range(DC) for t in range(NTT)]


def xsl(C, c, tt=None, lo=0, hi=128):
    if tt is None:
        return C.X[lo:hi, c * N:(c + 1) * N]
    return C.X[lo:hi, c * N + tt * TT: c * N + (tt + 1) * TT]


def emit_consts(C):
    P, al = C.P, C.al
    C.ri = al("c_ri", [128, 128], F32)
    C.ci = al("c_ci", [128, 128], F32)
    C.ident = al("c_ident", [128, 128], F32R)
    C.mask = al("c_mask", [128, 128], F32R)
    C.ones_col = al("c_ones_col", [128, 1], F32R)
    C.ones1 = al("c_ones1", [1, 128], F32R)
    P.op("POOL", lambda e: e.iota(C.ci[:], [[1, 128]], base=0, channel_multiplier=0,
                                  allow_small_or_imprecise_dtypes=True), writes=["c_ci"])
    P.op("POOL", lambda e: e.iota(C.ri[:], [[0, 128]], base=0, channel_multiplier=1,
                                  allow_small_or_imprecise_dtypes=True), writes=["c_ri"])
    P.op("DVE", lambda e: e.tensor_tensor(C.ident[:], C.ri[:], C.ci[:], ALU.is_equal),
         reads=["c_ri", "c_ci"], writes=["c_ident"])
    P.op("DVE", lambda e: e.tensor_tensor(C.mask[:], C.ci[:], C.ri[:], ALU.is_ge),
         reads=["c_ri", "c_ci"], writes=["c_mask"])
    P.op("DVE", lambda e: e.tensor_scalar(C.ones_col[:], C.ri[:, 0:1], -1.0, None, ALU.is_ge),
         reads=["c_ri"], writes=["c_ones_col"])
    P.op("DVE", lambda e: e.tensor_scalar(C.ones1[:], C.ri[0:1, :], -1.0, None, ALU.is_ge),
         reads=["c_ri"], writes=["c_ones1"])
    C.cb6 = C.nc.alloc_sbuf_tensor("c_eps6", [128, 1], F32)
    C.cb12 = C.nc.alloc_sbuf_tensor("c_eps12", [128, 1], F32)
    C.nc.const_aps.register(F32, 1e-6, C.cb6[:])
    C.nc.const_aps.register(F32, 1e-12, C.cb12[:])
    P.op("DVE", lambda e: e.tensor_scalar(C.cb6[:], C.ri[:, 0:1].bitcast(F32), -1.0, None, ALU.is_ge),
         reads=["c_ri"], writes=["c_eps6"])
    P.op("DVE", lambda e: e.tensor_scalar(C.cb6[:], C.cb6[:], 1e-6, None, ALU.mult),
         reads=["c_eps6"], writes=["c_eps6"])
    P.op("DVE", lambda e: e.tensor_scalar(C.cb12[:], C.cb6[:], 1e-6, None, ALU.mult),
         reads=["c_eps6"], writes=["c_eps12"])


def emit_stream_in_fm(C, x_d):
    C.P.dma("SP", lambda e: e.dma_start(
        out=C.X[:].rearrange("p (c t) -> p c t", c=DC),
        in_=x_d.ap().rearrange("(c p) t -> p c t", p=128).bitcast(F32R)),
        writes=xall(), tag="ld_x_fm")


def emit_stream_out_fm(C, y_d):
    C.P.dma("SP", lambda e: e.dma_start(
        out=y_d.ap().rearrange("(c p) t -> p c t", p=128),
        in_=C.X[:].rearrange("p (c t) -> p c t", c=DC).bitcast(F32)),
        reads=xall(), tag="st_x_fm")


def emit_stream_in_tm(C, x_d):
    P, al, nc = C.P, C.al, C.nc
    stg = [C.GSCR[:, 0:1024], C.GSCR[:, 1024:2048]]
    for tc in range(NTC):
        s = tc % 2
        P.dma("SP", (lambda tcc, ss: lambda e: e.dma_start(
            out=stg[ss], in_=x_d[tcc * TC:(tcc + 1) * TC, :].bitcast(F32R)))(tc, s),
            writes=[("gscr", s)], tag=f"ld_x_tm{tc}")
        for c in range(DC):
            bank = C.PS[6 + ((tc * DC + c) % 2)]
            bkey = f"PS{6 + ((tc * DC + c) % 2)}"
            P.op("PE", (lambda ss, cc, b: lambda e: e.transpose(
                b[:, 0:128].bitcast(F32R), stg[ss][:, cc * 128:(cc + 1) * 128], C.ident[:]))(s, c, bank),
                reads=[("gscr", s), "c_ident"], writes=[bkey], tag=f"tr_x{tc}_{c}")
            P.op("ACT", (lambda tcc, cc, b: lambda e: e.copy(
                C.X[:, cc * N + tcc * TC: cc * N + (tcc + 1) * TC], b[:, 0:128].bitcast(F32R)))(tc, c, bank),
                reads=[bkey], writes=[("X", c, tc // 4)], tag=f"ev_x{tc}_{c}")


def emit_stream_out_tm(C, y_d):
    P, al = C.P, C.al
    stg = [C.GSCR[:, 0:1024].bitcast(F32), C.GSCR[:, 1024:2048].bitcast(F32)]
    for tc in range(NTC):
        s = tc % 2
        for c in range(DC):
            bank = C.PS[6 + ((tc * DC + c) % 2)]
            bkey = f"PS{6 + ((tc * DC + c) % 2)}"
            P.op("PE", (lambda tcc, cc, b: lambda e: e.transpose(
                b[:, 0:128].bitcast(F32R),
                C.X[:, cc * N + tcc * TC: cc * N + (tcc + 1) * TC], C.ident[:]))(tc, c, bank),
                reads=[("X", c, tc // 4), "c_ident"], writes=[bkey], tag=f"trO{tc}_{c}")
            P.op("ACT", (lambda ss, cc, b: lambda e: e.copy(
                stg[ss][:, cc * 128:(cc + 1) * 128].bitcast(F32R), b[:, 0:128].bitcast(F32R)))(s, c, bank),
                reads=[bkey], writes=[("gscr", s, c)], tag=f"evO{tc}_{c}")
        P.dma("SP", (lambda tcc, ss: lambda e: e.dma_start(
            out=y_d[tcc * TC:(tcc + 1) * TC, :], in_=stg[ss]))(tc, s),
            reads=[("gscr", s, c2) for c2 in range(DC)] + [("gscr", s)], tag=f"st_y{tc}")


def emit_postln(C, tag):
    P, al, nc = C.P, C.al, C.nc
    pref = "ln" + tag
    B = C.ln_bufs
    AK = getattr(C, "ln_alias", [])

    def row(nm2, tt):
        return B[nm2][0:1, (tt % 2) * 512:(tt % 2) * 512 + 512]

    def stats(tt):
        par = tt % 2
        s1b, s2b = (0, 1) if par == 0 else (4, 5)
        for c in range(DC):
            i = (tt * DC + c) % 2
            P.op("ACT", (lambda cc, ttt, ii: lambda e: e.activation(
                B["sq"][ii], xsl(C, cc, ttt), AF.Square))(c, tt, i),
                reads=[("X", c, tt)], writes=[("lnsq", i)] + AK, tag=f"{pref}_sq{tt}_{c}")
            P.op("PE", (lambda b1, cc, ttt, st: lambda e: e.matmul(
                C.PS[b1][0:1, :], C.ones_col[:], xsl(C, cc, ttt),
                start=st, stop=(cc == DC - 1), skip_group_check=True))(s1b, c, tt, c == 0),
                reads=[("X", c, tt), "c_ones_col"], writes=[f"PS{s1b}"], tag=f"{pref}_s1_{tt}_{c}")
            P.op("PE", (lambda b2, cc, ii, st: lambda e: e.matmul(
                C.PS[b2][0:1, :], C.ones_col[:], B["sq"][ii],
                start=st, stop=(cc == DC - 1), skip_group_check=True))(s2b, c, i, c == 0),
                reads=[("lnsq", i), "c_ones_col"], writes=[f"PS{s2b}"], tag=f"{pref}_s2_{tt}_{c}")
        P.op("DVE", (lambda b1, ttt: lambda e: e.tensor_scalar(
            row("ms", ttt), C.PS[b1][0:1, :], 1.0 / 1024, None, ALU.mult))(s1b, tt),
            reads=[f"PS{s1b}"], writes=[("lnms", par)] + AK, tag=f"{pref}_ms{tt}")
        P.op("DVE", (lambda b2, ttt: lambda e: e.tensor_scalar(
            row("m2", ttt), C.PS[b2][0:1, :], 1.0 / 1024, None, ALU.mult))(s2b, tt),
            reads=[f"PS{s2b}"], writes=[("lnm2", par)] + AK, tag=f"{pref}_m2{tt}")
        P.op("DVE", (lambda ttt: lambda e: e.tensor_tensor(
            row("sc", ttt), row("ms", ttt), row("ms", ttt), ALU.mult))(tt),
            reads=[("lnms", par)], writes=[("lnsc", par)] + AK, tag=f"{pref}_msq{tt}")
        P.op("DVE", (lambda ttt: lambda e: e.scalar_tensor_tensor(
            row("m2", ttt), row("m2", ttt), 1e-5, row("sc", ttt),
            ALU.add, ALU.subtract))(tt),
            reads=[("lnm2", par), ("lnsc", par)], writes=[("lnm2", par)], tag=f"{pref}_var{tt}")
        P.op("ACT", (lambda ttt: lambda e: e.activation(row("m2", ttt), row("m2", ttt), AF.Ln))(tt),
             reads=[("lnm2", par)], writes=[("lnm2", par)], tag=f"{pref}_ln{tt}")
        P.op("ACT", (lambda ttt: lambda e: e.activation(
            row("rs", ttt), row("m2", ttt), AF.Exp, scale=-0.5))(tt),
            reads=[("lnm2", par)], writes=[("lnrs", par)] + AK, tag=f"{pref}_exp{tt}")
        P.op("DVE", (lambda ttt: lambda e: e.tensor_tensor(
            row("nm", ttt), row("ms", ttt), row("rs", ttt), ALU.mult))(tt),
            reads=[("lnms", par), ("lnrs", par)], writes=[("lnnm", par)] + AK, tag=f"{pref}_nm{tt}")

    def apply(tt):
        par = tt % 2
        bRb, bMb = (2, 3) if par == 0 else (6, 7)
        P.op("PE", (lambda bR, ttt: lambda e: e.matmul(
            C.PS[bR][:], C.ones1[:], row("rs", ttt), start=True, stop=True))(bRb, tt),
            reads=[("lnrs", par), "c_ones1"], writes=[f"PS{bRb}"], tag=f"{pref}_bR{tt}")
        P.op("PE", (lambda bM, ttt: lambda e: e.matmul(
            C.PS[bM][:], C.ones1[:], row("nm", ttt), start=True, stop=True))(bMb, tt),
            reads=[("lnnm", par), "c_ones1"], writes=[f"PS{bMb}"], tag=f"{pref}_bM{tt}")
        P.op("ACT", (lambda bM, ii: lambda e: e.copy(B["bc"][ii], C.PS[bM][:]))(bMb, par),
             reads=[f"PS{bMb}"], writes=[("lnbc", par)] + AK, tag=f"{pref}_bMe{tt}")
        for c in range(DC):
            P.op("DVE", (lambda bR, cc, ttt: lambda e: e.tensor_tensor(
                xsl(C, cc, ttt), xsl(C, cc, ttt), C.PS[bR][:], ALU.mult))(bRb, c, tt),
                reads=[("X", c, tt), f"PS{bRb}"], writes=[("X", c, tt)], tag=f"{pref}_ap1_{tt}_{c}")
            P.op("POOL", (lambda cc, ttt, ii: lambda e: e.tensor_tensor(
                xsl(C, cc, ttt), xsl(C, cc, ttt), B["bc"][ii], ALU.subtract))(c, tt, par),
                reads=[("X", c, tt), ("lnbc", par)], writes=[("X", c, tt)],
                tag=f"{pref}_ap2_{tt}_{c}")

    for tt in range(NTT):
        stats(tt)
        if tt > 0:
            apply(tt - 1)
    apply(NTT - 1)


def emit_l0_attn(C, Wq_d, Wk_d, Wv_d, Wo_d, debug=False, upto=None):
    P, al, nc = C.P, C.ba, C.nc
    H, HD = 4, 64

    mask4 = al("l0a_mask4", [128, 512], F32R)
    co128 = al("l0a_co128", [128, 512], F32R)
    onesq = C.mask[:, 63:64]
    onesk = al("l0a_onesk", [128, 1], F32R)
    ri4 = C.GSCR[:, 0:512]
    P.op("POOL", lambda e: e.iota(
        mask4[:].rearrange("p (q j) -> p q j", q=4), [[0, 4], [1, 128]],
        base=0, channel_multiplier=0, allow_small_or_imprecise_dtypes=True),
        writes=[("l0a_mask4",)], tag="c_ci4")
    P.op("POOL", lambda e: e.iota(ri4, [[0, 512]], base=0, channel_multiplier=1,
                                  allow_small_or_imprecise_dtypes=True),
         reads=[], writes=[("gscr", 0)], tag="c_ri4")
    P.op("DVE", lambda e: e.tensor_tensor(mask4[:], mask4[:], ri4, ALU.is_ge),
         reads=[("l0a_mask4",), ("gscr", 0)], writes=[("l0a_mask4",)], tag=("l0a_mask4",))
    P.op("DVE", lambda e: e.tensor_scalar(co128[:], ri4, -1.0, None, ALU.is_ge),
         reads=[("gscr", 0)], writes=[("l0a_co128",)], tag=("l0a_co128",))
    P.op("DVE", lambda e: e.tensor_scalar(onesk[:], C.mask[:, 63:64], 0.5, None, ALU.is_lt),
         reads=["c_mask"], writes=[("l0a_onesk",)], tag="c_onesk")

    VT = al("l0a_vt", [128, NTC * 256], F32R)
    sraw = [VT[:, 0:2048], VT[:, 2048:4096]]
    arena = al("l0a_warena", [128, 4096], F32R)
    WQK = arena[:, 0:4096]
    WQK4 = WQK.rearrange("p (kc h c) -> p kc h c", kc=8, h=4)
    WTv = al("l0a_wvt", [128, 8 * 256], F32R)
    WOT = arena[0:64, 0:4096]
    tb = [0]

    def wload(nm, src, ii):
        P.dma("ACT", (lambda sr: lambda e: e.dma_start(
            out=sraw[ii].rearrange("p (r d) -> p r d", r=2),
            in_=sr.ap().rearrange("(r p) d -> p r d", p=128).bitcast(F32R)))(src),
            writes=[("l0a_sraw", ii)], tag=f"ld_w{nm}")

    wload("q", Wq_d, 0)
    wload("k", Wk_d, 1)
    P.dma("ACT", lambda e: e.dma_start(
        out=C.GSCR[:].rearrange("p (r j) -> p r j", r=8),
        in_=Wo_d.ap().rearrange("(r p) j -> p r j", p=128).bitcast(F32R)),
        writes=[("gscr", 0), ("gscr", 1)], tag="ld_wo")
    for i, nm in enumerate(["q", "k", "v"]):
        if nm == "v":
            wload("v", Wv_d, 0)
        sr = sraw[i % 2]
        for kc in range(DC):
            for r in range(2):
                bnum = 6 + (tb[0] % 2)
                tb[0] += 1
                bank, bkey = C.PS[bnum], f"PS{bnum}"
                P.op("PE", (lambda rr, kcc, bb, sr2: lambda e: e.transpose(
                    bb[:, 0:128].bitcast(F32R),
                    sr2[:, rr * 1024 + kcc * 128: rr * 1024 + (kcc + 1) * 128],
                    C.ident[:]))(r, kc, bank, sr),
                    reads=[("l0a_sraw", i % 2), "c_ident"], writes=[bkey],
                    tag=f"tr_w{nm}_{kc}_{r}")
                if nm == "v":
                    P.op("ACT", (lambda kcc, rr, bb: lambda e: e.copy(
                        WTv[:, kcc * 256 + rr * 128: kcc * 256 + (rr + 1) * 128],
                        bb[:, 0:128].bitcast(F32R)))(kc, r, bank),
                        reads=[bkey], writes=[("l0a_wt", "v")], tag=f"ev_wv_{kc}_{r}")
                else:
                    off = 0 if nm == "q" else 64
                    P.op("ACT", (lambda kcc, rr, bb, o2: lambda e: e.copy(
                        WQK4[:, kcc:kcc + 1, 2 * rr: 2 * rr + 2, o2:o2 + 64],
                        bb[:, 0:128].rearrange("p (k h c) -> p k h c", k=1, c=64)
                        .bitcast(F32R)))(kc, r, bank, off),
                        reads=[bkey], writes=[("l0a_wqk",)], tag=f"ev_w{nm}_{kc}_{r}")

    q4off = C.ba.off
    Q4 = al("l0a_q4", [64, H * N], F32R)
    NQ = C.ba.h[64:65, q4off:q4off + H * N]
    K4 = al("l0a_k4", [64, H * N], F32R)
    T1 = [al(f"l0a_t1{i}", [128, 512], F32) for i in range(2)]
    UB = [al(f"l0a_u{i}", [128, 512], F32) for i in range(2)]
    MB = [al(f"l0a_m{i}", [128, 512], F32) for i in range(2)]
    PB = [al(f"l0a_pb{i}", [64, 512], F32) for i in range(2)]
    SQ = [T1[i][:] for i in range(2)]
    NK1 = [T1[i][0:1, :] for i in range(2)]
    NK2 = [MB[i][0:1, :] for i in range(2)]
    tiles = [(h, tt) for tt in range(NTT) for h in range(H)]

    def qk_proj(idx):
        h, tt = tiles[idx]
        bank, bkey = C.PS[idx % 4], f"PS{idx % 4}"
        for kc in range(DC):
            P.op("PE", (lambda bb, kcc, hh, ttt, st, sp: lambda e: e.matmul(
                bb[:], WQK[:, kcc * 512 + hh * 128: kcc * 512 + (hh + 1) * 128],
                xsl(C, kcc, ttt), start=st, stop=sp))(bank, kc, h, tt, kc == 0, kc == DC - 1),
                reads=[("X", kc, tt), ("l0a_wqk",)], writes=[bkey],
                tag=f"qk_{h}_{tt}_{kc}")

    def qk_chain(idx):
        h, tt = tiles[idx]
        bank, bkey = C.PS[idx % 4], f"PS{idx % 4}"
        i = idx % 2
        a2, b2 = h * N + tt * TT, h * N + (tt + 1) * TT
        q4keys = [("l0a_q4", h, tt * 4 + j) for j in range(4)]
        k4keys = [("l0a_k4", h, tt * 4 + j) for j in range(4)]
        oexp = P.op("ACT", (lambda bb, ii: lambda e: e.activation(T1[ii][:], bb[:], AF.Exp))(bank, i),
                    reads=[bkey], writes=[("l0a_t1", i)], tag=f"pu_exp{idx}")
        P.op("DVE", (lambda bb, ii: lambda e: e.tensor_tensor(
            UB[ii][:], bb[:], co128[:], ALU.add))(bank, i),
            reads=[bkey, ("l0a_co128",)], writes=[("l0a_u", i)], dep_ops=[oexp],
            tag=f"pu_add{idx}")
        P.op("POOL", (lambda ii: lambda e: e.tensor_scalar_min(
            MB[ii][:], T1[ii][:], 1.0))(i),
            reads=[("l0a_t1", i)], writes=[("l0a_m", i)], tag=f"pu_min{idx}")
        P.op("DVE", (lambda ii: lambda e: e.tensor_tensor(
            Q4[:, a2:b2], MB[ii][0:64, :], UB[ii][0:64, :], ALU.max))(i),
            reads=[("l0a_m", i), ("l0a_u", i)], writes=q4keys, tag=f"pu_phq{idx}")
        P.op("DVE", (lambda ii: lambda e: e.tensor_tensor(
            PB[ii][:], MB[ii][64:128, :], UB[ii][64:128, :], ALU.max))(i),
            reads=[("l0a_m", i), ("l0a_u", i)], writes=[("l0a_pb", i)], tag=f"pu_phk{idx}")
        P.op("POOL", (lambda ii: lambda e: e.tensor_tensor(
            SQ[ii][0:64, :], Q4[:, a2:b2], Q4[:, a2:b2], ALU.mult))(i),
            reads=q4keys, writes=[("l0a_sq", i, 0), ("l0a_t1", i)], tag=f"pu_sqq{idx}")
        P.op("DVE", (lambda ii: lambda e: e.tensor_tensor(
            SQ[ii][64:128, :], PB[ii][:], PB[ii][:], ALU.mult))(i),
            reads=[("l0a_pb", i)], writes=[("l0a_sq", i, 1), ("l0a_t1", i)],
            tag=f"pu_sqk{idx}")
        P.op("PE", (lambda ii: lambda e: e.matmul(
            C.PS[4][0:1, :], onesq, SQ[ii][:], start=True, stop=True))(i),
            reads=[("l0a_sq", i, 0), ("l0a_sq", i, 1), "c_mask"], writes=["PS4"],
            tag=f"pu_ssq{idx}")
        P.op("PE", (lambda ii: lambda e: e.matmul(
            C.PS[5][0:1, :], onesk[:], SQ[ii][:], start=True, stop=True))(i),
            reads=[("l0a_sq", i, 0), ("l0a_sq", i, 1), ("l0a_onesk",)], writes=["PS5"],
            tag=f"pu_ssk{idx}")
        P.op("ACT", (lambda aa, bb: lambda e: e.activation(
            NQ[:, aa:bb], C.PS[4][0:1, :], AF.Sqrt))(a2, b2),
            reads=["PS4"], writes=[("l0a_nq", h, tt)], tag=f"pu_nq{idx}")
        P.op("ACT", (lambda ii: lambda e: e.activation(
            NK1[ii], C.PS[5][0:1, :], AF.Sqrt))(i),
            reads=["PS5", ("l0a_t1", i)], writes=[("l0a_nk1", i), ("l0a_t1", i)],
            tag=f"pu_nk{idx}")
        P.op("ACT", (lambda ii: lambda e: e.activation(
            NK2[ii], NK1[ii], AF.Ln, bias=1e-6))(i),
            reads=[("l0a_nk1", i), ("l0a_m", i)], preads=["c_eps6"],
            writes=[("l0a_nk2", i), ("l0a_m", i)], tag=f"pu_ln{idx}")
        P.op("ACT", (lambda ii: lambda e: e.activation(
            NK2[ii], NK2[ii], AF.Exp, scale=-1.0))(i),
            reads=[("l0a_nk2", i)], writes=[("l0a_nk2", i)], tag=f"pu_rc{idx}")
        bb2 = 6 + (idx % 2)
        bank2, bkey2 = C.PS[bb2], f"PS{bb2}"
        P.op("PE", (lambda b3, ii: lambda e: e.matmul(
            b3[0:64, :], C.ones1[0:1, 0:64], NK2[ii], start=True, stop=True))(bank2, i),
            reads=[("l0a_nk2", i), "c_ones1"], writes=[bkey2], tag=f"pu_bc{idx}")
        P.op("DVE", (lambda ii, b3: lambda e: e.tensor_tensor(
            K4[:, a2:b2], PB[ii][:], b3[0:64, :], ALU.mult))(i, bank2),
            reads=[("l0a_pb", i), bkey2], writes=k4keys, tag=f"pu_wr{idx}")

    if upto == "weights":
        return arena, VT
    qk_proj(0)
    qk_proj(1)
    qk_proj(2)
    for t in range(3, len(tiles)):
        qk_proj(t)
        qk_chain(t - 3)
    qk_chain(len(tiles) - 3)
    qk_chain(len(tiles) - 2)
    qk_chain(len(tiles) - 1)
    if upto == "qkv":
        return arena, VT

    for tc in range(NTC):
        bv = tc % 4
        bank, bkey = C.PS[bv], f"PS{bv}"
        for kc in range(DC):
            P.op("PE", (lambda bb, kcc, tcc, st, sp: lambda e: e.matmul(
                bb[:, 0:256], C.X[:, kcc * N + tcc * TC: kcc * N + (tcc + 1) * TC],
                WTv[:, kcc * 256:(kcc + 1) * 256], start=st, stop=sp))(bank, kc, tc, kc == 0, kc == DC - 1),
                reads=[("X", kc, tc // 4), ("l0a_wt", "v")], writes=[bkey], tag=f"v_{tc}_{kc}")
        P.op("ACT", (lambda bb, tcc: lambda e: e.copy(
            VT[:, tcc * 256:(tcc + 1) * 256], bb[:, 0:256].bitcast(F32R)))(bank, tc),
            reads=[bkey],
            writes=[("l0a_vt", tc), ("l0a_sraw", 0 if tc < 8 else 1)], tag=f"v_ev{tc}")

    if upto == "v":
        return arena, VT
    for h in range(H):
        for r in range(8):
            bnum = 6 + (tb[0] % 2)
            tb[0] += 1
            bank, bkey = C.PS[bnum], f"PS{bnum}"
            P.op("PE", (lambda hh, rr, bb: lambda e: e.transpose(
                bb[0:64, 0:128].bitcast(F32R),
                C.GSCR[:, rr * 256 + hh * 64: rr * 256 + (hh + 1) * 64], C.ident[:]))(h, r, bank),
                reads=[("gscr", 0), ("gscr", 1), "c_ident"], writes=[bkey], tag=f"tr_wo{h}_{r}")
            P.op("ACT", (lambda hh, rr, bb: lambda e: e.copy(
                WOT[0:64, hh * 1024 + rr * 128: hh * 1024 + (rr + 1) * 128],
                bb[0:64, 0:128].bitcast(F32R)))(h, r, bank),
                reads=[bkey], writes=[("l0a_wot", h), ("l0a_wqk",)],
                tag=f"ev_wo{h}_{r}")

    if upto == "wostage":
        if debug:
            dk0 = C.nc.dram_tensor("dbg_k4pre", [64, H * N], F32, kind="ExternalOutput")
            dn0 = C.nc.dram_tensor("dbg_nq", [1, H * N], F32, kind="ExternalOutput")
            P.dma("SP", lambda e: e.dma_start(out=dk0.ap(), in_=K4[:].bitcast(F32)),
                  reads=[("l0a_k4", h2, t2) for h2 in range(H) for t2 in range(NTC)], tag="dbgk0")
            P.dma("SP", lambda e: e.dma_start(out=dn0.ap(), in_=NQ.bitcast(F32)),
                  reads=[("l0a_nq", h2, t2) for h2 in range(H) for t2 in range(NTT)], tag="dbgn0")
        return arena, VT
    KVK = al("l0a_kvk", [64, 272], F32R)
    SM4 = [al(f"l0a_sm4{i}", [128, 512], F32R) for i in range(2)]
    KTM4 = al("l0a_ktm4", [128, 256], F32R)
    DENR = [T1[i][0:1, :] for i in range(2)]
    RD = [UB[i][0:1, :] for i in range(2)]
    K4v = K4[:].rearrange("p (h t) -> p h t", h=H)
    NQv = NQ.rearrange("p (h t) -> p h t", h=H)

    def rnorm(tc):
        p = tc % 2
        yb, ybk = (C.PS[2], "PS2") if p == 0 else (C.PS[6], "PS6")
        k4keys = [("l0a_k4", h, tc) for h in range(H)]
        P.op("ACT", (lambda pp: lambda e: e.activation(
            RD[pp], DENR[pp], AF.Ln, bias=1e-12))(p),
            reads=[("l0a_denr", p), ("l0a_u", p)], preads=["c_eps12"],
            writes=[("l0a_rd", p), ("l0a_u", p)], tag=f"at_ln{tc}")
        P.op("ACT", (lambda pp: lambda e: e.activation(
            RD[pp], RD[pp], AF.Exp, scale=-1.0))(p),
            reads=[("l0a_rd", p)], writes=[("l0a_rd", p)], tag=f"at_rc{tc}")
        P.op("PE", (lambda pp: lambda e: e.matmul(
            C.PS[5][0:64, :], C.ones1[0:1, 0:64], RD[pp], start=True, stop=True))(p),
            reads=[("l0a_rd", p), "c_ones1"], writes=["PS5"], tag=f"at_bc{tc}")
        P.op("ACT", (lambda bb, tcc: lambda e: e.copy(
            K4v[:, :, tcc * TC:(tcc + 1) * TC],
            bb[0:64, :].rearrange("p (h t) -> p h t", h=H).bitcast(F32R)))(yb, tc),
            reads=[ybk], writes=k4keys, tag=f"at_yc{tc}")
        P.op("DVE", (lambda tcc: lambda e: e.tensor_tensor(
            K4v[:, :, tcc * TC:(tcc + 1) * TC], K4v[:, :, tcc * TC:(tcc + 1) * TC],
            C.PS[5][0:64, :].rearrange("p (h t) -> p h t", h=H), ALU.mult))(tc),
            reads=["PS5"] + k4keys, writes=k4keys, tag=f"at_ym{tc}")

    for tc in range(NTC):
        p = tc % 2
        sb, sbk = (C.PS[0], "PS0") if p == 0 else (C.PS[1], "PS1")
        yb, ybk = (C.PS[2], "PS2") if p == 0 else (C.PS[6], "PS6")
        qsl = [(h * N + tc * TC, h * N + (tc + 1) * TC) for h in range(H)]
        for h in range(H):
            P.op("PE", (lambda bb, aa, bb2, hh: lambda e: e.matmul(
                bb[:, hh * TC:(hh + 1) * TC], K4[:, aa:bb2], Q4[:, aa:bb2],
                start=(hh == 0), stop=(hh == H - 1), skip_group_check=(hh > 0)))(sb, *qsl[h], h),
                reads=[("l0a_q4", h, tc), ("l0a_k4", h, tc)], writes=[sbk],
                tag=f"at_s{h}_{tc}")
        P.op("DVE", (lambda bb, pp: lambda e: e.tensor_tensor(
            SM4[pp][:], bb[:], mask4[:], ALU.mult))(sb, p),
            reads=[sbk, ("l0a_mask4",)], writes=[("l0a_sm4", p)], tag=f"at_sm{tc}")
        P.op("PE", (lambda pp, solo: lambda e: e.matmul(
            C.PS[3][0:1, :], C.ones_col[:], SM4[pp][:], start=True, stop=solo))(p, tc == 0),
            reads=[("l0a_sm4", p), "c_ones_col"], writes=["PS3"], tag=f"at_d1_{tc}")
        if tc > 0:
            for h in range(H):
                P.op("PE", (lambda hh, aa, bb2: lambda e: e.matmul(
                    C.PS[3][0:1, hh * TC:(hh + 1) * TC], KVK[:, 256 + hh:257 + hh],
                    Q4[:, aa:bb2], start=False, stop=(hh == H - 1),
                    skip_group_check=True))(h, *qsl[h]),
                    reads=[("l0a_q4", h, tc), ("l0a_kvk",)], writes=["PS3"],
                    tag=f"at_d2{h}_{tc}")
        P.op("DVE", (lambda pp, tcc: lambda e: e.scalar_tensor_tensor(
            DENR[pp].rearrange("p (h t) -> p h t", h=H),
            NQv[:, :, tcc * TC:(tcc + 1) * TC], 1e-6,
            C.PS[3][0:1, :].rearrange("p (h t) -> p h t", h=H),
            ALU.mult, ALU.add))(p, tc),
            reads=["PS3"] + [("l0a_nq", h, tc // 4) for h in range(H)],
            writes=[("l0a_denr", p), ("l0a_t1", p)], tag=f"at_de{tc}")
        for h in range(H):
            P.op("PE", (lambda aa, bb2, hh: lambda e: e.transpose(
                C.PS[7][:, hh * 64:(hh + 1) * 64].bitcast(F32R), K4[:, aa:bb2],
                C.ident[0:64, 0:64]))(*qsl[h], h),
                reads=[("l0a_k4", h, tc), "c_ident"], writes=["PS7"], tag=f"at_kt{h}_{tc}")
        P.op("ACT", lambda e: e.copy(KTM4[:], C.PS[7][:, 0:256].bitcast(F32R)),
             reads=["PS7"], writes=[("l0a_ktm4",)], tag=f"at_kte{tc}")
        for h in range(H):
            P.op("PE", (lambda bb, pp, hh, tcc, st, sp: lambda e: e.matmul(
                bb[0:64, hh * TC:(hh + 1) * TC],
                VT[:, tcc * 256 + hh * 64: tcc * 256 + (hh + 1) * 64],
                SM4[pp][:, hh * TC:(hh + 1) * TC], start=st, stop=sp,
                skip_group_check=(hh > 0)))(yb, p, h, tc, h == 0, tc == 0 and h == H - 1),
                reads=[("l0a_vt", tc), ("l0a_sm4", p)], writes=[ybk], tag=f"at_y1{h}_{tc}")
        if tc > 0:
            for h in range(H):
                P.op("PE", (lambda bb, hh, aa, bb2: lambda e: e.matmul(
                    bb[0:64, hh * TC:(hh + 1) * TC], KVK[:, hh * 64:(hh + 1) * 64],
                    Q4[:, aa:bb2], start=False, stop=(hh == H - 1),
                    skip_group_check=True))(yb, h, *qsl[h]),
                    reads=[("l0a_q4", h, tc), ("l0a_kvk",)], writes=[ybk],
                    tag=f"at_y2{h}_{tc}")
        gf = (tc == 0)
        for h in range(H):
            P.op("PE", (lambda pp, hh, tcc, st: lambda e: e.matmul(
                C.PS[4][0:64, hh * 64:(hh + 1) * 64], KTM4[:, hh * 64:(hh + 1) * 64],
                VT[:, tcc * 256 + hh * 64: tcc * 256 + (hh + 1) * 64],
                start=st, stop=(tcc == NTC - 1), skip_group_check=(not st)))(p, h, tc, gf and h == 0),
                reads=[("l0a_ktm4",), ("l0a_vt", tc)], writes=["PS4"], tag=f"at_kv{h}_{tc}")
        for h in range(H):
            P.op("PE", (lambda pp, hh, tcc: lambda e: e.matmul(
                C.PS[4][0:64, 256 + hh:257 + hh], KTM4[:, hh * 64:(hh + 1) * 64].bitcast(F32),
                C.ones_col[:].bitcast(F32), start=False, stop=(tcc == NTC - 1),
                skip_group_check=True))(p, h, tc),
                reads=[("l0a_ktm4",), "c_ones_col"], writes=["PS4"], tag=f"at_ks{h}_{tc}")
        if tc < NTC - 1:
            P.op("ACT", lambda e: e.copy(KVK[:, 0:260], C.PS[4][0:64, 0:260].bitcast(F32R)),
                 reads=["PS4"], writes=[("l0a_kvk",)], tag=f"at_kvev{tc}")
        if tc > 0:
            rnorm(tc - 1)
    rnorm(NTC - 1)

    if upto == "attn":
        return arena, VT
    if debug:
        dq = C.nc.dram_tensor("dbg_q4", [64, H * N], F32, kind="ExternalOutput")
        dk = C.nc.dram_tensor("dbg_k4", [64, H * N], F32, kind="ExternalOutput")
        dv = C.nc.dram_tensor("dbg_vt", [128, NTC * 256], F32, kind="ExternalOutput")
        P.dma("SP", lambda e: e.dma_start(out=dq.ap(), in_=Q4[:].bitcast(F32)),
              reads=[("l0a_q4", h2, t2) for h2 in range(H) for t2 in range(NTC)], tag="dbg_q")
        P.dma("SP", lambda e: e.dma_start(out=dk.ap(), in_=K4[:].bitcast(F32)),
              reads=[("l0a_k4", h2, t2) for h2 in range(H) for t2 in range(NTC)], tag="dbg_k")
        P.dma("SP", lambda e: e.dma_start(out=dv.ap(), in_=VT[:].bitcast(F32)),
              reads=[("l0a_vt", t2) for t2 in range(NTC)], tag="dbg_v")

    for och in range(DC):
        for tt in range(NTT):
            b = (och * NTT + tt) % 4
            bank, bkey = C.PS[b], f"PS{b}"
            for h in range(H):
                P.op("PE", (lambda bb, hh, occ, ttt, st, sp: lambda e: e.matmul(
                    bb[:], WOT[0:64, hh * 1024 + occ * 128: hh * 1024 + (occ + 1) * 128],
                    K4[:, hh * N + ttt * TT: hh * N + (ttt + 1) * TT],
                    start=st, stop=sp))(bank, h, och, tt, h == 0, h == H - 1),
                    reads=[("l0a_wot", h)] + [("l0a_k4", h, tt * 4 + j) for j in range(4)],
                    writes=[bkey], tag=f"wo_{och}_{tt}_{h}")
            P.op("DVE", (lambda bb, occ, ttt: lambda e: e.tensor_tensor(
                xsl(C, occ, ttt), xsl(C, occ, ttt), bb[:], ALU.add))(bank, och, tt),
                reads=[("X", och, tt), bkey], writes=[("X", och, tt)], tag=f"wo_res{och}_{tt}")
    return arena, VT


def build_prog_l0_attn(first=True, debug=False, upto=None):
    nc = bass.Bass("TRN2", target_bir_lowering=False)
    x_d = nc.dram_tensor("x", [N, 1024], F32, kind="ExternalInput")
    wq = nc.dram_tensor("wq", [256, 1024], F32, kind="ExternalInput")
    wk = nc.dram_tensor("wk", [256, 1024], F32, kind="ExternalInput")
    wv = nc.dram_tensor("wv", [256, 1024], F32, kind="ExternalInput")
    wo = nc.dram_tensor("wo", [1024, 256], F32, kind="ExternalInput")
    xo = nc.dram_tensor("xo", [1024, N], F32, kind="ExternalOutput")
    C = make_ctx(nc)
    C.ba = ArenaAlloc(C.al("arena", [128, ACOLS], F32R), ACOLS)
    emit_consts(C)
    emit_stream_in_tm(C, x_d)
    arena, VT = emit_l0_attn(C, wq, wk, wv, wo, debug=debug, upto=upto)
    C.ln_bufs = {
        "sq": [arena[:, 0:512], arena[:, 512:1024]],
        "bc": [arena[:, 1024:1536], arena[:, 1536:2048]],
        "ms": arena[0:1, 2048:3072],
        "m2": arena[0:1, 3072:4096],
        "sc": VT[0:1, 0:1024],
        "rs": VT[0:1, 1024:2048],
        "nm": VT[0:1, 2048:3072],
    }
    C.ln_alias = ([("l0a_wot", h) for h in range(4)] +
                  [("l0a_wqk",), ("l0a_sraw", 0), ("l0a_sraw", 1)] +
                  [("l0a_vt", t) for t in range(12)])
    emit_postln(C, "0a")
    emit_stream_out_fm(C, xo)
    with ExitStack() as stack:
        C.P.build(stack)
    return nc, C


_PROG_CACHE = {}


def _get_prog(name, builder):
    if name not in _PROG_CACHE:
        _PROG_CACHE[name] = builder()
    return _PROG_CACHE[name]


def emit_l0_ffn(C, router_d, wg_d, wu_d, wd_d, wsg_d, wsu_d, wsd_d, debug=False, vex_filter=None):
    P, al, nc = C.P, C.ba, C.nc
    E = 8

    RING = al("ffn_ring", [128, 3 * 4096], F32R)
    RW = al("ffn_rw", [128, 64], F32R)
    SFMs = al("ffn_sfms", [8, 512], F32R)
    STM = al("ffn_stm", [128, 128], F32)
    SCR = al("ffn_scr", [128, 128], F32)
    CNT = al("ffn_cnt", [128, 128], F32)
    MWF = al("ffn_mwf", [8, N], F32R)
    MW8 = al("ffn_mw8", [128, 128], F32R)
    RI8 = al("ffn_ri8", [8, 1024], F32)
    OH8 = al("ffn_oh8", [8, 1024], F32R)
    BLK = OH8
    SX = al("ffn_sx", [128, 8], F32)
    RMX = al("ffn_rmx", [128, 1], F32)
    RSM = al("ffn_rsm", [128, 1], F32)
    CEQ = al("ffn_ceq", [128, 1], F32)
    COLB = [al(f"ffn_col{j}", [128, 1], F32) for j in range(8)]
    SG = [al(f"ffn_sg{i}", [128, 512], F32) for i in range(2)]
    HA = [al(f"ffn_ha{i}", [128, 512], F32) for i in range(2)]
    H2 = [al(f"ffn_h2{i}", [128, 4 * 512], F32R) for i in range(2)]
    BCW = [al(f"ffn_bcw{i}", [128, 512], F32R) for i in range(2)]
    YH = al("ffn_yh", [128, DC * 1024], F32R)

    P.op("POOL", lambda e: e.iota(RI8[:], [[0, 1024]], base=0, channel_multiplier=1,
                                  allow_small_or_imprecise_dtypes=True), writes=["ffn_ri8"])
    P.op("POOL", lambda e: e.iota(BLK[:].rearrange("p (b j) -> p b j", b=8), [[1, 8], [0, 128]],
                                  base=0, channel_multiplier=0,
                                  allow_small_or_imprecise_dtypes=True), writes=["ffn_oh8"])
    P.op("DVE", lambda e: e.tensor_tensor(OH8[:], RI8[:], OH8[:], ALU.is_equal),
         reads=["ffn_ri8", "ffn_oh8"], writes=["ffn_oh8"])

    P.dma("SP", lambda e: e.dma_start(
        out=RW[:].rearrange("p (k e) -> p k e", k=8),
        in_=router_d.ap().rearrange("(k p) e -> p k e", p=128).bitcast(F32R)),
        writes=["ffn_rw"], tag="ld_rw")
    for tt in range(NTT):
        b = tt % 2
        bank, bkey = C.PS[b], f"PS{b}"
        for kc in range(DC):
            P.op("PE", (lambda bb, kcc, ttt, st, sp: lambda e: e.matmul(
                bb[0:8, :], RW[:, kcc * 8:(kcc + 1) * 8], xsl(C, kcc, ttt),
                start=st, stop=sp))(bank, kc, tt, kc == 0, kc == DC - 1),
                reads=[("X", kc, tt), "ffn_rw"], writes=[bkey], tag=f"rt_{tt}_{kc}")
        P.op("DVE", (lambda bb: lambda e: e.tensor_copy(SFMs[:], bb[0:8, :]))(bank),
            reads=[bkey], writes=["ffn_sfms"], tag=f"rt_ev{tt}")
        for tcl in range(4):
            tc = tt * 4 + tcl
            b2 = 2 + (tcl % 2)
            bank2, bkey2 = C.PS[b2], f"PS{b2}"
            P.op("PE", (lambda bb, tcll: lambda e: e.transpose(
                bb[0:128, 0:8].bitcast(F32R), SFMs[:, tcll * TC:(tcll + 1) * TC],
                C.ident[0:8, 0:8]))(bank2, tcl),
                reads=["ffn_sfms", "c_ident"], writes=[bkey2], tag=f"rtT{tc}")
            P.op("DVE", (lambda bb, tcc: lambda e: e.tensor_copy(
                STM[:, tcc * 8:(tcc + 1) * 8], bb[0:128, 0:8].bitcast(F32R)))(bank2, tc),
                reads=[bkey2], writes=[("ffn_stm", tc)], tag=f"rtTe{tc}")
            sl = (tc * 8, (tc + 1) * 8)
            P.op("DVE", (lambda a, bb2: lambda e: e.reduce_max(RMX[:], STM[:, a:bb2], axis=AX.X))(*sl),
                 preads=[("ffn_stm", tc)], writes=["ffn_rmx"], tag=f"sm_mx{tc}", fence_out=True)
            P.op("DVE", (lambda a, bb2: lambda e: e.tensor_scalar(SX[:], STM[:, a:bb2], RMX[:].bitcast(F32), None,
                                                                  ALU.subtract))(*sl),
                 reads=[("ffn_stm", tc)], preads=["ffn_rmx"], writes=["ffn_sx"], tag=f"sm_sb{tc}")
            P.op("ACT", lambda e: e.activation(SX[:], SX[:], AF.Exp),
                 reads=["ffn_sx"], writes=["ffn_sx"], tag=f"sm_ex{tc}")
            P.op("DVE", lambda e: e.reduce_sum(RSM[:], SX[:], axis=AX.X),
                 preads=["ffn_sx"], writes=["ffn_rsm"], tag=f"sm_su{tc}", fence_out=True)
            P.op("DVE", lambda e: e.reciprocal(RMX[:], RSM[:]),
                 preads=["ffn_rsm"], writes=["ffn_rmx"], tag=f"sm_rc{tc}")
            P.op("DVE", (lambda a, bb2: lambda e: e.tensor_scalar(SCR[:, a:bb2], SX[:], RMX[:].bitcast(F32), None,
                                                                  ALU.mult))(*sl),
                 reads=["ffn_sx"], preads=["ffn_rmx"], writes=[("ffn_scr", tc)], tag=f"sm_ml{tc}")
            for ex in range(8):
                P.op("POOL", (lambda a, exx: lambda e: e.tensor_copy(
                    COLB[exx][:], SCR[:, a + exx:a + exx + 1]))(tc * 8, ex),
                    reads=[("ffn_scr", tc)], writes=[("ffn_colb", ex)], tag=f"tk_c{tc}_{ex}")
            for ex in range(8):
                P.op("DVE", (lambda a, bb2, exx: lambda e: e.tensor_scalar(
                    SX[:], SCR[:, a:bb2], COLB[exx][:].bitcast(F32), None, ALU.is_gt))(*sl, ex),
                    reads=[("ffn_scr", tc)], preads=[("ffn_colb", ex)], writes=["ffn_sx"],
                    tag=f"tk_gt{tc}_{ex}")
                if ex > 0:
                    P.op("DVE", (lambda a, exx: lambda e: e.tensor_scalar(
                        SX[:, 0:exx], SCR[:, a:a + exx], COLB[exx][:].bitcast(F32), None, ALU.is_ge))(tc * 8, ex),
                        reads=[("ffn_scr", tc)], preads=[("ffn_colb", ex)], writes=["ffn_sx"],
                        tag=f"tk_eq{tc}_{ex}")
                P.op("DVE", (lambda a, exx: lambda e: e.reduce_sum(
                    CNT[:, a + exx:a + exx + 1], SX[:], axis=AX.X))(tc * 8, ex),
                    preads=["ffn_sx"], writes=[("ffn_cnt", tc)], tag=f"tk_s{tc}_{ex}")
            P.op("DVE", (lambda a, bb2: lambda e: e.tensor_scalar(
                CNT[:, a:bb2], CNT[:, a:bb2], 5.5, None, ALU.is_le))(*sl),
                reads=[], preads=[("ffn_cnt", tc)], writes=[("ffn_cnt", tc)], tag=f"tk_le{tc}")
            P.op("DVE", (lambda a, bb2: lambda e: e.tensor_tensor(
                MW8[:, a:bb2], SCR[:, a:bb2], CNT[:, a:bb2], ALU.mult))(*sl),
                reads=[("ffn_scr", tc), ("ffn_cnt", tc)], writes=[("ffn_mw8", tc)], tag=f"tk_mw{tc}")
            b3 = 4 + (tcl % 2)
            bank3, bkey3 = C.PS[b3], f"PS{b3}"
            P.op("PE", (lambda a, bb2, bk: lambda e: e.transpose(
                bk[0:8, 0:128].bitcast(F32R), MW8[:, a:bb2], C.ident[:]))(*sl, bank3),
                reads=[("ffn_mw8", tc), "c_ident"], writes=[bkey3], tag=f"tk_T{tc}")
            P.op("DVE", (lambda tcc, bk: lambda e: e.tensor_copy(
                MWF[:, tcc * TC:(tcc + 1) * TC], bk[0:8, 0:128].bitcast(F32R)))(tc, bank3),
                reads=[bkey3], writes=[("ffn_mwf", tc)], tag=f"tk_Te{tc}")

    if debug:
        dmw = nc.dram_tensor("dbg_mwf", [8, N], F32, kind="ExternalOutput")
        P.dma("SP", lambda e: e.dma_start(out=dmw.ap(), in_=MWF[:].bitcast(F32)),
              reads=[("ffn_mwf", t2) for t2 in range(NTC)], tag="dbg_mwf")

    vex = []
    for fs in range(2):
        vex.append((wsg_d[:, fs * 512:(fs + 1) * 512], wsu_d[:, fs * 512:(fs + 1) * 512],
                    wsd_d[fs * 512:(fs + 1) * 512, :], False, f"s{fs}"))
    for e in range(E):
        vex.append((wg_d[e], wu_d[e], wd_d[e], True, f"e{e}"))
    if vex_filter is not None:
        vex = [v for v in vex if v[4] in vex_filter]

    rp = [0]

    def ring_load(src_ap, kparts, tag):
        i = rp[0] % 3
        rp[0] += 1
        cols = src_ap.shape[-1]
        view = RING[:, i * 4096: i * 4096 + kparts * cols]
        P.dma("SP", (lambda v, sa, kp: lambda e: e.dma_start(
            out=v.rearrange("p (k c) -> p k c", k=kp),
            in_=sa.rearrange("(k p) c -> p k c", p=128).bitcast(F32R)))(view, src_ap, kparts),
            writes=[("ring", i)], tag="rl_" + tag)
        return view, ("ring", i)

    dcnt = [0]
    for th in range(2):
        for vi, (g_ap, u_ap, d_ap, scaled, nm) in enumerate(vex):
            GV, gk = ring_load(g_ap, 8, f"{nm}g{th}")
            UV, uk = ring_load(u_ap, 8, f"{nm}u{th}")
            DV, dk = ring_load(d_ap, 4, f"{nm}d{th}")
            ei = int(nm[1]) if nm[0] == "e" else None
            for ttl in range(2):
                tt = th * 2 + ttl
                if scaled:
                    bcb = 4 + (dcnt[0] % 4)
                    dcnt[0] += 1
                    bank, bkey = C.PS[bcb], f"PS{bcb}"
                    P.op("PE", (lambda bb, eee, ttt: lambda e: e.matmul(
                        bb[:], OH8[:, eee * 128:(eee + 1) * 128],
                        MWF[:, ttt * TT:(ttt + 1) * TT], start=True, stop=True))(bank, ei, tt),
                        reads=["ffn_oh8"] + [("ffn_mwf", tt * 4 + j) for j in range(4)],
                        writes=[bkey], tag=f"bcw{nm}_{tt}")
                    P.op("ACT", (lambda bb, ii: lambda e: e.copy(BCW[ii][:], bb[:]))(bank, ttl),
                         reads=[bkey], writes=[("ffn_bcw", ttl)], tag=f"bcwe{nm}_{tt}")
                for fch in range(4):
                    gb, ub = fch % 2, 2 + (fch % 2)
                    gbank, gbkey = C.PS[gb], f"PS{gb}"
                    ubank, ubkey = C.PS[ub], f"PS{ub}"
                    for kc in range(DC):
                        P.op("PE", (lambda bb, gv, kcc, fc, ttt, st, sp: lambda e: e.matmul(
                            bb[:], gv[:, kcc * 512 + fc * 128: kcc * 512 + (fc + 1) * 128],
                            xsl(C, kcc, ttt), start=st, stop=sp))(gbank, GV, kc, fch, tt,
                                                                 kc == 0, kc == DC - 1),
                            reads=[("X", kc, tt), gk], writes=[gbkey], tag=f"g{nm}_{tt}_{fch}_{kc}")
                    for kc in range(DC):
                        P.op("PE", (lambda bb, uv, kcc, fc, ttt, st, sp: lambda e: e.matmul(
                            bb[:], uv[:, kcc * 512 + fc * 128: kcc * 512 + (fc + 1) * 128],
                            xsl(C, kcc, ttt), start=st, stop=sp))(ubank, UV, kc, fch, tt,
                                                                 kc == 0, kc == DC - 1),
                            reads=[("X", kc, tt), uk], writes=[ubkey], tag=f"u{nm}_{tt}_{fch}_{kc}")
                    i = fch % 2
                    P.op("ACT", (lambda bb, ii: lambda e: e.activation(SG[ii][:], bb[:], AF.Silu))(gbank, i),
                         reads=[gbkey], writes=[("ffn_sg", i)], tag=f"sl{nm}_{tt}_{fch}")
                    if scaled:
                        P.op("DVE", (lambda bb, ii: lambda e: e.tensor_tensor(
                            HA[ii][:], SG[ii][:], bb[:], ALU.mult))(ubank, i),
                            reads=[("ffn_sg", i), ubkey], writes=[("ffn_ha", i)], tag=f"hm{nm}_{tt}_{fch}")
                        P.op("POOL", (lambda ii, fc, wi: lambda e: e.tensor_tensor(
                            H2[wi][:, fc * 512:(fc + 1) * 512], HA[ii][:], BCW[wi][:],
                            ALU.mult))(i, fch, ttl),
                            reads=[("ffn_ha", i), ("ffn_bcw", ttl)],
                            writes=[("ffn_h2", ttl, fch)], tag=f"hw{nm}_{tt}_{fch}")
                    else:
                        P.op("DVE", (lambda bb, ii, fc, wi: lambda e: e.tensor_tensor(
                            H2[wi][:, fc * 512:(fc + 1) * 512], SG[ii][:], bb[:],
                            ALU.mult))(ubank, i, fch, ttl),
                            reads=[("ffn_sg", i), ubkey],
                            writes=[("ffn_h2", ttl, fch)], tag=f"hm{nm}_{tt}_{fch}")
                for och in range(DC):
                    db = 4 + (dcnt[0] % 4)
                    dcnt[0] += 1
                    bank, bkey = C.PS[db], f"PS{db}"
                    for kc2 in range(4):
                        P.op("PE", (lambda bb, dv, kcc, occ, wi, st, sp: lambda e: e.matmul(
                            bb[:], dv[:, kcc * 1024 + occ * 128: kcc * 1024 + (occ + 1) * 128],
                            H2[wi][:, kcc * 512:(kcc + 1) * 512], start=st, stop=sp))(
                                bank, DV, kc2, och, ttl, kc2 == 0, kc2 == 3),
                            reads=[("ffn_h2", ttl, kc2), dk], writes=[bkey],
                            tag=f"d{nm}_{tt}_{och}_{kc2}")
                    yh = YH[:, och * 1024 + ttl * 512: och * 1024 + (ttl + 1) * 512]
                    if vi == 0:
                        P.op("DVE", (lambda bb, y2: lambda e: e.tensor_copy(y2, bb[:]))(bank, yh),
                            reads=[bkey], writes=[("ffn_yh", och, ttl)], tag=f"dy{nm}_{tt}_{och}")
                    else:
                        P.op("DVE", (lambda bb, y2: lambda e: e.tensor_tensor(
                            y2, y2, bb[:], ALU.add))(bank, yh),
                            reads=[bkey, ("ffn_yh", och, ttl)], writes=[("ffn_yh", och, ttl)],
                            tag=f"dy{nm}_{tt}_{och}")
        for och in range(DC):
            for ttl in range(2):
                tt = th * 2 + ttl
                yh = YH[:, och * 1024 + ttl * 512: och * 1024 + (ttl + 1) * 512]
                P.op("DVE", (lambda occ, ttt, y2: lambda e: e.tensor_tensor(
                    xsl(C, occ, ttt), xsl(C, occ, ttt), y2, ALU.add))(och, tt, yh),
                    reads=[("X", och, tt), ("ffn_yh", och, ttl)], writes=[("X", och, tt)],
                    tag=f"xadd{th}_{och}_{tt}")
    if debug:
        dx = nc.dram_tensor("dbg_x", [1024, N], F32, kind="ExternalOutput")
        P.dma("SP", lambda e: e.dma_start(
            out=dx.ap().rearrange("(c p) t -> p c t", p=128),
            in_=C.X[:].rearrange("p (c t) -> p c t", c=DC).bitcast(F32)),
            reads=xall(), tag="dbg_x")
    return RING


def build_prog_l0_ffn(debug=False, vex_filter=None):
    nc = bass.Bass("TRN2", target_bir_lowering=False)
    x_d = nc.dram_tensor("x", [1024, N], F32, kind="ExternalInput")
    rw = nc.dram_tensor("rw", [1024, 8], F32, kind="ExternalInput")
    wg = nc.dram_tensor("wg", [8, 1024, 512], F32, kind="ExternalInput")
    wu = nc.dram_tensor("wu", [8, 1024, 512], F32, kind="ExternalInput")
    wd = nc.dram_tensor("wd", [8, 512, 1024], F32, kind="ExternalInput")
    wsg = nc.dram_tensor("wsg", [1024, 1024], F32, kind="ExternalInput")
    wsu = nc.dram_tensor("wsu", [1024, 1024], F32, kind="ExternalInput")
    wsd = nc.dram_tensor("wsd", [1024, 1024], F32, kind="ExternalInput")
    xo = nc.dram_tensor("xo", [1024, N], F32, kind="ExternalOutput")
    C = make_ctx(nc)
    C.ba = ArenaAlloc(C.al("arena", [128, ACOLS], F32R), ACOLS)
    emit_consts(C)
    emit_stream_in_fm(C, x_d)
    RING = emit_l0_ffn(C, rw, wg.ap(), wu.ap(), wd.ap(), wsg.ap(), wsu.ap(), wsd.ap(),
                       debug=debug, vex_filter=vex_filter)
    C.ln_bufs = {
        "sq": [RING[:, 0:512], RING[:, 512:1024]],
        "bc": [RING[:, 1024:1536], RING[:, 1536:2048]],
        "ms": RING[0:1, 2048:3072],
        "m2": RING[0:1, 3072:4096],
        "sc": RING[0:1, 4096:5120],
        "rs": RING[0:1, 5120:6144],
        "nm": RING[0:1, 6144:7168],
    }
    C.ln_alias = [("ring", 0), ("ring", 1)]
    emit_postln(C, "0f")
    emit_stream_out_fm(C, xo)
    with ExitStack() as stack:
        C.P.build(stack)
    return nc, C


def emit_l1_attn(C, Wq_d, bq_d, Wk_d, bk_d, Wv_d, bv_d, Wp_d, bp_d, wqf_d, wkf_d):
    P, al, nc = C.P, C.ba, C.nc

    RINGW = al("l1a_ringw", [128, 3 * 1024], F32R)
    PAIRBUF = al("l1a_pairbuf", [128, 6 * 2048], F32R)
    KHP = [PAIRBUF[:, (3 * i + 0) * 2048:(3 * i + 1) * 2048] for i in range(2)]
    VHP = [PAIRBUF[:, (3 * i + 1) * 2048:(3 * i + 2) * 2048] for i in range(2)]
    QHP = [PAIRBUF[:, (3 * i + 2) * 2048:(3 * i + 3) * 2048] for i in range(2)]
    KP = al("l1a_kp", [128, NTC * 512], F32R)
    VTM = al("l1a_vtm", [128, NTC * 128], F32R)
    OUTP = VTM
    QPT = [al(f"l1a_qpt{i}", [128, 4 * 512], F32R) for i in range(2)]
    CTXP = [al(f"l1a_ctxp{i}", [128, 512], F32R) for i in range(2)]
    WQF = al("l1a_wqf", [128, 512], F32R)
    WKF = al("l1a_wkf", [128, 512], F32R)
    BQA = al("l1a_bqa", [128, 8], F32R)
    BKA = al("l1a_bka", [128, 8], F32R)
    BVA = al("l1a_bva", [128, 8], F32R)
    BPA = al("l1a_bpa", [128, 8], F32R)
    BC = {k: al(f"l1a_bc{k}", [128, 1], F32R) for k in ("q", "k", "v", "p")}
    BC2 = {k: al(f"l1a_bc2{k}", [128, 1], F32R) for k in ("q", "k", "v")}
    KSR = [al("l1a_ksr0", [1, 512], F32R)] * 2
    KSP = [al(f"l1a_ksp{i}", [128, 4], F32R) for i in range(2)]
    _den = al("l1a_den0", [1, 512], F32R)
    _rd = al("l1a_rd10", [1, 512], F32R)
    DEN = [_den, _den]
    RD1 = [_rd, _rd]
    BST = KP[0:8, 0:128]
    outf = nc.dram_tensor("l1a_outf", [1024, N], F32, kind="Internal")

    WPJ = KP[:, 0:8 * 1024]
    OFV = PAIRBUF[:, 0:8 * 1024]

    for (T_, nm, src) in [(WQF, "l1a_wqf", wqf_d), (WKF, "l1a_wkf", wkf_d)]:
        P.op("POOL", (lambda t: lambda e: e.iota(t[:], [[0, 512]], base=0, channel_multiplier=0,
                                                 allow_small_or_imprecise_dtypes=True))(T_),
             writes=[nm], tag=nm + "_z")
        P.dma("SP", (lambda t, sa: lambda e: e.dma_start(out=t[0:64, 0:256],
                                                         in_=sa.ap().bitcast(F32R)))(T_, src),
              writes=[nm], tag=nm + "_a")
        P.dma("SP", (lambda t, sa: lambda e: e.dma_start(out=t[64:128, 256:512],
                                                         in_=sa.ap().bitcast(F32R)))(T_, src),
              writes=[nm], tag=nm + "_b")
    for (Tb, nm, src) in [(BQA, "l1a_bqa", bq_d), (BKA, "l1a_bka", bk_d),
                          (BVA, "l1a_bva", bv_d), (BPA, "l1a_bpa", bp_d)]:
        P.dma("SP", (lambda sa: lambda e: e.dma_start(
            out=BST, in_=sa.ap().rearrange("(c p) -> c p", p=128).bitcast(F32R)))(src),
            writes=["l1a_bst", ("l1a_kp", 0)], tag=nm + "_ld")
        P.op("PE", lambda e: e.transpose(C.PS[6][:, 0:8].bitcast(F32R), BST, C.ident[0:8, 0:8]),
             reads=["l1a_bst", "c_ident"], writes=["PS6"], tag=nm + "_tr")
        P.op("DVE", (lambda t: lambda e: e.tensor_copy(t[:], C.PS[6][:, 0:8].bitcast(F32R)))(Tb),
             reads=["PS6"], writes=[nm], tag=nm + "_ev")

    rp = [0]

    def wload(src_col_ap, tag):
        i = rp[0] % 3
        rp[0] += 1
        view = RINGW[:, i * 1024:(i + 1) * 1024]
        P.dma("SP", (lambda v, sa: lambda e: e.dma_start(
            out=v.rearrange("p (k j) -> p k j", k=8),
            in_=sa.rearrange("(k p) j -> p k j", p=128).bitcast(F32R)))(view, src_col_ap),
            writes=[("l1a_ring", i)], tag="wl_" + tag)
        return view, ("l1a_ring", i)

    def stage_A(p):
        pi = p % 2
        csl = (p * 128, (p + 1) * 128)
        WKV, wkk = wload(Wk_d[:, csl[0]:csl[1]], f"k{p}")
        WVV, wvk = wload(Wv_d[:, csl[0]:csl[1]], f"v{p}")
        WQV, wqk = wload(Wq_d[:, csl[0]:csl[1]], f"q{p}")
        bcs = BC if pi == 0 else BC2
        for (dstk, srcb, nmb, key) in [("k", BKA, f"l1a_bck{pi}", "l1a_bka"),
                                       ("v", BVA, f"l1a_bcv{pi}", "l1a_bva"),
                                       ("q", BQA, f"l1a_bcq{pi}", "l1a_bqa")]:
            P.op("DVE", (lambda d, s2, pp: lambda e: e.tensor_copy(d[:], s2[:, pp:pp + 1]))(bcs[dstk], srcb, p),
                 reads=[key], writes=[nmb], tag=f"{nmb}_{p}")
        for (dst, dkey, WV, wk2, bkey, bcol) in [
                (KHP[pi], f"l1a_khp{pi}", WKV, wkk, f"l1a_bck{pi}", bcs["k"]),
                (VHP[pi], f"l1a_vhp{pi}", WVV, wvk, f"l1a_bcv{pi}", bcs["v"]),
                (QHP[pi], f"l1a_qhp{pi}", WQV, wqk, f"l1a_bcq{pi}", bcs["q"])]:
            for tt in range(NTT):
                b2 = tt % 2
                bank, bkey2 = C.PS[b2], f"PS{b2}"
                for kc in range(DC):
                    P.op("PE", (lambda bb, wv, kcc, ttt, st, sp: lambda e: e.matmul(
                        bb[:], wv[:, kcc * 128:(kcc + 1) * 128], xsl(C, kcc, ttt),
                        start=st, stop=sp))(bank, WV, kc, tt, kc == 0, kc == DC - 1),
                        reads=[("X", kc, tt), wk2], writes=[bkey2], tag=f"l1p{p}_{dkey}_{tt}_{kc}")
                P.op("ACT", (lambda bb, d, ttt, bcc: lambda e: e.activation(
                    d[:, ttt * TT:(ttt + 1) * TT], bb[:], AF.Identity,
                    bias=bcc[:].bitcast(F32)))(bank, dst, tt, bcol),
                    reads=[bkey2, bkey], writes=[(dkey, tt)], tag=f"l1pe{p}_{dkey}_{tt}")

    def stage_B(p):
        pi = p % 2
        for tc in range(NTC):
            b2 = 6 + (tc % 2)
            bank, bkey2 = C.PS[b2], f"PS{b2}"
            P.op("PE", (lambda bb, tcc: lambda e: e.transpose(
                bb[:, 0:128].bitcast(F32R), VHP[pi][:, tcc * TC:(tcc + 1) * TC],
                C.ident[:]))(bank, tc),
                reads=[(f"l1a_vhp{pi}", tc // 4), "c_ident"], writes=[bkey2], tag=f"l1vt{p}_{tc}")
            P.op("DVE", (lambda bb, tcc: lambda e: e.tensor_copy(
                VTM[:, tcc * TC:(tcc + 1) * TC], bb[:, 0:128].bitcast(F32R)))(bank, tc),
                reads=[bkey2], writes=[("l1a_vtm", tc)], tag=f"l1vte{p}_{tc}")
        for tc in range(NTC):
            b2 = 2 + (tc % 2)
            bank, bkey2 = C.PS[b2], f"PS{b2}"
            P.op("PE", (lambda bb, tcc: lambda e: e.matmul(
                bb[:], KHP[pi][:, tcc * TC:(tcc + 1) * TC], WKF[:], start=True, stop=True))(bank, tc),
                reads=[(f"l1a_khp{pi}", tc // 4), "l1a_wkf"], writes=[bkey2], tag=f"l1kp{p}_{tc}")
            P.op("ACT", (lambda bb, tcc: lambda e: e.activation(
                KP[:, tcc * 512:(tcc + 1) * 512], bb[:], AF.Exp))(bank, tc),
                reads=[bkey2], writes=[("l1a_kp", tc)], tag=f"l1kpe{p}_{tc}")
            P.op("PE", (lambda tcc, st, sp: lambda e: e.matmul(
                C.PS[5][0:1, :], C.ones_col[:], KP[:, tcc * 512:(tcc + 1) * 512],
                start=st, stop=sp, skip_group_check=True))(tc, tc == 0, tc == NTC - 1),
                reads=[("l1a_kp", tc), "c_ones_col"], writes=["PS5"], tag=f"l1ks{p}_{tc}")
        P.op("DVE", (lambda pi2: lambda e: e.tensor_copy(KSR[pi2][:], C.PS[5][0:1, :]))(pi),
             reads=["PS5"], writes=[("l1a_ksr",)], tag=f"l1ksr{p}")
        for fc in range(4):
            P.op("PE", (lambda pi2, fcc: lambda e: e.matmul(
                C.PS[4][0:128, fcc:fcc + 1].bitcast(F32),
                KSR[pi2][0:1, fcc * 128:(fcc + 1) * 128].bitcast(F32),
                C.ones1[0:1, 0:1].bitcast(F32), start=True, stop=True,
                skip_group_check=True))(pi, fc),
                reads=[("l1a_ksr",), "c_ones1"], writes=["PS4"], tag=f"l1kst{p}_{fc}")
        P.op("DVE", (lambda pi2: lambda e: e.tensor_copy(KSP[pi2][:], C.PS[4][:, 0:4]))(pi),
             reads=["PS4"], writes=[(f"l1a_ksp{pi}",)], tag=f"l1kspe{p}")

    def stage_C(p):
        pi = p % 2
        P.op("POOL", (lambda pi2: lambda e: e.iota(CTXP[pi2][:], [[0, 512]], base=0,
                                                   channel_multiplier=0,
                                                   allow_small_or_imprecise_dtypes=True))(pi),
             writes=[(f"l1a_ctxp{pi}",)], tag=f"l1cz{p}")
        for gi, (hd, fc) in enumerate([(h2, f2) for h2 in range(2) for f2 in range(2)]):
            cb = 5 if gi % 2 == 0 else 3
            for tc in range(NTC):
                P.op("PE", (lambda c2, hdd, fcc, tcc, st, sp: lambda e: e.matmul(
                    C.PS[c2][:, 0:64], KP[:, tcc * 512 + hdd * 256 + fcc * 128:
                                          tcc * 512 + hdd * 256 + (fcc + 1) * 128],
                    VTM[:, tcc * TC + hdd * 64: tcc * TC + (hdd + 1) * 64],
                    start=st, stop=sp, skip_group_check=True))(cb, hd, fc, tc, tc == 0, tc == NTC - 1),
                    reads=[("l1a_kp", tc), ("l1a_vtm", tc)], writes=[f"PS{cb}"],
                    tag=f"l1cx{p}_{hd}_{fc}_{tc}")
            P.op("DVE", (lambda pi2, c2, hdd, fcc: lambda e: e.tensor_copy(
                CTXP[pi2][:, (hdd * 2 + fcc) * 128 + hdd * 64:
                          (hdd * 2 + fcc) * 128 + (hdd + 1) * 64],
                C.PS[c2][:, 0:64].bitcast(F32R)))(pi, cb, hd, fc),
                reads=[f"PS{cb}"], writes=[(f"l1a_ctxp{pi}",)], tag=f"l1cxe{p}_{hd}_{fc}")

    def stage_D(p):
        pi = p % 2
        for tt in range(NTT):
            qi = tt % 2
            for fch in range(4):
                b2 = 6 + (fch % 2)
                bank, bkey2 = C.PS[b2], f"PS{b2}"
                P.op("PE", (lambda bb, pi2, fcc, ttt: lambda e: e.matmul(
                    bb[:], WQF[:, fcc * 128:(fcc + 1) * 128],
                    QHP[pi2][:, ttt * TT:(ttt + 1) * TT], start=True, stop=True))(bank, pi, fch, tt),
                    reads=[(f"l1a_qhp{pi}", tt), "l1a_wqf"], writes=[bkey2], tag=f"l1qp{p}_{fch}_{tt}")
                P.op("ACT", (lambda bb, qi2, fcc: lambda e: e.activation(
                    QPT[qi2][:, fcc * 512:(fcc + 1) * 512], bb[:], AF.Exp))(bank, qi, fch),
                    reads=[bkey2], writes=[("l1a_qpt", qi, fch)], tag=f"l1qpe{p}_{fch}_{tt}")
            for hd in range(2):
                db = 2 + hd
                for fc in range(2):
                    fch = hd * 2 + fc
                    P.op("PE", (lambda d2, pi2, qi2, fcc, st, sp: lambda e: e.matmul(
                        C.PS[d2][0:1, :], KSP[pi2][:, fcc:fcc + 1],
                        QPT[qi2][:, fcc * 512:(fcc + 1) * 512],
                        start=st, stop=sp))(db, pi, qi, fch, fc == 0, fc == 1),
                        reads=[(f"l1a_ksp{pi}",), ("l1a_qpt", qi, fch)], writes=[f"PS{db}"],
                        tag=f"l1dn{p}_{hd}_{tt}_{fc}")
                P.op("DVE", (lambda d2, ii: lambda e: e.tensor_scalar(
                    DEN[ii][:], C.PS[d2][0:1, :], 1e-6, None, ALU.add))(db, hd),
                    reads=[f"PS{db}"], writes=[("l1a_den",)], tag=f"l1de{p}_{hd}_{tt}")
                P.op("DVE", (lambda ii: lambda e: e.reciprocal(RD1[ii][:], DEN[ii][:]))(hd),
                     preads=[("l1a_den",)], writes=[("l1a_rd1",)], tag=f"l1dr{p}_{hd}_{tt}")
                P.op("PE", (lambda b3, ii: lambda e: e.matmul(
                    C.PS[b3][:], C.ones1[:], RD1[ii][:], start=True, stop=True))(4 + hd // 2 * 0, hd),
                    reads=[("l1a_rd1",), "c_ones1"], writes=["PS4"], tag=f"l1db{p}_{hd}_{tt}")
                for fc in range(2):
                    fch = hd * 2 + fc
                    P.op("DVE", (lambda qi2, fcc: lambda e: e.tensor_tensor(
                        QPT[qi2][:, fcc * 512:(fcc + 1) * 512],
                        QPT[qi2][:, fcc * 512:(fcc + 1) * 512],
                        C.PS[4][:], ALU.mult))(qi, fch),
                        reads=[("l1a_qpt", qi, fch), "PS4"], writes=[("l1a_qpt", qi, fch)],
                        tag=f"l1ds{p}_{hd}_{tt}_{fc}")
            ob = 5
            for kc in range(4):
                P.op("PE", (lambda pi2, qi2, kcc, st, sp: lambda e: e.matmul(
                    C.PS[ob][:], CTXP[pi2][:, kcc * 128:(kcc + 1) * 128],
                    QPT[qi2][:, kcc * 512:(kcc + 1) * 512],
                    start=st, stop=sp))(pi, qi, kc, kc == 0, kc == 3),
                    reads=[(f"l1a_ctxp{pi}",), ("l1a_qpt", qi, kc)], writes=[f"PS{ob}"],
                    tag=f"l1o{p}_{tt}_{kc}")
            P.op("ACT", (lambda ttt: lambda e: e.copy(
                OUTP[:, ttt * TT:(ttt + 1) * TT], C.PS[ob][:]))(tt),
                reads=[f"PS{ob}"],
                writes=[("l1a_outp", tt)] + [("l1a_vtm", tt * 4 + j) for j in range(4)],
                tag=f"l1oe{p}_{tt}")
        P.dma("SP", (lambda pp: lambda e: e.dma_start(
            out=outf[pp * 128:(pp + 1) * 128, :], in_=OUTP[:].bitcast(F32)))(p),
            reads=([("l1a_outp", t2) for t2 in range(NTT)]
                   + [("l1a_vtm", t2) for t2 in range(NTC)]),
            writes=[("l1a_outf", p)], tag=f"st_outf{p}")

    stage_A(0)
    stage_B(0)
    for p in range(8):
        if p < 7:
            stage_A(p + 1)
        stage_C(p)
        stage_D(p)
        if p < 7:
            stage_B(p + 1)

    for kc in range(DC):
        kp_alias = [("l1a_kp", t2) for t2 in range(kc * 2, kc * 2 + 2)]
        P.dma("SP", (lambda kcc: lambda e: e.dma_start(
            out=WPJ[:, kcc * 1024:(kcc + 1) * 1024].rearrange("p (o j) -> p o j", o=8),
            in_=Wp_d[kcc * 128:(kcc + 1) * 128, :].rearrange("p (o j) -> p o j", o=8).bitcast(F32R)))(kc),
            writes=[("l1a_wpj", kc)] + kp_alias, tag=f"ld_wpj{kc}")
    pb_alias = ([(f"l1a_khp{i2}", t2) for i2 in range(2) for t2 in range(NTT)]
                + [(f"l1a_vhp{i2}", t2) for i2 in range(2) for t2 in range(NTT)]
                + [("l1a_qhp0", t2) for t2 in range(NTT)])
    for th in range(2):
        for kc in range(DC):
            P.dma("SP", (lambda kcc, thh: lambda e: e.dma_start(
                out=OFV[:, kcc * 1024:(kcc + 1) * 1024],
                in_=outf[kcc * 128:(kcc + 1) * 128, thh * 1024:(thh + 1) * 1024].bitcast(F32R)))(kc, th),
                reads=[("l1a_outf", kc)], writes=[("l1a_of", kc)] + (pb_alias if th == 0 else []),
                tag=f"ld_of{th}_{kc}")
        for och in range(DC):
            for ttl in range(2):
                tt = th * 2 + ttl
                b2 = 6 + ((och * 2 + ttl) % 2)
                bank, bkey2 = C.PS[b2], f"PS{b2}"
                for kc in range(DC):
                    P.op("PE", (lambda bb, kcc, occ, thh, ttt2, st, sp: lambda e: e.matmul(
                        bb[:], WPJ[:, kcc * 1024 + occ * 128: kcc * 1024 + (occ + 1) * 128],
                        OFV[:, kcc * 1024 + ttt2 * TT: kcc * 1024 + (ttt2 + 1) * TT],
                        start=st, stop=sp))(bank, kc, och, th, ttl, kc == 0, kc == DC - 1),
                        reads=[("l1a_wpj", kc), ("l1a_of", kc)], writes=[bkey2],
                        tag=f"l1z{th}_{och}_{ttl}_{kc}")
                P.op("DVE", (lambda occ: lambda e: e.tensor_copy(BC["p"][:], BPA[:, occ:occ + 1]))(och),
                     reads=["l1a_bpa"], writes=["l1a_bcp"], tag=f"l1zb{th}_{och}_{ttl}")
                P.op("DVE", (lambda bb, occ, ttt: lambda e: e.scalar_tensor_tensor(
                    xsl(C, occ, ttt), bb[:], BC["p"][:].bitcast(F32), xsl(C, occ, ttt),
                    ALU.add, ALU.add))(bank, och, tt),
                    reads=[bkey2, ("X", och, tt)], preads=["l1a_bcp"], writes=[("X", och, tt)],
                    tag=f"l1zr{th}_{och}_{ttl}")
    C.QHPv = QHP[1]
    C.KPv = KP
    return KHP[0], VHP[1]


def build_prog_l1_attn():
    nc = bass.Bass("TRN2", target_bir_lowering=False)
    x_d = nc.dram_tensor("x", [1024, N], F32, kind="ExternalInput")
    wq = nc.dram_tensor("wq", [1024, 1024], F32, kind="ExternalInput")
    bq = nc.dram_tensor("bq", [1024], F32, kind="ExternalInput")
    wk = nc.dram_tensor("wk", [1024, 1024], F32, kind="ExternalInput")
    bk = nc.dram_tensor("bk", [1024], F32, kind="ExternalInput")
    wv = nc.dram_tensor("wv", [1024, 1024], F32, kind="ExternalInput")
    bv = nc.dram_tensor("bv", [1024], F32, kind="ExternalInput")
    wp = nc.dram_tensor("wp", [1024, 1024], F32, kind="ExternalInput")
    bp = nc.dram_tensor("bp", [1024], F32, kind="ExternalInput")
    wqf = nc.dram_tensor("wqf", [64, 256], F32, kind="ExternalInput")
    wkf = nc.dram_tensor("wkf", [64, 256], F32, kind="ExternalInput")
    xo = nc.dram_tensor("xo", [1024, N], F32, kind="ExternalOutput")
    C = make_ctx(nc)
    C.ba = ArenaAlloc(C.al("arena", [128, ACOLS], F32R), ACOLS)
    emit_consts(C)
    emit_stream_in_fm(C, x_d)
    KHP = emit_l1_attn(C, wq.ap(), bq, wk.ap(), bk, wv.ap(), bv, wp.ap(), bp, wqf, wkf)
    KHPv, VHPv = KHP
    C.ln_bufs = {
        "sq": [KHPv[:, 0:512], KHPv[:, 512:1024]],
        "bc": [KHPv[:, 1024:1536], KHPv[:, 1536:2048]],
        "ms": VHPv[0:1, 0:1024],
        "m2": VHPv[0:1, 1024:2048],
        "sc": C.QHPv[0:1, 0:1024],
        "rs": C.QHPv[0:1, 1024:2048],
        "nm": C.KPv[0:1, 0:1024],
    }
    C.ln_alias = ([("l1a_khp", t) for t in range(NTT)] + [("l1a_vhp", t) for t in range(NTT)]
                  + [("l1a_qhp", t) for t in range(NTT)]
                  + [("l1a_kp", 0), ("l1a_kp", 1), ("l1a_of", 0)])
    emit_postln(C, "1a")
    emit_stream_out_fm(C, xo)
    with ExitStack() as stack:
        C.P.build(stack)
    return nc, C


def emit_l1_ffn(C, Wr_d, Wk_d, Wv_d, mur_d, muk_d):
    P, al, nc = C.P, C.ba, C.nc

    RING = al("f1_ring", [128, 8 * 1024], F32R)
    TBUF = [al(f"f1_tb{i}", [128, 1024], F32R) for i in range(2)]
    XK = al("f1_xk", [128, DC * 512], F32R)
    XR = al("f1_xr", [128, DC * 512], F32R)
    KACT = al("f1_kact", [128, DC * 512], F32R)
    XPREV = al("f1_xprev", [128, 512], F32R)
    XLAST = al("f1_xlast", [128, 8], F32R)
    RT = [al(f"f1_rt{i}", [128, 512], F32) for i in range(2)]
    TMP = [al(f"f1_tmp{i}", [128, 512], F32) for i in range(2)]
    MUA = {}
    for nm in ("r", "k", "r1", "k1"):
        MUA[nm] = al(f"f1_mu{nm}", [128, 8], F32R)
    MC = {nm: al(f"f1_mc{nm}", [128, 1], F32R) for nm in ("r", "k")}
    BST = al("f1_bst", [8, 128], F32R)
    wt_dram = {nm: nc.dram_tensor(f"f1_wt_{nm}", [1024, 1024], F32, kind="Internal")
               for nm in ("r", "k", "v")}

    for nm, src in (("r", mur_d), ("k", muk_d)):
        P.dma("SP", (lambda sa: lambda e: e.dma_start(
            out=BST[:], in_=sa.ap().rearrange("(c p) -> c p", p=128).bitcast(F32R)))(src),
            writes=["f1_bst"], tag=f"mu{nm}_ld")
        P.op("PE", lambda e: e.transpose(C.PS[6][:, 0:8].bitcast(F32R), BST[:], C.ident[0:8, 0:8]),
             reads=["f1_bst", "c_ident"], writes=["PS6"], tag=f"mu{nm}_tr")
        P.op("DVE", (lambda n2: lambda e: e.tensor_copy(MUA[n2][:], C.PS[6][:, 0:8].bitcast(F32R)))(nm),
             reads=["PS6"], writes=[f"f1_mu{nm}"], tag=f"mu{nm}_ev")
        P.op("DVE", (lambda n2: lambda e: e.tensor_scalar(
            MUA[n2 + "1"][:], MUA[n2][:], 1.0, -1.0, ALU.subtract, ALU.mult))(nm),
            reads=[f"f1_mu{nm}"], writes=[f"f1_mu{nm}1"], tag=f"mu{nm}_1m")

    ldq = [0]
    for nm, src in (("k", Wk_d), ("r", Wr_d), ("v", Wv_d)):
        for r in range(8):
            sl = r % 4
            tb = r % 2
            P.dma("SP" if ldq[0] % 2 == 0 else "POOL", (lambda sa, rr, sll: lambda e: e.dma_start(
                out=RING[:, sll * 1024:(sll + 1) * 1024],
                in_=sa[rr * 128:(rr + 1) * 128, :].bitcast(F32R)))(src, r, sl),
                writes=[("f1_ring", sl)], tag=f"wt{nm}_ld{r}")
            ldq[0] += 1
            for j in range(8):
                b = 6 + (j % 2)
                bank, bkey = C.PS[b], f"PS{b}"
                P.op("PE", (lambda sll, jj, bb: lambda e: e.transpose(
                    bb[:, 0:128].bitcast(F32R),
                    RING[:, sll * 1024 + jj * 128: sll * 1024 + (jj + 1) * 128],
                    C.ident[:]))(sl, j, bank),
                    reads=[("f1_ring", sl), "c_ident"], writes=[bkey], tag=f"wt{nm}_tr{r}_{j}")
                P.op("ACT", (lambda tbb, jj, bb: lambda e: e.copy(
                    TBUF[tbb][:, jj * 128:(jj + 1) * 128], bb[:, 0:128].bitcast(F32R)))(tb, j, bank),
                    reads=[bkey], writes=[("f1_tb", tb)], tag=f"wt{nm}_ev{r}_{j}")
            P.dma("ACT", (lambda n2, rr, tbb: lambda e: e.dma_start(
                out=wt_dram[n2].ap().rearrange("(j p) d -> p j d", p=128)[:, :, rr * 128:(rr + 1) * 128],
                in_=TBUF[tbb][:].rearrange("p (j d) -> p j d", j=8).bitcast(F32)))(nm, r, tb),
                reads=[("f1_tb", tb)], writes=[("f1_wt", nm, r)], tag=f"wt{nm}_st{r}")

    rp = [0]

    def wload(nm, och, tag):
        i = rp[0] % 8
        rp[0] += 1
        view = RING[:, i * 1024:(i + 1) * 1024]
        P.dma("SP", (lambda n2, occ, v: lambda e: e.dma_start(
            out=v.rearrange("p (k j) -> p k j", k=8),
            in_=wt_dram[n2].ap()[:, occ * 128:(occ + 1) * 128]
                .rearrange("(k p) j -> p k j", p=128).bitcast(F32R)))(nm, och, view),
            reads=[("f1_wt", nm, r2) for r2 in range(8)],
            writes=[("f1_ring", i)], tag="wl_" + tag)
        return view, ("f1_ring", i)

    for tt in range(NTT):
        for kc in range(DC):
            if tt == 0:
                P.op("POOL", lambda e: e.iota(XPREV[:, 0:1], [[0, 1]], base=0, channel_multiplier=0,
                                              allow_small_or_imprecise_dtypes=True),
                     writes=["f1_xprev"], tag=f"f1z{tt}_{kc}")
                P.op("DVE", (lambda kcc: lambda e: e.tensor_copy(
                    XPREV[:, 1:512], C.X[:, kcc * N: kcc * N + 511]))(kc),
                    reads=[("X", kc, 0)], writes=["f1_xprev"], tag=f"f1xp{tt}_{kc}")
            else:
                P.op("DVE", (lambda kcc, ttt: lambda e: e.tensor_copy(
                    XPREV[:, 0:1], XLAST[:, kcc:kcc + 1]))(kc, tt),
                    reads=[("f1_xlast", kc)], writes=["f1_xprev"], tag=f"f1xp0{tt}_{kc}")
                P.op("DVE", (lambda kcc, ttt: lambda e: e.tensor_copy(
                    XPREV[:, 1:512], C.X[:, kcc * N + ttt * TT: kcc * N + ttt * TT + 511]))(kc, tt),
                    reads=[("X", kc, tt)], writes=["f1_xprev"], tag=f"f1xp{tt}_{kc}")
            P.op("DVE", (lambda kcc, ttt: lambda e: e.tensor_tensor(
                TMP[0][:], xsl(C, kcc, ttt), XPREV[:], ALU.subtract))(kc, tt),
                reads=[("X", kc, tt), "f1_xprev"], writes=[("f1_tmp", 0)], tag=f"f1d1{tt}_{kc}")
            for n2 in ("r", "k"):
                P.op("DVE", (lambda n3, kcc: lambda e: e.tensor_copy(
                    MC[n3][:], MUA[n3][:, kcc:kcc + 1]))(n2, kc),
                    reads=[f"f1_mu{n2}"], writes=[f"f1_mc{n2}"], tag=f"f1mc{n2}{tt}_{kc}")
                dst = XR if n2 == "r" else XK
                dkey = "f1_xr" if n2 == "r" else "f1_xk"
                P.op("DVE", (lambda d, n3, kcc: lambda e: e.scalar_tensor_tensor(
                    d[:, kcc * 512:(kcc + 1) * 512], TMP[0][:], MC[n3][:].bitcast(F32), XPREV[:],
                    ALU.mult, ALU.add))(dst, n2, kc),
                    reads=[("f1_tmp", 0), "f1_xprev"], preads=[f"f1_mc{n2}"],
                    writes=[(dkey, kc)], tag=f"f1mx{n2}{tt}_{kc}")
        for kc in range(DC):
            P.op("DVE", (lambda kcc, ttt: lambda e: e.tensor_copy(
                XLAST[:, kcc:kcc + 1], C.X[:, kcc * N + ttt * TT + TT - 1: kcc * N + ttt * TT + TT]))(kc, tt),
                reads=[("X", kc, tt), ("f1_xlast", kc)], writes=[("f1_xlast", kc)],
                tag=f"f1xl{tt}_{kc}")
        for och in range(DC):
            WV, wk2 = wload("k", och, f"k{tt}_{och}")
            b = och % 2
            bank, bkey = C.PS[b], f"PS{b}"
            for kc in range(DC):
                P.op("PE", (lambda bb, v, kcc, st, sp: lambda e: e.matmul(
                    bb[:], v[:, kcc * 128:(kcc + 1) * 128], XK[:, kcc * 512:(kcc + 1) * 512],
                    start=st, stop=sp))(bank, WV, kc, kc == 0, kc == DC - 1),
                    reads=[("f1_xk", kc), wk2], writes=[bkey], tag=f"f1k{tt}_{och}_{kc}")
            i = och % 2
            P.op("DVE", (lambda bb, ii: lambda e: e.tensor_scalar_max(TMP[ii][:], bb[:], 0.0))(bank, i),
                 reads=[bkey], writes=[("f1_tmp", i)], tag=f"f1kr{tt}_{och}")
            P.op("ACT", (lambda ii, occ: lambda e: e.activation(
                KACT[:, occ * 512:(occ + 1) * 512], TMP[ii][:], AF.Square))(i, och),
                reads=[("f1_tmp", i)], writes=[("f1_kact", och)], tag=f"f1ks{tt}_{och}")
        for och in range(DC):
            WVr, wrk = wload("r", och, f"r{tt}_{och}")
            WVv, wvk = wload("v", och, f"v{tt}_{och}")
            b = och % 2
            bank, bkey = C.PS[b], f"PS{b}"
            for kc in range(DC):
                P.op("PE", (lambda bb, v, kcc, st, sp: lambda e: e.matmul(
                    bb[:], v[:, kcc * 128:(kcc + 1) * 128], XR[:, kcc * 512:(kcc + 1) * 512],
                    start=st, stop=sp))(bank, WVr, kc, kc == 0, kc == DC - 1),
                    reads=[("f1_xr", kc), wrk], writes=[bkey], tag=f"f1r{tt}_{och}_{kc}")
            i = och % 2
            P.op("ACT", (lambda bb, ii: lambda e: e.activation(RT[ii][:], bb[:], AF.Sigmoid))(bank, i),
                 reads=[bkey], writes=[("f1_rt", i)], tag=f"f1rs{tt}_{och}")
            b2 = 2 + (och % 2)
            bank2, bkey2 = C.PS[b2], f"PS{b2}"
            for kc in range(DC):
                P.op("PE", (lambda bb, v, kcc, st, sp: lambda e: e.matmul(
                    bb[:], v[:, kcc * 128:(kcc + 1) * 128], KACT[:, kcc * 512:(kcc + 1) * 512],
                    start=st, stop=sp))(bank2, WVv, kc, kc == 0, kc == DC - 1),
                    reads=[("f1_kact", kc), wvk], writes=[bkey2], tag=f"f1v{tt}_{och}_{kc}")
            P.op("DVE", (lambda bb, ii: lambda e: e.tensor_tensor(
                TMP[ii][:], RT[ii][:], bb[:], ALU.mult))(bank2, i),
                reads=[("f1_rt", i), bkey2], writes=[("f1_tmp", i)], tag=f"f1rv{tt}_{och}")
            P.op("POOL", (lambda ii, occ, ttt: lambda e: e.tensor_tensor(
                xsl(C, occ, ttt), xsl(C, occ, ttt), TMP[ii][:], ALU.add))(i, och, tt),
                reads=[("X", och, tt), ("f1_tmp", i)], writes=[("X", och, tt)],
                tag=f"f1xa{tt}_{och}")
    return XK, XR, KACT


def build_prog_l1_ffn():
    nc = bass.Bass("TRN2", target_bir_lowering=False)
    x_d = nc.dram_tensor("x", [1024, N], F32, kind="ExternalInput")
    wr = nc.dram_tensor("wr", [1024, 1024], F32, kind="ExternalInput")
    wk = nc.dram_tensor("wk", [1024, 1024], F32, kind="ExternalInput")
    wv = nc.dram_tensor("wv", [1024, 1024], F32, kind="ExternalInput")
    mur = nc.dram_tensor("mur", [1024], F32, kind="ExternalInput")
    muk = nc.dram_tensor("muk", [1024], F32, kind="ExternalInput")
    yo = nc.dram_tensor("yo", [N, 1024], F32, kind="ExternalOutput")
    C = make_ctx(nc)
    C.ba = ArenaAlloc(C.al("arena", [128, ACOLS], F32R), ACOLS)
    emit_consts(C)
    emit_stream_in_fm(C, x_d)
    XK, XR, KACT = emit_l1_ffn(C, wr.ap().bitcast(F32R), wk.ap().bitcast(F32R),
                               wv.ap().bitcast(F32R), mur, muk)
    C.ln_bufs = {
        "sq": [XK[:, 0:512], XK[:, 512:1024]],
        "bc": [XK[:, 1024:1536], XK[:, 1536:2048]],
        "ms": XK[0:1, 2048:3072],
        "m2": XK[0:1, 3072:4096],
        "sc": XR[0:1, 0:1024],
        "rs": XR[0:1, 1024:2048],
        "nm": KACT[0:1, 0:1024],
    }
    C.ln_alias = ([("f1_xk", k) for k in range(DC)] + [("f1_xr", k) for k in range(DC)]
                  + [("f1_kact", k) for k in range(DC)])
    emit_postln(C, "1f")
    emit_stream_out_tm(C, yo.ap())
    with ExitStack() as stack:
        C.P.build(stack)
    return nc, C




def build_prog_fused():
    nc = bass.Bass("TRN2", target_bir_lowering=False)
    I = {}
    I["x"] = nc.dram_tensor("x", [N, 1024], F32, kind="ExternalInput")
    for nm, shp in [("l0a_wq", [256, 1024]), ("l0a_wk", [256, 1024]), ("l0a_wv", [256, 1024]),
                    ("l0a_wo", [1024, 256]), ("l0f_rw", [1024, 8]), ("l0f_wg", [8, 1024, 512]),
                    ("l0f_wu", [8, 1024, 512]), ("l0f_wd", [8, 512, 1024]),
                    ("l0f_wsg", [1024, 1024]), ("l0f_wsu", [1024, 1024]), ("l0f_wsd", [1024, 1024]),
                    ("l1a_wq", [1024, 1024]), ("l1a_bq", [1024]), ("l1a_wk", [1024, 1024]),
                    ("l1a_bk", [1024]), ("l1a_wv", [1024, 1024]), ("l1a_bv", [1024]),
                    ("l1a_wp", [1024, 1024]), ("l1a_bp", [1024]), ("l1a_wqf", [64, 256]),
                    ("l1a_wkf", [64, 256]), ("l1f_wr", [1024, 1024]), ("l1f_wk", [1024, 1024]),
                    ("l1f_wv", [1024, 1024]), ("l1f_mur", [1024]), ("l1f_muk", [1024])]:
        I[nm] = nc.dram_tensor(nm, shp, F32, kind="ExternalInput")
    yo = nc.dram_tensor("yo", [N, 1024], F32, kind="ExternalOutput")

    C = make_ctx(nc)
    ARENA = C.al("arena", [128, ACOLS], F32R)
    C.ba = ArenaAlloc(ARENA, ACOLS)
    emit_consts(C)
    emit_stream_in_tm(C, I["x"])

    C.ba.reset()
    arena, VT = emit_l0_attn(C, I["l0a_wq"], I["l0a_wk"], I["l0a_wv"], I["l0a_wo"])
    C.ln_bufs = {
        "sq": [arena[:, 0:512], arena[:, 512:1024]],
        "bc": [arena[:, 1024:1536], arena[:, 1536:2048]],
        "ms": arena[0:1, 2048:3072],
        "m2": arena[0:1, 3072:4096],
        "sc": VT[0:1, 0:1024],
        "rs": VT[0:1, 1024:2048],
        "nm": VT[0:1, 2048:3072],
    }
    C.ln_alias = ([("l0a_wot", h) for h in range(4)] +
                  [("l0a_wqk",), ("l0a_sraw", 0), ("l0a_sraw", 1)] +
                  [("l0a_vt", t) for t in range(12)])
    emit_postln(C, "0a")
    C.P.barrier()

    C.ba.reset()
    RING = emit_l0_ffn(C, I["l0f_rw"], I["l0f_wg"].ap(), I["l0f_wu"].ap(), I["l0f_wd"].ap(),
                       I["l0f_wsg"].ap(), I["l0f_wsu"].ap(), I["l0f_wsd"].ap())
    C.ln_bufs = {
        "sq": [RING[:, 0:512], RING[:, 512:1024]],
        "bc": [RING[:, 1024:1536], RING[:, 1536:2048]],
        "ms": RING[0:1, 2048:3072],
        "m2": RING[0:1, 3072:4096],
        "sc": RING[0:1, 4096:5120],
        "rs": RING[0:1, 5120:6144],
        "nm": RING[0:1, 6144:7168],
    }
    C.ln_alias = [("ring", 0), ("ring", 1)]
    emit_postln(C, "0f")
    C.P.barrier()

    C.ba.reset()
    KHPv, VHPv = emit_l1_attn(C, I["l1a_wq"].ap(), I["l1a_bq"], I["l1a_wk"].ap(), I["l1a_bk"],
                              I["l1a_wv"].ap(), I["l1a_bv"], I["l1a_wp"].ap(), I["l1a_bp"],
                              I["l1a_wqf"], I["l1a_wkf"])
    C.ln_bufs = {
        "sq": [KHPv[:, 0:512], KHPv[:, 512:1024]],
        "bc": [KHPv[:, 1024:1536], KHPv[:, 1536:2048]],
        "ms": VHPv[0:1, 0:1024],
        "m2": VHPv[0:1, 1024:2048],
        "sc": C.QHPv[0:1, 0:1024],
        "rs": C.QHPv[0:1, 1024:2048],
        "nm": C.KPv[0:1, 0:1024],
    }
    C.ln_alias = ([("l1a_khp", t) for t in range(NTT)] + [("l1a_vhp", t) for t in range(NTT)]
                  + [("l1a_qhp", t) for t in range(NTT)]
                  + [("l1a_kp", 0), ("l1a_kp", 1), ("l1a_of", 0)])
    emit_postln(C, "1a")
    C.P.barrier()

    C.ba.reset()
    XK, XR, KACT = emit_l1_ffn(C, I["l1f_wr"].ap().bitcast(F32R), I["l1f_wk"].ap().bitcast(F32R),
                               I["l1f_wv"].ap().bitcast(F32R), I["l1f_mur"], I["l1f_muk"])
    C.ln_bufs = {
        "sq": [XK[:, 0:512], XK[:, 512:1024]],
        "bc": [XK[:, 1024:1536], XK[:, 1536:2048]],
        "ms": XK[0:1, 2048:3072],
        "m2": XK[0:1, 3072:4096],
        "sc": XR[0:1, 0:1024],
        "rs": XR[0:1, 1024:2048],
        "nm": KACT[0:1, 0:1024],
    }
    C.ln_alias = ([("f1_xk", k) for k in range(DC)] + [("f1_xr", k) for k in range(DC)]
                  + [("f1_kact", k) for k in range(DC)])
    emit_postln(C, "1f")
    emit_stream_out_tm(C, yo.ap())
    with ExitStack() as stack:
        C.P.build(stack)
    return nc, C


def kernel(**inputs):
    I = inputs
    x0 = np.ascontiguousarray(np.asarray(I["L0_attn_x"], dtype=np.float32))
    nc, _ = _get_prog("fused", build_prog_fused)
    im = [dict(x=np.ascontiguousarray(x0[b]),
               l0a_wq=I["L0_attn_Wq"], l0a_wk=I["L0_attn_Wk"], l0a_wv=I["L0_attn_Wv"],
               l0a_wo=I["L0_attn_Wo"], l0f_rw=I["L0_ffn_router_w"], l0f_wg=I["L0_ffn_w_gate"],
               l0f_wu=I["L0_ffn_w_up"], l0f_wd=I["L0_ffn_w_down"], l0f_wsg=I["L0_ffn_ws_gate"],
               l0f_wsu=I["L0_ffn_ws_up"], l0f_wsd=I["L0_ffn_ws_down"],
               l1a_wq=I["L1_attn_Wq"], l1a_bq=I["L1_attn_bq"], l1a_wk=I["L1_attn_Wk"],
               l1a_bk=I["L1_attn_bk"], l1a_wv=I["L1_attn_Wv"], l1a_bv=I["L1_attn_bv"],
               l1a_wp=I["L1_attn_Wproj"], l1a_bp=I["L1_attn_bproj"],
               l1a_wqf=I["L1_attn_wq_feat"], l1a_wkf=I["L1_attn_wk_feat"],
               l1f_wr=I["L1_ffn_W_r"], l1f_wk=I["L1_ffn_W_k"], l1f_wv=I["L1_ffn_W_v"],
               l1f_mur=I["L1_ffn_mu_r"], l1f_muk=I["L1_ffn_mu_k"]) for b in range(8)]
    r = run_bass_kernel_spmd(nc, im, core_ids=list(range(8)))
    return np.stack([np.asarray(r.results[b]["yo"]) for b in range(8)], axis=0)
